# version 19
# baseline (speedup 1.0000x reference)
import numpy as np
import ml_dtypes
import concourse.bass as bass
import concourse.mybir as mybir
from concourse.bass_utils import run_bass_kernel_spmd
from contextlib import ExitStack

F32 = mybir.dt.float32
BF16 = mybir.dt.bfloat16
AF = mybir.ActivationFunctionType
ALU = mybir.AluOpType
AX = mybir.AxisListType

TAPS = False
BIGH = 16384.0
LNEPS = 1e-5
REG = 5
NFLIP = 8

SEM_NAMES = [
    "dSy", "dG", "dB",
    "fSy", "fG", "fT", "fS", "fV",
    "sCast", "sTrX", "sEvX",
    "sMM1", "sEv1",
    "sSc", "sSd", "sAvR", "sRet", "sEvR",
    "sAoMM", "sH1", "sVar", "sRstd", "sY", "sYb", "sTrY", "sEvY",
    "sWcast", "sQKMM", "sEvP",
    "sSq", "sN2MM", "sDotMM", "sEvB", "sRsq",
    "sCol", "sMid",
    "sQKb", "sLog", "sExp", "sAv", "sDen", "sBc", "sNorm",
    "sPjMM", "sH2", "sVar2", "sRstd2", "sZ",
]


class Sem:
    def __init__(self, nc, ctx, name):
        self.sem = ctx.enter_context(nc.semaphore(name))
        self.n = 0

    def inc(self, instr, k=1):
        instr.then_inc(self.sem, k)
        self.n += k


def build(nseq=2, T=2048, dm=1024, taps=TAPS):
    C = 256
    H, DH = 16, 64
    NT = T // 128
    NQ = T // 512
    DCH = dm // 128
    scale = DH ** -0.5
    TF = T + NFLIP

    nc = bass.Bass("TRN2", target_bir_lowering=False, detect_race_conditions=False)

    xin = nc.dram_tensor("xin", [nseq, T, dm], F32, kind="ExternalInput")
    basis = nc.dram_tensor("basis", [dm, 128], F32, kind="ExternalInput")
    coeffs = {nm: nc.dram_tensor(nm, [C, 128], F32, kind="ExternalInput")
              for nm in ["qc", "kc", "vc", "oc"]}
    dlog = nc.dram_tensor("dlog", [1, 1], F32, kind="ExternalInput")
    oscale = nc.dram_tensor("oscale", [1, 1], F32, kind="ExternalInput")
    gin = nc.dram_tensor("gin", [16, 64], F32, kind="ExternalInput")
    wqkv = nc.dram_tensor("wqkv", [dm, 3 * dm], F32, kind="ExternalInput")
    wproj = nc.dram_tensor("wproj", [dm, dm], F32, kind="ExternalInput")
    bproj = nc.dram_tensor("bproj", [1, dm], F32, kind="ExternalInput")
    zout = nc.dram_tensor("zout", [nseq, T, dm], F32, kind="ExternalOutput")

    sck = "ExternalOutput" if taps else "Internal"
    sc_y = nc.dram_tensor("sc_y", [nseq, T, dm], F32, kind=sck)
    sc_q = nc.dram_tensor("sc_q", [nseq, DCH, 128, T], BF16, kind=sck)
    sc_k = nc.dram_tensor("sc_k", [nseq, DCH, 128, T], BF16, kind=sck)
    sc_v = nc.dram_tensor("sc_v", [nseq, NT, 128, H * 65], BF16, kind=sck)
    sc_sim = nc.dram_tensor("sc_sim", [nseq, 1, T], F32, kind=sck)
    cc_in = nc.dram_tensor("cc_in", [1, 2], F32, kind="Internal")
    cc_out = nc.dram_tensor("cc_out", [1, 2], F32, kind="Internal")
    sc_k3 = nc.dram_tensor("sc_k3", [nseq, 3, T], BF16, kind="Internal")
    dbg_b = nc.dram_tensor("dbg_b", [8, 128, 512], BF16, kind=sck)
    dbg_f = nc.dram_tensor("dbg_f", [1, 64], F32, kind=sck)
    dbg_h = nc.dram_tensor("dbg_h", [128, 1024 + 16], F32, kind=sck)
    sc_q2 = nc.dram_tensor("sc_q2", [nseq, 2, TF], BF16, kind="Internal")
    sc_m = nc.dram_tensor("sc_m", [nseq, H, TF], BF16, kind="Internal")

    octx = ExitStack()
    with octx:
        _uid = [0]

        def sb(ctx, name, shape, dt):
            _uid[0] += 1
            return ctx.enter_context(nc.sbuf_tensor(f"{name}_{_uid[0]}", shape, dt))

        def pst(ctx, name, shape, dt=F32):
            _uid[0] += 1
            return ctx.enter_context(nc.psum_tensor(f"{name}_{_uid[0]}", shape, dt))

        ident = sb(octx, "ident", [128, 128], BF16)
        e0 = sb(octx, "e0", [128, 512], F32)
        dconst = sb(octx, "dconst", [128, NT * 512], BF16)
        bias_bc = sb(octx, "bias_bc", [128, dm], F32)
        rqw = sb(octx, "rqw", [128, DCH * C], BF16)
        rkw = sb(octx, "rkw", [128, DCH * C], BF16)
        rvw = sb(octx, "rvw", [128, DCH * C], BF16)
        rowT = sb(octx, "rowT", [128, 2 * dm], BF16)
        gbd = sb(octx, "gbd", [128, 128], BF16)
        obd = sb(octx, "obd", [128, 128], BF16)
        onesf = sb(octx, "onesf", [1, 128], F32)
        ones16f = sb(octx, "ones16f", [16, 1], F32)
        scal = sb(octx, "scal", [1, 32], F32)
        cols = sb(octx, "cols", [128, 4], F32)
        lncols = sb(octx, "lncols", [128, 16], F32)
        arenaF = sb(octx, "arenaF", [128, T], F32)
        arenaB = sb(octx, "arenaB", [128, TF], BF16)
        cbias = sb(octx, "cbias", [128, 2], F32)

        S = {name: Sem(nc, octx, name) for name in SEM_NAMES}

        class N:
            pass

        for k_, v_ in S.items():
            setattr(N, k_, v_)

        block = octx.enter_context(nc.Block())
        ENGS = ["sync", "gpsimd", "tensor", "scalar", "vector"]

        def fence():
            fence.round += 1
            r = fence.round
            fs = {"sync": N.fSy, "gpsimd": N.fG, "tensor": N.fT,
                  "scalar": N.fS, "vector": N.fV}

            def mk(which):
                def f(eng):
                    if which != "sync":
                        eng.drain()
                    eng.sem_inc(fs[which].sem, 1)
                    for w2, s2 in fs.items():
                        if w2 != which:
                            eng.wait_ge(s2.sem, r)
                return f
            for w in ENGS:
                getattr(block, w)(mk(w))
            for s2 in fs.values():
                s2.n = r
        fence.round = 0

        with ExitStack() as pctx:
            bss = sb(pctx, "bss", [128, 1024], F32)
            bsb = sb(pctx, "bsb", [128, 1024], BF16)
            bsT = sb(pctx, "bsT", [128, 1024], BF16)
            ccs = sb(pctx, "ccs", [128, 256], F32)
            ccb = sb(pctx, "ccb", [128, 256], BF16)
            ccT = sb(pctx, "ccT", [128, 256], BF16)
            bpr = sb(pctx, "bpr", [1, dm], F32)
            sct = sb(pctx, "sct", [1, 32], F32)
            gh = sb(pctx, "gh", [16, 64], F32)
            ghb = sb(pctx, "ghb", [16, 64], BF16)
            ghTb = sb(pctx, "ghTb", [64, 16], BF16)
            gtmp = sb(pctx, "gtmp", [16, 64], F32)
            gcol = sb(pctx, "gcol", [16, 4], F32)
            dta = sb(pctx, "dta", [128, 512], F32)
            dtb = sb(pctx, "dtbx", [128, 512], F32)
            dtm = sb(pctx, "dtm", [128, 512], F32)
            ps1 = pst(pctx, "ps1", [128, 512])
            ps2 = pst(pctx, "ps2", [128, 512])
            psT = pst(pctx, "psT", [128, 1024], BF16)
            psB = pst(pctx, "psB", [64, 16], BF16)
            psC = pst(pctx, "psC", [128, 16])

            @block.sync
            def _(sy):
                for c in range(DCH):
                    i = sy.dma_start(out=bss[:, c * 128:(c + 1) * 128],
                                     in_=basis[c * 128:(c + 1) * 128, :])
                    N.dSy.inc(i, 16)
                i = sy.dma_start(out=sct[0:1, 0:1], in_=dlog[:, :])
                N.dSy.inc(i, 16)
                i = sy.dma_start(out=sct[0:1, 8:9], in_=oscale[:, :])
                N.dSy.inc(i, 16)
                i = sy.dma_start(out=gh[:, :], in_=gin[:, :])
                N.dSy.inc(i, 16)
                i = sy.dma_start(out=bpr[:, :], in_=bproj[:, :])
                N.dSy.inc(i, 16)

            @block.gpsimd
            def _(g):
                i = g.iota(e0[:], pattern=[[-1, 512]], base=0, channel_multiplier=1,
                           allow_small_or_imprecise_dtypes=True)
                N.sMid.inc(i)

            @block.vector
            def _(v):
                v.wait_ge(N.sMid.sem, N.sMid.n)
                v.memset(onesf[:], 1.0)
                v.memset(cbias[:, 0:1], LNEPS)
                v.memset(scal[:], 0.0)
                v.memset(lncols[:], 0.0)
                v.memset(cols[:], 0.0)
                v.memset(cbias[:, 1:2], -BIGH)
                v.memset(ones16f[:], 1.0 / 16.0)
                v.memset(obd[:], 0.0)
                v.memset(gbd[:], 0.0)
                for h in range(H):
                    v.memset(obd[(h % 2) * 64:(h % 2) * 64 + 64,
                                 (h // 2) * 16 + h:(h // 2) * 16 + h + 1], 1.0)
                v.tensor_scalar(out=ident[:], in0=e0[:, 0:128], scalar1=0.0,
                                scalar2=None, op0=ALU.is_equal)
                v.wait_ge(N.dSy.sem, N.dSy.n)
                v.tensor_copy(out=bsb[:], in_=bss[:])
                v.tensor_tensor(out=gtmp[:], in0=gh[:], in1=gh[:], op=ALU.mult)
                i = v.tensor_reduce(out=gcol[:, 0:1], in_=gtmp[:], axis=AX.X,
                                    op=ALU.add)
                N.sVar.inc(i)

            @block.scalar
            def _(s):
                s.wait_ge(N.sVar.sem, N.sVar.n)
                s.activation(out=gcol[:, 1:2], in_=gcol[:, 0:1], func=AF.Ln)
                s.activation(out=gcol[:, 2:3], in_=gcol[:, 1:2], func=AF.Exp,
                             scale=-0.5)
                i = s.activation(out=sct[0:1, 2:3], in_=sct[0:1, 0:1], func=AF.Exp,
                                 scale=-1.0)
                N.sRstd.inc(i)

            @block.vector
            def _(v):
                v.wait_ge(N.sRstd.sem, N.sRstd.n)
                v.tensor_scalar(out=ghb[:], in0=gh[:], scalar1=gcol[:, 2:3],
                                scalar2=None, op0=ALU.mult)
                i = v.tensor_scalar(out=sct[0:1, 3:4], in0=sct[0:1, 2:3], scalar1=1.0,
                                    scalar2=None, op0=ALU.add)
                N.sY.inc(i)

            @block.scalar
            def _(s):
                s.wait_ge(N.sY.sem, N.sY.n)
                i = s.activation(out=sct[0:1, 16:17], in_=sct[0:1, 3:4], func=AF.Ln)
                N.sRstd.inc(i)

            @block.vector
            def _(v):
                v.wait_ge(N.sRstd.sem, N.sRstd.n)
                i = v.tensor_scalar(out=scal[0:1, 1:2], in0=sct[0:1, 16:17],
                                    scalar1=-1.0, scalar2=None, op0=ALU.mult)
                N.sY.inc(i)

            @block.scalar
            def _(s):
                s.wait_ge(N.sY.sem, N.sY.n)
                i = s.engine_nop() if False else s.activation(
                    out=sct[0:1, 17:18], in_=sct[0:1, 16:17], func=AF.Identity)
                N.sRstd.inc(i)

            fence()

            @block.tensor
            def _(t):
                for c in range(DCH):
                    i = t.transpose(psT[:, c * 128:(c + 1) * 128],
                                    bsb[:, c * 128:(c + 1) * 128], ident[:])
                i = t.transpose(psB[:], ghb[:], ident[0:16, 0:16])
                i = t.matmul(psC[:, 0:1], lhsT=onesf[:], rhs=scal[0:1, 1:2],
                             start=True, stop=True)
                i = t.matmul(psC[:, 1:2], lhsT=onesf[:], rhs=sct[0:1, 8:9],
                             start=True, stop=True)
                i = t.matmul(ps1[:], lhsT=onesf[:], rhs=bpr[0:1, 0:512],
                             start=True, stop=True)
                i = t.matmul(ps2[:], lhsT=onesf[:], rhs=bpr[0:1, 512:1024],
                             start=True, stop=True)
                N.sMM1.inc(i)

            @block.vector
            def _(v):
                v.wait_ge(N.sMM1.sem, N.sMM1.n)
                v.tensor_copy(out=bsT[:], in_=psT[:])
                v.tensor_copy(out=ghTb[:], in_=psB[:])
                v.drain()
                for h in range(H):
                    v.tensor_copy(out=gbd[(h % 2) * 64:(h % 2) * 64 + 64,
                                          (h // 2) * 16 + h:(h // 2) * 16 + h + 1],
                                  in_=ghTb[:, h:h + 1])
                v.tensor_copy(out=cols[:, 0:1], in_=psC[:, 0:1])
                v.tensor_copy(out=cols[:, 1:2], in_=psC[:, 1:2])
                v.tensor_copy(out=bias_bc[:, 0:512], in_=ps1[:])
                i = v.tensor_copy(out=bias_bc[:, 512:1024], in_=ps2[:])
                N.sEv1.inc(i)

            bV0, bA0 = N.sVar.n, N.sRstd.n

            @block.vector
            def _(v):
                for d in range(NT):
                    buf = dta if d % 2 == 0 else dtb
                    if d >= 2:
                        v.wait_ge(N.sRstd.sem, bA0 + d - 1)
                    i = v.tensor_scalar(out=buf[:], in0=e0[:], scalar1=128.0 * d - 1.0,
                                        scalar2=0.0, op0=ALU.add, op1=ALU.max)
                    N.sVar.inc(i)

            @block.scalar
            def _(s):
                s.wait_ge(N.sEv1.sem, N.sEv1.n)
                for d in range(NT):
                    buf = dta if d % 2 == 0 else dtb
                    s.wait_ge(N.sVar.sem, bV0 + d + 1)
                    i = s.activation(out=dconst[:, d * 512:(d + 1) * 512], in_=buf[:],
                                     func=AF.Exp, scale=cols[:, 0:1])
                    N.sRstd.inc(i)

            @block.vector
            def _(v):
                v.wait_ge(N.sRstd.sem, N.sRstd.n)
                i = None
                for d in range(min(4, NT)):
                    v.tensor_scalar(out=dtm[:], in0=e0[:], scalar1=128.0 * d - 1.0,
                                    scalar2=0.0, op0=ALU.add, op1=ALU.is_ge)
                    i = v.tensor_tensor(out=dconst[:, d * 512:(d + 1) * 512],
                                        in0=dconst[:, d * 512:(d + 1) * 512],
                                        in1=dtm[:], op=ALU.mult)
                N.sEv1.inc(i)

            for nm, dst in [("qc", rqw), ("kc", rkw), ("vc", rvw), ("oc", None)]:
                fence()

                @block.sync
                def _(sy, nm=nm):
                    for c in range(2):
                        i = sy.dma_start(out=ccs[:, c * 128:(c + 1) * 128],
                                         in_=coeffs[nm][c * 128:(c + 1) * 128, :])
                        N.dSy.inc(i, 16)

                @block.vector
                def _(v):
                    v.wait_ge(N.dSy.sem, N.dSy.n)
                    i = v.tensor_copy(out=ccb[:], in_=ccs[:])
                    N.sEv1.inc(i)

                @block.tensor
                def _(t):
                    t.wait_ge(N.sEv1.sem, N.sEv1.n)
                    for c in range(2):
                        i = t.transpose(psT[:, c * 128:(c + 1) * 128],
                                        ccb[:, c * 128:(c + 1) * 128], ident[:])
                    N.sMM1.inc(i)

                @block.vector
                def _(v):
                    v.wait_ge(N.sMM1.sem, N.sMM1.n)
                    i = v.tensor_copy(out=ccT[:], in_=psT[:, 0:256])
                    N.sEv1.inc(i)

                if dst is not None:
                    bM, bE = N.sMM1.n, N.sEv1.n

                    @block.tensor
                    def _(t):
                        t.wait_ge(N.sEv1.sem, bE)
                        for c in range(DCH):
                            if c >= 2:
                                t.wait_ge(N.sEv1.sem, bE + c - 1)
                            i = t.matmul((ps1 if c % 2 == 0 else ps2)[:, 0:C],
                                         lhsT=bsT[:, c * 128:(c + 1) * 128],
                                         rhs=ccT[:], start=True, stop=True)
                            N.sMM1.inc(i)

                    @block.vector
                    def _(v, dst=dst):
                        for c in range(DCH):
                            v.wait_ge(N.sMM1.sem, bM + c + 1)
                            i = v.tensor_copy(out=dst[:, c * C:(c + 1) * C],
                                              in_=(ps1 if c % 2 == 0 else ps2)[:, 0:C])
                            N.sEv1.inc(i)
                else:
                    bM, bE = N.sMM1.n, N.sEv1.n

                    @block.tensor
                    def _(t):
                        t.wait_ge(N.sEv1.sem, bE)
                        k = 0
                        for cc in range(2):
                            for nh in range(2):
                                if k >= 2:
                                    t.wait_ge(N.sEv1.sem, bE + k - 1)
                                i = t.matmul((ps1 if k % 2 == 0 else ps2)[:],
                                             lhsT=ccT[:, cc * 128:(cc + 1) * 128],
                                             rhs=bsT[:, nh * 512:(nh + 1) * 512],
                                             start=True, stop=True)
                                N.sMM1.inc(i)
                                k += 1

                    @block.vector
                    def _(v):
                        k = 0
                        for cc in range(2):
                            for nh in range(2):
                                v.wait_ge(N.sMM1.sem, bM + k + 1)
                                i = v.tensor_scalar(
                                    out=rowT[:, cc * dm + nh * 512:cc * dm + (nh + 1) * 512],
                                    in0=(ps1 if k % 2 == 0 else ps2)[:],
                                    scalar1=cols[:, 1:2], scalar2=None, op0=ALU.mult)
                                N.sEv1.inc(i)
                                k += 1

            fence()

        if taps:
            @block.gpsimd
            def _(g):
                ii = g.dma_start(out=dbg_b[0, :, :], in_=dconst[:, 0:512])
                N.dG.inc(ii, 16)
                ii = g.dma_start(out=dbg_b[1, :, :],
                                 in_=dconst[:, (NT - 1) * 512:NT * 512])
                N.dG.inc(ii, 16)
                ii = g.dma_start(out=dbg_b[2, :, :], in_=rqw[:, 0:512])
                N.dG.inc(ii, 16)
                ii = g.dma_start(out=dbg_f[0, 0:32], in_=scal[0:1, 0:32])
                N.dG.inc(ii, 16)
                ii = g.dma_start(out=dbg_f[0, 32:36], in_=cols[0:1, 0:4])
                N.dG.inc(ii, 16)
                g.wait_ge(N.dG.sem, N.dG.n)

        SDEP_A = 2
        with ExitStack() as actx:
            xyT = sb(actx, "xyT", [128, DCH * T], BF16)
            qtb = sb(actx, "qtb", [128, 2 * T], BF16)
            ktb = sb(actx, "ktb", [128, 2 * T], BF16)
            vtb = sb(actx, "vtb", [128, NT * C], BF16)
            rtb = sb(actx, "rtb", [128, 2 * T], BF16)
            sdb = sb(actx, "sdb", [128, SDEP_A * 512], BF16)

            for seq in range(nseq):
                with ExitStack() as s1:
                    xld = sb(s1, "xld", [128, 2 * dm], F32)
                    xcb = sb(s1, "xcb", [128, 2 * dm], BF16)
                    ps_tr = pst(s1, "ps_tr", [128, 2 * 1024], BF16)
                    bSy, bC, bT, bE = N.dSy.n, N.sCast.n, N.sTrX.n, N.sEvX.n

                    @block.sync
                    def _(sy):
                        for i in range(NT):
                            if i >= 2:
                                sy.wait_ge(N.sCast.sem, bC + i - 1)
                            ii = sy.dma_start(out=xld[:, (i % 2) * dm:(i % 2 + 1) * dm],
                                              in_=xin[seq, i * 128:(i + 1) * 128, :])
                            N.dSy.inc(ii, 16)

                    @block.vector
                    def _(v):
                        for i in range(NT):
                            v.wait_ge(N.dSy.sem, bSy + 16 * (i + 1))
                            if i >= 2:
                                v.wait_ge(N.sTrX.sem, bT + i - 1)
                            ii = v.tensor_copy(out=xcb[:, (i % 2) * dm:(i % 2 + 1) * dm],
                                               in_=xld[:, (i % 2) * dm:(i % 2 + 1) * dm])
                            N.sCast.inc(ii)
                            if i >= 1:
                                j = i - 1
                                v.wait_ge(N.sTrX.sem, bT + j + 1)
                                dst = xyT[:].rearrange("p (c n) -> p c n", c=DCH)[
                                    :, :, j * 128:(j + 1) * 128]
                                src = ps_tr[:, (j % 2) * 1024:(j % 2 + 1) * 1024
                                            ].rearrange("p (c n) -> p c n", c=DCH)
                                ii = v.tensor_copy(out=dst, in_=src)
                                N.sEvX.inc(ii)
                        j = NT - 1
                        v.wait_ge(N.sTrX.sem, bT + NT)
                        dst = xyT[:].rearrange("p (c n) -> p c n", c=DCH)[
                            :, :, j * 128:(j + 1) * 128]
                        src = ps_tr[:, (j % 2) * 1024:(j % 2 + 1) * 1024
                                    ].rearrange("p (c n) -> p c n", c=DCH)
                        ii = v.tensor_copy(out=dst, in_=src)
                        N.sEvX.inc(ii)

                    @block.tensor
                    def _(t):
                        for i in range(NT):
                            t.wait_ge(N.sCast.sem, bC + i + 1)
                            if i >= 2:
                                t.wait_ge(N.sEvX.sem, bE + i - 1)
                            for c in range(DCH):
                                ii = t.transpose(
                                    ps_tr[:, (i % 2) * 1024 + c * 128:
                                          (i % 2) * 1024 + (c + 1) * 128],
                                    xcb[:, (i % 2) * dm + c * 128:(i % 2) * dm + (c + 1) * 128],
                                    ident[:])
                            N.sTrX.inc(ii)

                fence()

                with ExitStack() as s1:
                    ps_q = [pst(s1, "ps_q", [128, 512]) for _ in range(2)]
                    ps_v = [pst(s1, "ps_v", [128, C]) for _ in range(2)]
                    bM, bE = N.sQKMM.n, N.sEvP.n
                    qk_tiles = [(w, cc, nt) for w in range(2) for cc in range(2)
                                for nt in range(NQ)]

                    @block.tensor
                    def _(t):
                        k = 0
                        for (w, cc, nt) in qk_tiles:
                            wsrc = rqw if w == 0 else rkw
                            if k >= 2:
                                t.wait_ge(N.sEvP.sem, bE + k - 1)
                            for c in range(DCH):
                                ii = t.matmul(
                                    ps_q[k % 2][:],
                                    lhsT=wsrc[:, c * C + cc * 128:c * C + (cc + 1) * 128],
                                    rhs=xyT[:, c * T + nt * 512:c * T + (nt + 1) * 512],
                                    start=(c == 0), stop=(c == DCH - 1))
                            N.sQKMM.inc(ii)
                            k += 1
                        for i in range(NT):
                            if k >= 2:
                                t.wait_ge(N.sEvP.sem, bE + k - 1)
                            for c in range(DCH):
                                ii = t.matmul(
                                    ps_v[k % 2][:],
                                    lhsT=xyT[:, c * T + i * 128:c * T + (i + 1) * 128],
                                    rhs=rvw[:, c * C:(c + 1) * C],
                                    start=(c == 0), stop=(c == DCH - 1))
                            N.sQKMM.inc(ii)
                            k += 1

                    @block.vector
                    def _(v):
                        k = 0
                        for (w, cc, nt) in qk_tiles:
                            dstt = qtb if w == 0 else ktb
                            v.wait_ge(N.sQKMM.sem, bM + k + 1)
                            ii = v.tensor_copy(
                                out=dstt[:, cc * T + nt * 512:cc * T + (nt + 1) * 512],
                                in_=ps_q[k % 2][:])
                            N.sEvP.inc(ii)
                            k += 1
                        for i in range(NT):
                            v.wait_ge(N.sQKMM.sem, bM + k + 1)
                            ii = v.tensor_copy(out=vtb[:, i * C:(i + 1) * C],
                                               in_=ps_v[k % 2][:])
                            N.sEvP.inc(ii)
                            k += 1

                fence()

                with ExitStack() as s1:
                    SDEP = SDEP_A
                    ps_sc = [pst(s1, "ps_sc", [128, 512]) for _ in range(SDEP)]
                    ps_rt = [pst(s1, "ps_rt", [128, 512]) for _ in range(2)]
                    bSc, bSd, bAvR, bRet, bEvR = (N.sSc.n, N.sSd.n, N.sAvR.n,
                                                  N.sRet.n, N.sEvR.n)
                    jlists = [list(range(4 * j, NT)) for j in range(NQ)]

                    @block.tensor
                    def _(t):
                        g = 0
                        for j in range(NQ):
                            il = jlists[j]
                            for ii_, i in enumerate(il):
                                if g >= SDEP:
                                    t.wait_ge(N.sSd.sem, bSd + g - SDEP + 1)
                                for cc in range(2):
                                    mm = t.matmul(
                                        ps_sc[g % SDEP][:],
                                        lhsT=ktb[:, cc * T + i * 128:cc * T + (i + 1) * 128],
                                        rhs=qtb[:, cc * T + j * 512:cc * T + (j + 1) * 512],
                                        start=(cc == 0), stop=(cc == 1))
                                N.sSc.inc(mm)
                                if ii_ >= 1:
                                    if ii_ == 1 and j >= 1:
                                        t.wait_ge(N.sEvR.sem, bEvR + 2 * j)
                                    t.wait_ge(N.sSd.sem, bSd + g)
                                    pi = il[ii_ - 1]
                                    for cc in range(2):
                                        mm = t.matmul(
                                            ps_rt[cc][:],
                                            lhsT=vtb[:, pi * C + cc * 128:pi * C + (cc + 1) * 128],
                                            rhs=sdb[:, ((g - 1) % SDEP) * 512:((g - 1) % SDEP + 1) * 512],
                                            start=(ii_ == 1), stop=False)
                                    N.sAvR.inc(mm)
                                g += 1
                            t.wait_ge(N.sSd.sem, bSd + g)
                            pi = il[-1]
                            if len(il) == 1 and j >= 1:
                                t.wait_ge(N.sEvR.sem, bEvR + 2 * j)
                            for cc in range(2):
                                mm = t.matmul(
                                    ps_rt[cc][:],
                                    lhsT=vtb[:, pi * C + cc * 128:pi * C + (cc + 1) * 128],
                                    rhs=sdb[:, ((g - 1) % SDEP) * 512:((g - 1) % SDEP + 1) * 512],
                                    start=(len(il) == 1), stop=True)
                            N.sAvR.inc(mm)

                    @block.vector
                    def _(v):
                        g = 0
                        for j in range(NQ):
                            il = jlists[j]
                            for ii_, i in enumerate(il):
                                d = i - 4 * j
                                v.wait_ge(N.sSc.sem, bSc + g + 1)
                                if g >= SDEP:
                                    v.wait_ge(N.sAvR.sem, bAvR + g - SDEP + 1)
                                ii = v.tensor_tensor(
                                    out=sdb[:, (g % SDEP) * 512:(g % SDEP + 1) * 512],
                                    in0=ps_sc[g % SDEP][:],
                                    in1=dconst[:, d * 512:(d + 1) * 512], op=ALU.mult)
                                N.sSd.inc(ii)
                                g += 1
                            cum = sum(len(jlists[jj]) for jj in range(j + 1))
                            v.wait_ge(N.sAvR.sem, bAvR + cum)
                            for cc in range(2):
                                ii = v.tensor_copy(
                                    out=rtb[:, cc * T + j * 512:cc * T + (j + 1) * 512],
                                    in_=ps_rt[cc][:])
                                N.sEvR.inc(ii)

                fence()

                if taps and seq == 0:
                    @block.gpsimd
                    def _(g):
                        for sl, (src_, c0) in enumerate([(qtb, 0), (ktb, 0),
                                                         (vtb, 0), (rtb, 0)]):
                            ii = g.dma_start(out=dbg_b[3 + sl, :, :],
                                             in_=src_[:, c0:c0 + 512])
                            N.dG.inc(ii, 16)
                        g.wait_ge(N.dG.sem, N.dG.n)
                    fence()

                with ExitStack() as s1:
                    xrs = sb(s1, "xrs", [128, 2 * dm], F32)
                    dbgh = sb(s1, "dbgh", [128, dm + 16], F32) if (taps and seq == 0) else None
                    h1b = sb(s1, "h1b", [128, 2 * dm], F32)
                    sqb = sb(s1, "sqb", [128, dm], F32)
                    yst = sb(s1, "yst", [128, 2 * dm], F32)
                    ybb = sb(s1, "ybb", [128, 2 * dm], BF16)
                    ps_ao = [pst(s1, "ps_ao", [128, 1024]) for _ in range(2)]
                    ps_yt = pst(s1, "ps_yt", [128, 2 * 1024], BF16)
                    bSy, bG = N.dSy.n, N.dG.n
                    bAo, bH1, bVar, bRstd = N.sAoMM.n, N.sH1.n, N.sVar.n, N.sRstd.n
                    bY, bYb, bTrY, bEvY = N.sY.n, N.sYb.n, N.sTrY.n, N.sEvY.n
                    bSqA = N.sSq.n

                    @block.sync
                    def _(sy):
                        for i in range(NT):
                            if i >= 2:
                                sy.wait_ge(N.sH1.sem, bH1 + i - 1)
                            ii = sy.dma_start(out=xrs[:, (i % 2) * dm:(i % 2 + 1) * dm],
                                              in_=xin[seq, i * 128:(i + 1) * 128, :])
                            N.dSy.inc(ii, 16)

                    @block.tensor
                    def _(t):
                        for i in range(NT):
                            if i >= 2:
                                t.wait_ge(N.sH1.sem, bH1 + i - 1)
                            for nh in range(2):
                                for cc in range(2):
                                    mm = t.matmul(
                                        ps_ao[i % 2][:, nh * 512:(nh + 1) * 512],
                                        lhsT=rtb[:, cc * T + i * 128:cc * T + (i + 1) * 128],
                                        rhs=rowT[:, cc * dm + nh * 512:cc * dm + (nh + 1) * 512],
                                        start=(cc == 0), stop=(cc == 1))
                            N.sAoMM.inc(mm)
                            if i >= 1:
                                j = i - 1
                                t.wait_ge(N.sYb.sem, bYb + j + 1)
                                if j >= 2:
                                    t.wait_ge(N.sEvY.sem, bEvY + j - 1)
                                for c in range(DCH):
                                    mm = t.transpose(
                                        ps_yt[:, (j % 2) * 1024 + c * 128:
                                              (j % 2) * 1024 + (c + 1) * 128],
                                        ybb[:, (j % 2) * dm + c * 128:(j % 2) * dm + (c + 1) * 128],
                                        ident[:])
                                N.sTrY.inc(mm)
                        j = NT - 1
                        t.wait_ge(N.sYb.sem, bYb + NT)
                        if j >= 2:
                            t.wait_ge(N.sEvY.sem, bEvY + j - 1)
                        for c in range(DCH):
                            mm = t.transpose(
                                ps_yt[:, (j % 2) * 1024 + c * 128:(j % 2) * 1024 + (c + 1) * 128],
                                ybb[:, (j % 2) * dm + c * 128:(j % 2) * dm + (c + 1) * 128],
                                ident[:])
                        N.sTrY.inc(mm)

                    @block.vector
                    def _(v):
                        def emit_y(j):
                            pj = j % 2
                            cb = pj * 8
                            v.wait_ge(N.sRstd.sem, bRstd + j + 1)
                            if j >= 2:
                                v.wait_ge(N.dG.sem, bG + 16 * (j - 1))
                            ii = v.tensor_scalar(
                                out=yst[:, pj * dm:(pj + 1) * dm],
                                in0=h1b[:, pj * dm:(pj + 1) * dm],
                                scalar1=lncols[:, cb + 0:cb + 1],
                                scalar2=lncols[:, cb + 5:cb + 6],
                                op0=ALU.subtract, op1=ALU.mult)
                            N.sY.inc(ii)
                            ii = v.tensor_copy(out=ybb[:, pj * dm:(pj + 1) * dm],
                                               in_=yst[:, pj * dm:(pj + 1) * dm])
                            N.sYb.inc(ii)
                            if j >= 1:
                                jj = j - 1
                                v.wait_ge(N.sTrY.sem, bTrY + jj + 1)
                                dst = xyT[:].rearrange("p (c n) -> p c n", c=DCH)[
                                    :, :, jj * 128:(jj + 1) * 128]
                                src = ps_yt[:, (jj % 2) * 1024:(jj % 2 + 1) * 1024
                                            ].rearrange("p (c n) -> p c n", c=DCH)
                                ii = v.tensor_copy(out=dst, in_=src)
                                N.sEvY.inc(ii)

                        for i in range(NT):
                            pc = i % 2
                            cb = pc * 8
                            v.wait_ge(N.sAoMM.sem, bAo + i + 1)
                            v.wait_ge(N.dSy.sem, bSy + 16 * (i + 1))
                            ii = v.scalar_tensor_tensor(
                                out=h1b[:, pc * dm:(pc + 1) * dm], in0=ps_ao[pc][:],
                                scalar=1.0, in1=xrs[:, pc * dm:(pc + 1) * dm],
                                op0=ALU.mult, op1=ALU.add,
                                accum_out=lncols[:, cb + 3:cb + 4])
                            N.sH1.inc(ii)
                            v.drain()
                            v.tensor_scalar(out=lncols[:, cb + 0:cb + 1],
                                            in0=lncols[:, cb + 3:cb + 4],
                                            scalar1=1.0 / dm, scalar2=None, op0=ALU.mult)
                            v.drain()
                            v.tensor_tensor(out=lncols[:, cb + 1:cb + 2],
                                            in0=lncols[:, cb + 0:cb + 1],
                                            in1=lncols[:, cb + 0:cb + 1], op=ALU.mult)
                            v.wait_ge(N.sSq.sem, bSqA + i + 1)
                            v.tensor_scalar(out=lncols[:, cb + 2:cb + 3],
                                            in0=lncols[:, cb + 4:cb + 5],
                                            scalar1=1.0 / dm, scalar2=None, op0=ALU.mult)
                            v.drain()
                            ii = v.tensor_tensor(out=lncols[:, cb + 2:cb + 3],
                                                 in0=lncols[:, cb + 2:cb + 3],
                                                 in1=lncols[:, cb + 1:cb + 2],
                                                 op=ALU.subtract)
                            N.sVar.inc(ii)
                            if dbgh is not None and i == 0:
                                v.tensor_copy(out=dbgh[:, 0:dm], in_=h1b[:, 0:dm])
                                ii = v.tensor_copy(out=dbgh[:, dm:dm + 16], in_=lncols[:])
                                N.sEvB.inc(ii)
                            if i >= 1:
                                emit_y(i - 1)
                        emit_y(NT - 1)
                        j = NT - 1
                        v.wait_ge(N.sTrY.sem, bTrY + NT)
                        dst = xyT[:].rearrange("p (c n) -> p c n", c=DCH)[
                            :, :, j * 128:(j + 1) * 128]
                        src = ps_yt[:, (j % 2) * 1024:(j % 2 + 1) * 1024
                                    ].rearrange("p (c n) -> p c n", c=DCH)
                        ii = v.tensor_copy(out=dst, in_=src)
                        N.sEvY.inc(ii)

                    @block.scalar
                    def _(s):
                        for i in range(NT):
                            cb = (i % 2) * 8
                            s.wait_ge(N.sH1.sem, bH1 + i + 1)
                            ii = s.activation(out=sqb[:],
                                              in_=h1b[:, (i % 2) * dm:(i % 2 + 1) * dm],
                                              func=AF.Square,
                                              accum_out=lncols[:, cb + 4:cb + 5])
                            N.sSq.inc(ii)
                            s.wait_ge(N.sVar.sem, bVar + i + 1)
                            s.activation(out=lncols[:, cb + 6:cb + 7],
                                         in_=lncols[:, cb + 2:cb + 3],
                                         func=AF.Ln, bias=cbias[:, 0:1])
                            ii = s.activation(out=lncols[:, cb + 5:cb + 6],
                                              in_=lncols[:, cb + 6:cb + 7],
                                              func=AF.Exp, scale=-0.5)
                            N.sRstd.inc(ii)

                    @block.gpsimd
                    def _(g):
                        for i in range(NT):
                            g.wait_ge(N.sY.sem, bY + i + 1)
                            ii = g.dma_start(out=sc_y[seq, i * 128:(i + 1) * 128, :],
                                             in_=yst[:, (i % 2) * dm:(i % 2 + 1) * dm])
                            N.dG.inc(ii, 16)
                        if dbgh is not None:
                            g.wait_ge(N.sEvB.sem, N.sEvB.n)
                            ii = g.dma_start(out=dbg_h[:, :], in_=dbgh[:])
                            N.dG.inc(ii, 16)
                            g.wait_ge(N.dG.sem, N.dG.n)

                fence()

                with ExitStack() as s2:
                    prod = sb(s2, "prod", [128, NT * 1040], BF16)
                    wst = sb(s2, "wst", [128, 2 * 1024], F32)
                    wbb = sb(s2, "wbb", [128, DCH * 1040], BF16)
                    ps_pA = pst(s2, "ps_pA", [128, 512])
                    ps_pB = pst(s2, "ps_pB", [128, 512])
                    ps_p16 = pst(s2, "ps_p16", [128, 16])

                    def emit_part(part, seq=seq, prod=prod, wst=wst, wbb=wbb,
                                  stage=0):
                        if True:
                            if part < 2:
                                ps_p = [ps_pA, ps_pB]
                            else:
                                ps_p = [ps_pA, ps_pB, ps_p16]
                            if stage != 2:
                                emit_part.bases = (N.dSy.n, N.sWcast.n, N.sQKMM.n,
                                                   N.sEvP.n, N.dG.n, N.sDotMM.n)
                            bSy, bW, bM, bE, bG0, bDot0 = emit_part.bases

                            if stage == 2:
                                pass
                            else:
                              @block.sync
                              def _(sy):
                                for c in range(DCH):
                                    if c >= 2:
                                        sy.wait_ge(N.sWcast.sem, bW + c - 1)
                                    elif part > 0:
                                        sy.wait_ge(N.sWcast.sem, bW)
                                    ii = sy.dma_start(
                                        out=wst[:, (c % 2) * 1024:(c % 2 + 1) * 1024],
                                        in_=wqkv[c * 128:(c + 1) * 128,
                                                 part * dm:(part + 1) * dm])
                                    N.dSy.inc(ii, 16)

                            if stage == 2:
                                pass
                            else:
                              @block.vector
                              def _(v):
                                v.wait_ge(N.sQKMM.sem, bM)
                                for c in range(DCH):
                                    v.wait_ge(N.dSy.sem, bSy + 16 * (c + 1))
                                    if part < 2:
                                        v.memset(wbb[:, c * 1040 + 1024:(c + 1) * 1040], 0.0)
                                        ii = v.tensor_copy(
                                            out=wbb[:, c * 1040:c * 1040 + 1024],
                                            in_=wst[:, (c % 2) * 1024:(c % 2 + 1) * 1024])
                                    else:
                                        vv = wbb[:, c * 1040:(c + 1) * 1040].rearrange(
                                            "p (h j) -> p h j", j=65)[:, :, 64:65]
                                        v.memset(vv, 0.0)
                                        for h in range(H):
                                            ii = v.tensor_copy(
                                                out=wbb[:, c * 1040 + h * 65:
                                                        c * 1040 + h * 65 + 64],
                                                in_=wst[:, (c % 2) * 1024 + h * 64:
                                                        (c % 2) * 1024 + (h + 1) * 64])
                                    N.sWcast.inc(ii)

                            if stage == 1:
                                return
                            @block.tensor
                            def _(t):
                                t.wait_ge(N.sWcast.sem, bW + DCH)
                                t.wait_ge(N.sEvP.sem, bE)
                                if part < 2:
                                    k = 0
                                    for mc in range(DCH):
                                        for nt in range(NQ):
                                            if k >= 2:
                                                t.wait_ge(N.sEvP.sem, bE + k - 1)
                                            for kc in range(DCH):
                                                mm = t.matmul(
                                                    ps_p[k % 2][:],
                                                    lhsT=wbb[:, kc * 1040 + mc * 128:
                                                             kc * 1040 + (mc + 1) * 128],
                                                    rhs=xyT[:, kc * T + nt * 512:
                                                            kc * T + (nt + 1) * 512],
                                                    start=(kc == 0), stop=(kc == DCH - 1))
                                            N.sQKMM.inc(mm)
                                            k += 1
                                else:
                                    for i in range(NT):
                                        if i >= 1:
                                            t.wait_ge(N.sEvP.sem, bE + i)
                                        for kc in range(DCH):
                                            for sl, (n0, n1) in enumerate(
                                                    [(0, 512), (512, 1024), (1024, 1040)]):
                                                mm = t.matmul(
                                                    ps_p[sl][:, 0:n1 - n0],
                                                    lhsT=xyT[:, kc * T + i * 128:
                                                             kc * T + (i + 1) * 128],
                                                    rhs=wbb[:, kc * 1040 + n0:kc * 1040 + n1],
                                                    start=(kc == 0), stop=(kc == DCH - 1))
                                        N.sQKMM.inc(mm)

                            @block.vector
                            def _(v):
                                if part >= 1:
                                    v.wait_ge(N.dG.sem, bG0)
                                    v.wait_ge(N.sDotMM.sem, bDot0)
                                if part < 2:
                                    k = 0
                                    for mc in range(DCH):
                                        for nt in range(NQ):
                                            v.wait_ge(N.sQKMM.sem, bM + k + 1)
                                            if part == 0:
                                                ii = v.tensor_scalar(
                                                    out=prod[:, mc * T + nt * 512:
                                                             mc * T + (nt + 1) * 512],
                                                    in0=ps_p[k % 2][:],
                                                    scalar1=scale, scalar2=None,
                                                    op0=ALU.mult)
                                            else:
                                                ii = v.tensor_copy(
                                                    out=prod[:, mc * T + nt * 512:
                                                             mc * T + (nt + 1) * 512],
                                                    in_=ps_p[k % 2][:])
                                            N.sEvP.inc(ii)
                                            k += 1
                                else:
                                    for i in range(NT):
                                        v.wait_ge(N.sQKMM.sem, bM + i + 1)
                                        for sl, (n0, n1) in enumerate(
                                                [(0, 512), (512, 1024), (1024, 1040)]):
                                            v.tensor_copy(
                                                out=prod[:, i * 1040 + n0:i * 1040 + n1],
                                                in_=ps_p[sl][:, 0:n1 - n0])
                                        vv = prod[:, i * 1040:(i + 1) * 1040].rearrange(
                                            "p (h j) -> p h j", j=65)[:, :, 64:65]
                                        ii = v.memset(vv, 1.0)
                                        N.sEvP.inc(ii)

                            @block.gpsimd
                            def _(g):
                                if part < 2:
                                    dstd = sc_q if part == 0 else sc_k
                                    for mc in range(DCH):
                                        g.wait_ge(N.sEvP.sem, bE + (mc + 1) * NQ)
                                        ii = g.dma_start(out=dstd[seq, mc, :, :],
                                                         in_=prod[:, mc * T:(mc + 1) * T])
                                        N.dG.inc(ii, 16)
                                else:
                                    for i in range(NT):
                                        g.wait_ge(N.sEvP.sem, bE + i + 1)
                                        ii = g.dma_start(out=sc_v[seq, i, :, :],
                                                         in_=prod[:, i * 1040:(i + 1) * 1040])
                                        N.dG.inc(ii, 16)
                                g.wait_ge(N.dG.sem, N.dG.n)

                    emit_part(0)
                    emit_part(1, stage=1)

                    with ExitStack() as s3:
                        sq2 = sb(s3, "sq2", [128, 2 * T], BF16)
                        dots = sb(s3, "dots", [16, T], F32)
                        n2q = sb(s3, "n2q", [16, T], F32)
                        rsq_t = sb(s3, "rsq", [16, T], F32)
                        rsq = rsq_t[:, :]
                        ps_sim = [pst(s3, "ps_sim", [16, 512])
                                  for _ in range(T // 512)]
                        ps_mean = pst(s3, "ps_mean", [1, 512])
                        bSq, bN2, bDot, bEvB, bRsq = (N.sSq.n, N.sN2MM.n, N.sDotMM.n,
                                                      N.sEvB.n, N.sRsq.n)
                        bEvP0 = N.sEvP.n

                        @block.vector
                        def _(v):
                            for mc in range(DCH):
                                v.wait_ge(N.sEvP.sem, bEvP0 - DCH * NQ + (mc + 1) * NQ)
                                if mc >= 2:
                                    v.wait_ge(N.sN2MM.sem, bN2 + mc - 1)
                                ii = v.tensor_tensor(
                                    out=sq2[:, (mc % 2) * T:(mc % 2 + 1) * T],
                                    in0=prod[:, mc * T:(mc + 1) * T],
                                    in1=prod[:, mc * T:(mc + 1) * T], op=ALU.mult)
                                N.sSq.inc(ii)

                        @block.tensor
                        def _(t):
                            for mc in range(DCH):
                                t.wait_ge(N.sSq.sem, bSq + mc + 1)
                                for nsl in range(T // 512):
                                    mm = t.matmul(
                                        ps_sim[nsl][:],
                                        lhsT=obd[:, mc * 16:(mc + 1) * 16],
                                        rhs=sq2[:, (mc % 2) * T + nsl * 512:
                                                (mc % 2) * T + (nsl + 1) * 512],
                                        start=(mc == 0), stop=(mc == DCH - 1))
                                N.sN2MM.inc(mm)

                        @block.vector
                        def _(v):
                            v.wait_ge(N.sN2MM.sem, bN2 + DCH)
                            for nsl in range(T // 512):
                                ii = v.tensor_copy(
                                    out=n2q[:, nsl * 512:(nsl + 1) * 512],
                                    in_=ps_sim[nsl][:])
                            N.sEvB.inc(ii)

                        @block.tensor
                        def _(t):
                            t.wait_ge(N.sEvB.sem, bEvB + 1)
                            for mc in range(DCH):
                                for nsl in range(T // 512):
                                    mm = t.matmul(
                                        ps_sim[nsl][:],
                                        lhsT=gbd[:, mc * 16:(mc + 1) * 16],
                                        rhs=prod[:, mc * T + nsl * 512:mc * T + (nsl + 1) * 512],
                                        start=(mc == 0), stop=(mc == DCH - 1))
                            N.sDotMM.inc(mm)

                        @block.scalar
                        def _(s):
                            s.wait_ge(N.sEvB.sem, bEvB + 1)
                            s.activation(out=rsq, in_=n2q[:], func=AF.Ln)
                            ii = s.activation(out=rsq, in_=rsq, func=AF.Exp,
                                              scale=-0.5)
                            N.sRsq.inc(ii)

                        @block.vector
                        def _(v):
                            v.wait_ge(N.sDotMM.sem, bDot + 1)
                            v.wait_ge(N.sRsq.sem, bRsq + 1)
                            for nsl in range(T // 512):
                                v.tensor_copy(out=dots[:, nsl * 512:(nsl + 1) * 512],
                                              in_=ps_sim[nsl][:])
                            ii = v.tensor_tensor(out=dots[:], in0=dots[:], in1=rsq,
                                                 op=ALU.mult)
                            N.sEvB.inc(ii)

                        @block.tensor
                        def _(t):
                            t.wait_ge(N.sEvB.sem, bEvB + 2)
                            for nsl in range(T // 512):
                                if nsl >= 1:
                                    t.wait_ge(N.sEvB.sem, bEvB + 2 + nsl)
                                mm = t.matmul(ps_mean[:], lhsT=ones16f[:],
                                              rhs=dots[:, nsl * 512:(nsl + 1) * 512],
                                              start=True, stop=True)
                                N.sDotMM.inc(mm)

                        @block.vector
                        def _(v):
                            b0 = 0 if seq == 0 else 32
                            simrow = arenaF[b0:b0 + 1, :]
                            for nsl in range(T // 512):
                                v.wait_ge(N.sDotMM.sem, bDot + 2 + nsl)
                                ii = v.tensor_copy(
                                    out=simrow[:, nsl * 512:(nsl + 1) * 512],
                                    in_=ps_mean[:])
                                N.sEvB.inc(ii)
                            v.tensor_reduce(out=scal[0:1, 4 + 2 * seq:5 + 2 * seq],
                                            in_=simrow[:, REG:T], axis=AX.X, op=ALU.max)
                            ii = v.tensor_reduce(out=scal[0:1, 5 + 2 * seq:6 + 2 * seq],
                                                 in_=simrow[:, REG:T], axis=AX.X,
                                                 op=ALU.min)
                            N.sEvB.inc(ii)

                        if taps:
                            @block.gpsimd
                            def _(g):
                                g.wait_ge(N.sEvB.sem, N.sEvB.n)
                                b0 = 0 if seq == 0 else 32
                                ii = g.dma_start(out=sc_sim[seq, :, :],
                                                 in_=arenaF[b0:b0 + 1, :])
                                N.dG.inc(ii, 16)

                    emit_part(1, stage=2)
                    emit_part(2)
                    fence()

        def emit_mid():
            bGmid = N.dG.n
            with ExitStack() as mctx:
                tmpb = sb(mctx, "tmpb", [1, 16], F32)

                @block.vector
                def _(v):
                    if nseq == 2:
                        v.tensor_tensor(out=tmpb[0:1, 1:2], in0=scal[0:1, 4:5],
                                        in1=scal[0:1, 6:7], op=ALU.max)
                        v.tensor_tensor(out=tmpb[0:1, 2:3], in0=scal[0:1, 5:6],
                                        in1=scal[0:1, 7:8], op=ALU.min)
                    else:
                        v.tensor_copy(out=tmpb[0:1, 1:2], in_=scal[0:1, 4:5])
                        v.tensor_copy(out=tmpb[0:1, 2:3], in_=scal[0:1, 5:6])
                    v.drain()
                    ii = v.tensor_scalar(out=tmpb[0:1, 0:1], in0=tmpb[0:1, 2:3],
                                         scalar1=-1.0, scalar2=None, op0=ALU.mult)
                    N.sMid.inc(ii)

                @block.gpsimd
                def _(g):
                    g.wait_ge(N.sMid.sem, N.sMid.n)
                    ii = g.dma_start(out=cc_in[:, :], in_=tmpb[0:1, 0:2])
                    N.dG.inc(ii, 16)
                    g.wait_ge(N.dG.sem, N.dG.n)
                    ii = g.collective_compute(
                        "AllReduce", ALU.max, replica_groups=[list(range(8))],
                        ins=[cc_in.ap()], outs=[cc_out.ap()])
                    N.sCol.inc(ii)
                    g.wait_ge(N.sCol.sem, N.sCol.n)
                    ii = g.dma_start(out=tmpb[0:1, 4:6], in_=cc_out[:, :])
                    N.dG.inc(ii, 16)

                @block.vector
                def _(v):
                    v.wait_ge(N.dG.sem, N.dG.n)
                    v.tensor_scalar(out=tmpb[0:1, 6:7], in0=tmpb[0:1, 4:5],
                                    scalar1=-0.1, scalar2=None, op0=ALU.mult)
                    v.tensor_scalar(out=tmpb[0:1, 7:8], in0=tmpb[0:1, 5:6],
                                    scalar1=0.9, scalar2=None, op0=ALU.mult)
                    v.drain()
                    v.tensor_tensor(out=scal[0:1, 2:3], in0=tmpb[0:1, 6:7],
                                    in1=tmpb[0:1, 7:8], op=ALU.add)
                    v.drain()
                    for seq in range(nseq):
                        if seq >= 1:
                            v.wait_ge(N.dG.sem, bGmid + 32 + 48 * seq)
                        b0 = 0 if seq == 0 else 32
                        simrow = arenaF[b0:b0 + 1, :]
                        v.tensor_scalar(out=simrow[:, :], in0=simrow[:, :],
                                        scalar1=scal[0:1, 2:3], scalar2=0.0,
                                        op0=ALU.subtract, op1=ALU.is_gt)
                        v.tensor_scalar(out=simrow[:, :], in0=simrow[:, :],
                                        scalar1=2.0, scalar2=-1.0, op0=ALU.mult,
                                        op1=ALU.add)
                        v.memset(simrow[:, 0:REG], 1.0)
                        v.drain()
                        v.tensor_copy(out=arenaB[0:1, 0:T], in_=simrow[:, :])
                        v.tensor_scalar(out=arenaB[0:1, T:TF], in0=simrow[:, 0:NFLIP],
                                        scalar1=-1.0, scalar2=None, op0=ALU.mult)
                        v.tensor_scalar(out=arenaB[32:33, 0:T], in0=simrow[:, :],
                                        scalar1=BIGH, scalar2=None, op0=ALU.mult)
                        v.memset(arenaB[32:33, 0:REG], 0.0)
                        v.memset(arenaB[64:65, 0:T], 0.0)
                        ii = v.memset(arenaB[64:65, 0:REG], BIGH)
                        N.sMid.inc(ii)

                bMid = N.sMid.n - nseq

                @block.gpsimd
                def _(g):
                    for seq in range(nseq):
                        g.wait_ge(N.sMid.sem, bMid + seq + 1)
                        ii = g.dma_start(out=sc_k3[seq, 0:1, :], in_=arenaB[32:33, 0:T])
                        N.dG.inc(ii, 16)
                        ii = g.dma_start(out=sc_k3[seq, 2:3, :], in_=arenaB[64:65, 0:T])
                        N.dG.inc(ii, 16)
                        ii = g.dma_start(out=sc_q2[seq, 0:1, :], in_=arenaB[0:1, 0:TF])
                        N.dG.inc(ii, 16)

                @block.vector
                def _(v):
                    v.wait_ge(N.dG.sem, N.dG.n)
                    ii = v.memset(arenaB[0:1, 0:TF], 1.0)
                    N.sMid.inc(ii)

                @block.gpsimd
                def _(g):
                    g.wait_ge(N.sMid.sem, N.sMid.n)
                    for seq in range(nseq):
                        ii = g.dma_start(out=sc_k3[seq, 1:2, :], in_=arenaB[0:1, 0:T])
                        N.dG.inc(ii, 16)
                        ii = g.dma_start(out=sc_q2[seq, 1:2, :], in_=arenaB[0:1, 0:TF])
                        N.dG.inc(ii, 16)


        fence()

        with ExitStack() as bctx:
            vsb = sb(bctx, "vsb", [128, NT * 1040], BF16)
            oT = sb(bctx, "oT", [128, DCH * T], BF16)
            qch = sb(bctx, "qch", [128, 2 * T], BF16)
            kch = sb(bctx, "kch", [128, 2 * T], BF16)
            qpb = sb(bctx, "qpb", [67, 2 * TF], BF16)
            kpb = sb(bctx, "kpb", [67, 2 * T], BF16)
            ptb = sb(bctx, "ptb", [128, 2 * 1024], BF16)
            osb = sb(bctx, "osb", [64, 512], BF16)
            wpb = sb(bctx, "wpb", [128, DCH * 1024], BF16)
            flb = sb(bctx, "flb", [128, DCH * NFLIP], BF16)
            fltmp = sb(bctx, "fltmp", [64, REG], F32)
            denr = sb(bctx, "denr", [1, 1024], F32)
            recr = None

            with ExitStack() as s0:
                wpl = sb(s0, "wpl", [128, 2 * 1024], F32)
                bSy, bW = N.dSy.n, N.sWcast.n

                @block.sync
                def _(sy):
                    for c in range(DCH):
                        if c >= 2:
                            sy.wait_ge(N.sWcast.sem, bW + c - 1)
                        ii = sy.dma_start(out=wpl[:, (c % 2) * 1024:(c % 2 + 1) * 1024],
                                          in_=wproj[c * 128:(c + 1) * 128, :])
                        N.dSy.inc(ii, 16)

                @block.vector
                def _(v):
                    for c in range(DCH):
                        v.wait_ge(N.dSy.sem, bSy + 16 * (c + 1))
                        ii = v.tensor_copy(out=wpb[:, c * 1024:(c + 1) * 1024],
                                           in_=wpl[:, (c % 2) * 1024:(c % 2 + 1) * 1024])
                        N.sWcast.inc(ii)

            fence()

            bGA = N.dG.n
            for seq in range(nseq):
                NQT = NQ + 1
                NG = NT // 2
                QW = [512] * NQ + [NFLIP]
                QOF = [512 * i for i in range(NQ)] + [T]

                with ExitStack() as s1:
                    sqh = sb(s1, "sqh", [128, 2 * T], BF16)
                    kmx = sb(s1, "kmx", [16, 12], F32)
                    mall = sb(s1, "mall", [16, T], F32)
                    mneg = sb(s1, "mneg", [16, TF], BF16)
                    ps_n1 = [pst(s1, "ps_n1", [16, 512]) for _ in range(T // 512)]
                    ps_n2 = [pst(s1, "ps_n2", [16, 512]) for _ in range(T // 512)]
                    bSy, bSq, bN2, bEvB, bRsq, bB = (N.dSy.n, N.sSq.n, N.sN2MM.n,
                                                     N.sEvB.n, N.sRsq.n, N.dB.n)

                    @block.sync
                    def _(sy):
                        sy.wait_ge(N.dG.sem, bGA)
                        sy.wait_ge(N.sQKb.sem, N.sQKb.n)
                        for c in range(DCH):
                            if c >= 2:
                                sy.wait_ge(N.sSq.sem, bSq + c - 1)
                            ii = sy.dma_start(out=qch[:, (c % 2) * T:(c % 2 + 1) * T],
                                              in_=sc_q[seq, c, :, :])
                            N.dSy.inc(ii, 16)
                            ii = sy.dma_start(out=kch[:, (c % 2) * T:(c % 2 + 1) * T],
                                              in_=sc_k[seq, c, :, :])
                            N.dSy.inc(ii, 16)

                    @block.vector
                    def _(v):
                        for c in range(DCH):
                            v.wait_ge(N.dSy.sem, bSy + 32 * (c + 1))
                            if c >= 1:
                                v.wait_ge(N.sN2MM.sem, bN2 + c)
                            v.tensor_tensor(out=sqh[:, 0:T],
                                            in0=qch[:, (c % 2) * T:(c % 2 + 1) * T],
                                            in1=qch[:, (c % 2) * T:(c % 2 + 1) * T],
                                            op=ALU.mult)
                            ii = v.tensor_tensor(out=sqh[:, T:2 * T],
                                                 in0=kch[:, (c % 2) * T:(c % 2 + 1) * T],
                                                 in1=kch[:, (c % 2) * T:(c % 2 + 1) * T],
                                                 op=ALU.mult)
                            N.sSq.inc(ii)

                    @block.tensor
                    def _(t):
                        for c in range(DCH):
                            t.wait_ge(N.sSq.sem, bSq + c + 1)
                            for w in range(2):
                                pdst = ps_n1 if w == 0 else ps_n2
                                for nsl in range(T // 512):
                                    mm = t.matmul(
                                        pdst[nsl][:],
                                        lhsT=obd[:, c * 16:(c + 1) * 16],
                                        rhs=sqh[:, w * T + nsl * 512:w * T + (nsl + 1) * 512],
                                        start=(c == 0), stop=(c == DCH - 1))
                            N.sN2MM.inc(mm)

                    @block.vector
                    def _(v):
                        v.wait_ge(N.sN2MM.sem, bN2 + DCH)
                        for nsl in range(T // 512):
                            v.tensor_reduce(out=kmx[:, 4 + nsl:5 + nsl],
                                            in_=ps_n2[nsl][:], axis=AX.X, op=ALU.max)
                        v.drain()
                        ii = v.tensor_reduce(out=kmx[:, 0:1],
                                             in_=kmx[:, 4:4 + T // 512], axis=AX.X,
                                             op=ALU.max)
                        N.sEvB.inc(ii)

                    @block.scalar
                    def _(s):
                        s.wait_ge(N.sEvB.sem, bEvB + 1)
                        ii = s.activation(out=kmx[:, 1:2], in_=kmx[:, 0:1], func=AF.Ln)
                        N.sRsq.inc(ii)

                    @block.vector
                    def _(v):
                        v.wait_ge(N.sRsq.sem, bRsq + 1)
                        ii = v.tensor_scalar(out=kmx[:, 2:3], in0=kmx[:, 1:2],
                                             scalar1=0.5, scalar2=None, op0=ALU.mult)
                        N.sEvB.inc(ii)

                    @block.scalar
                    def _(s):
                        s.wait_ge(N.sEvB.sem, bEvB + 2)
                        for nsl in range(T // 512):
                            s.activation(out=mall[:, nsl * 512:(nsl + 1) * 512],
                                         in_=ps_n1[nsl][:], func=AF.Ln)
                        ii = s.activation(out=mall[:, 0:T], in_=mall[:, 0:T],
                                          func=AF.Exp, scale=0.5, bias=kmx[:, 2:3])
                        N.sRsq.inc(ii)

                    @block.vector
                    def _(v):
                        v.wait_ge(N.sRsq.sem, bRsq + 2)
                        v.tensor_scalar(out=mneg[:, 0:T], in0=mall[:, 0:T],
                                        scalar1=-1.0, scalar2=None, op0=ALU.mult)
                        ii = v.tensor_scalar(out=mneg[:, T:TF], in0=mall[:, 0:NFLIP],
                                             scalar1=-1.0, scalar2=None, op0=ALU.mult)
                        N.sEvB.inc(ii)

                    @block.gpsimd
                    def _(g):
                        g.wait_ge(N.sEvB.sem, N.sEvB.n)
                        for h in range(H):
                            ii = g.dma_start(out=sc_m[seq, h, :], in_=mneg[h:h + 1, :])
                            N.dB.inc(ii, 16)

                if seq == 0:
                    emit_mid()
                fence()

                with ExitStack() as s1:
                    ps_log = [pst(s1, "ps_log", [128, 1024]) for _ in range(2)]
                    ps_av = [pst(s1, "ps_av", [65, 512]) for _ in range(2)]
                    ps_bcp = pst(s1, "ps_bcp", [64, 512])
                    bLog, bExp, bAv, bDen, bBc, bNorm, bQKb = (
                        N.sLog.n, N.sExp.n, N.sAv.n, N.sDen.n, N.sBc.n, N.sNorm.n,
                        N.sQKb.n)
                    bSy, bB = N.dSy.n, N.dB.n
                    bSqB = N.sSq.n

                    def idx(h, qt):
                        return h * NQT + qt

                    NGq = [NG] * NQ + [1]
                    GPH = sum(NGq)
                    G0T = {}
                    acc = 0
                    for hh in range(H):
                        for qq in range(NQT):
                            G0T[(hh, qq)] = acc
                            acc += NGq[qq]
                    NGG_ALL = acc

                    def gofq(h, qt):
                        return G0T[(h, qt)]

                    @block.sync
                    def _(sy):
                        sy.wait_ge(N.dG.sem, N.dG.n)
                        sy.wait_ge(N.sAv.sem, bAv)
                        sy.wait_ge(N.sLog.sem, bLog)
                        for i in range(NT):
                            ii = sy.dma_start(out=vsb[:, i * 1040:(i + 1) * 1040],
                                              in_=sc_v[seq, i, :, :])
                            N.dSy.inc(ii, 16)
                        for par in range(2):
                            ii = sy.dma_start(out=kpb[64:67, par * T:(par + 1) * T],
                                              in_=sc_k3[seq, :, :])
                            N.dSy.inc(ii, 16)
                            ii = sy.dma_start(out=qpb[64:65, par * TF:(par + 1) * TF],
                                              in_=sc_q2[seq, 0:1, :])
                            N.dSy.inc(ii, 16)
                            ii = sy.dma_start(out=qpb[66:67, par * TF:(par + 1) * TF],
                                              in_=sc_q2[seq, 1:2, :])
                            N.dSy.inc(ii, 16)
                        for c in range(DCH):
                            if c >= 2:
                                sy.wait_ge(N.sQKb.sem, bQKb + 2 * (c - 1))
                            else:
                                sy.wait_ge(N.sSq.sem, bSqB)
                            ii = sy.dma_start(out=qch[:, (c % 2) * T:(c % 2 + 1) * T],
                                              in_=sc_q[seq, c, :, :])
                            N.dSy.inc(ii, 16)
                            ii = sy.dma_start(out=kch[:, (c % 2) * T:(c % 2 + 1) * T],
                                              in_=sc_k[seq, c, :, :])
                            N.dSy.inc(ii, 16)

                    @block.gpsimd
                    def _(g):
                        g.wait_ge(N.dB.sem, bB)
                        for h in range(H):
                            if h >= 2:
                                g.wait_ge(N.sLog.sem, bLog + (h - 1) * GPH)
                            else:
                                g.wait_ge(N.sLog.sem, bLog)
                            ii = g.dma_start(
                                out=qpb[65:66, (h % 2) * TF:(h % 2 + 1) * TF],
                                in_=sc_m[seq, h, :])
                            N.dB.inc(ii, 16)

                    def emit_build(v, hb):
                        c = hb // 2
                        hw = (hb % 2) * 64
                        v.wait_ge(N.dSy.sem, bSy + 16 * (NT + 6 + 2 * (c + 1)))
                        if hb >= 1:
                            v.wait_ge(N.sAv.sem, bAv + (hb - 1) * NQT)
                        v.tensor_copy(
                            out=qpb[0:64, (hb % 2) * TF:(hb % 2) * TF + T],
                            in_=qch[hw:hw + 64, (c % 2) * T:(c % 2 + 1) * T])
                        v.tensor_copy(
                            out=qpb[0:64, (hb % 2) * TF + T:(hb % 2 + 1) * TF],
                            in_=qch[hw:hw + 64, (c % 2) * T:(c % 2) * T + NFLIP])
                        ii = v.tensor_copy(
                            out=kpb[0:64, (hb % 2) * T:(hb % 2 + 1) * T],
                            in_=kch[hw:hw + 64, (c % 2) * T:(c % 2 + 1) * T])
                        N.sQKb.inc(ii)

                    @block.vector
                    def _(v):
                        emit_build(v, 0)
                        for h in range(H):
                            hw = (h % 2) * 64
                            if h + 1 < H:
                                emit_build(v, h + 1)
                            for qt in range(NQT):
                                ix = idx(h, qt)
                                v.wait_ge(N.sAv.sem, bAv + ix + 1)
                                v.tensor_copy(
                                    out=denr[0:1, 0:QW[qt]],
                                    in_=ps_av[ix % 2][64:65, 0:QW[qt]])
                                if ix >= 1:
                                    v.wait_ge(N.sBc.sem, bBc + ix)
                                v.drain()
                                ii = v.reciprocal(out=denr[0:1, 512:512 + QW[qt]],
                                                  in_=denr[0:1, 0:QW[qt]])
                                N.sDen.inc(ii)
                                v.wait_ge(N.sBc.sem, bBc + ix + 1)
                                v.tensor_copy(
                                    out=osb[:, 0:QW[qt]],
                                    in_=ps_av[ix % 2][0:64, 0:QW[qt]])
                                v.drain()
                                if qt < NQT - 1:
                                    dst = oT[hw:hw + 64,
                                             (h // 2) * T + QOF[qt]:(h // 2) * T + QOF[qt] + QW[qt]]
                                else:
                                    dst = flb[hw:hw + 64,
                                              (h // 2) * NFLIP:(h // 2 + 1) * NFLIP]
                                ii = v.tensor_tensor(out=dst, in0=osb[:, 0:QW[qt]],
                                                     in1=ps_bcp[:, 0:QW[qt]],
                                                     op=ALU.mult)
                                N.sNorm.inc(ii)
                            oslc = oT[hw:hw + 64, (h // 2) * T:(h // 2) * T + REG]
                            fslc = flb[hw:hw + 64, (h // 2) * NFLIP:(h // 2) * NFLIP + REG]
                            v.drain()
                            v.tensor_tensor(out=fltmp[:, :], in0=oslc, in1=fslc,
                                            op=ALU.add)
                            v.drain()
                            v.tensor_scalar(out=oslc, in0=fltmp[:, :], scalar1=0.5,
                                            scalar2=None, op0=ALU.mult)

                    @block.tensor
                    def _(t):
                        NGG = NGG_ALL
                        DECT = {}
                        for hh in range(H):
                            for qq in range(NQT):
                                for g_ in range(NGq[qq]):
                                    DECT[G0T[(hh, qq)] + g_] = (hh, qq, g_)

                        def emit_av(t, gg):
                            hh, qq, g = dec_ = DECT[gg]
                            ixx = idx(hh, qq)
                            if g == 0 and ixx >= 2:
                                t.wait_ge(N.sNorm.sem, bNorm + ixx - 1)
                            t.wait_ge(N.sExp.sem, bExp + gg + 1)
                            if qq < NQ:
                                for half in range(2):
                                    kt = 2 * g + half
                                    mm = t.matmul(
                                        ps_av[ixx % 2][:, 0:QW[qq]],
                                        lhsT=vsb[:, kt * 1040 + hh * 65:
                                                 kt * 1040 + (hh + 1) * 65],
                                        rhs=ptb[:, (gg % 2) * 1024 + half * QW[qq]:
                                                (gg % 2) * 1024 + (half + 1) * QW[qq]],
                                        start=(g == 0 and half == 0),
                                        stop=(g == NG - 1 and half == 1))
                                if g == NG - 1:
                                    N.sAv.inc(mm)
                            else:
                                for kt in range(NT):
                                    mm = t.matmul(
                                        ps_av[ixx % 2][:, 0:NFLIP],
                                        lhsT=vsb[:, kt * 1040 + hh * 65:
                                                 kt * 1040 + (hh + 1) * 65],
                                        rhs=ptb[:, (gg % 2) * 1024 + kt * NFLIP:
                                                (gg % 2) * 1024 + (kt + 1) * NFLIP],
                                        start=(kt == 0), stop=(kt == NT - 1))
                                N.sAv.inc(mm)

                        def emit_bcast(t, ixx):
                            t.wait_ge(N.sDen.sem, bDen + ixx + 1)
                            t.wait_ge(N.sNorm.sem, bNorm + max(0, ixx))
                            pqw = QW[ixx % NQT]
                            mm = t.matmul(ps_bcp[:, 0:pqw], lhsT=onesf[0:1, 0:64],
                                          rhs=denr[0:1, 512:512 + pqw],
                                          start=True, stop=True)
                            N.sBc.inc(mm)

                        cslack = min(7, NG - 1)
                        slot_of = {}
                        tail_bcasts = []
                        for j in range(H * NQT):
                            hh, qq = j // NQT, j % NQT
                            gend = G0T[(hh, qq)] + NGq[qq]
                            s_ = gend + cslack
                            if j + 1 < H * NQT:
                                h2_, q2_ = (j + 1) // NQT, (j + 1) % NQT
                                gend2 = G0T[(h2_, q2_)] + NGq[q2_]
                                s_ = min(s_, gend2)
                            if s_ >= NGG:
                                tail_bcasts.append(j)
                            else:
                                slot_of.setdefault(s_, []).append(j)

                        for h in range(H):
                            t.wait_ge(N.sQKb.sem, bQKb + h + 1)
                            t.wait_ge(N.dB.sem, bB + 16 * (h + 1))
                            for qt in range(NQT):
                                ix = idx(h, qt)
                                g0 = gofq(h, qt)
                                for g in range(NGq[qt]):
                                    gg = g0 + g
                                    for j in slot_of.get(gg, []):
                                        emit_bcast(t, j)
                                    if gg >= 2:
                                        t.wait_ge(N.sExp.sem, bExp + gg - 1)
                                    if qt < NQ:
                                        for half in range(2):
                                            kt = 2 * g + half
                                            kk = 67 if kt == 0 else 66
                                            mm = t.matmul(
                                                ps_log[gg % 2][:, half * QW[qt]:
                                                               (half + 1) * QW[qt]],
                                                lhsT=kpb[0:kk, (h % 2) * T + kt * 128:
                                                         (h % 2) * T + (kt + 1) * 128],
                                                rhs=qpb[0:kk, (h % 2) * TF + QOF[qt]:
                                                        (h % 2) * TF + QOF[qt] + QW[qt]],
                                                start=True, stop=True)
                                    else:
                                        for kt in range(NT):
                                            kk = 67 if kt == 0 else 66
                                            mm = t.matmul(
                                                ps_log[gg % 2][:, kt * NFLIP:
                                                               (kt + 1) * NFLIP],
                                                lhsT=kpb[0:kk, (h % 2) * T + kt * 128:
                                                         (h % 2) * T + (kt + 1) * 128],
                                                rhs=qpb[0:kk, (h % 2) * TF + QOF[qt]:
                                                        (h % 2) * TF + QOF[qt] + QW[qt]],
                                                start=True, stop=True)
                                    N.sLog.inc(mm)
                                    if gg >= 1:
                                        emit_av(t, gg - 1)
                        emit_av(t, NGG - 1)
                        for j in tail_bcasts:
                            emit_bcast(t, j)

                    @block.scalar
                    def _(s):
                        for h in range(H):
                            for qt in range(NQT):
                                g0 = gofq(h, qt)
                                for g in range(NGq[qt]):
                                    gg = g0 + g
                                    s.wait_ge(N.sLog.sem, bLog + gg + 1)
                                    w2 = 2 * QW[qt] if qt < NQ else NT * NFLIP
                                    ii = s.activation(
                                        out=ptb[:, (gg % 2) * 1024:(gg % 2) * 1024 + w2],
                                        in_=ps_log[gg % 2][:, 0:w2],
                                        func=AF.Exp, bias=cbias[:, 1:2])
                                    N.sExp.inc(ii)

                fence()

                with ExitStack() as s1:
                    yrl = sb(s1, "yrl", [128, 2 * dm], F32)
                    h2b = sb(s1, "h2b", [128, 2 * dm], F32)
                    sqc = sb(s1, "sqc", [128, dm], F32)
                    zst = sb(s1, "zst", [128, 2 * dm], F32)
                    ps_pj = [pst(s1, "ps_pj", [128, 1024]) for _ in range(2)]
                    bSy, bG = N.dSy.n, N.dG.n
                    bPj, bH2, bVar2, bRstd2, bZ = (N.sPjMM.n, N.sH2.n, N.sVar2.n,
                                                   N.sRstd2.n, N.sZ.n)
                    bSqB = N.sSq.n

                    @block.sync
                    def _(sy):
                        sy.wait_ge(N.dG.sem, bG)
                        for i in range(NT):
                            if i >= 2:
                                sy.wait_ge(N.sH2.sem, bH2 + i - 1)
                            ii = sy.dma_start(out=yrl[:, (i % 2) * dm:(i % 2 + 1) * dm],
                                              in_=sc_y[seq, i * 128:(i + 1) * 128, :])
                            N.dSy.inc(ii, 16)

                    @block.tensor
                    def _(t):
                        for i in range(NT):
                            if i >= 2:
                                t.wait_ge(N.sH2.sem, bH2 + i - 1)
                            for nh in range(2):
                                for cc in range(DCH):
                                    mm = t.matmul(
                                        ps_pj[i % 2][:, nh * 512:(nh + 1) * 512],
                                        lhsT=oT[:, cc * T + i * 128:cc * T + (i + 1) * 128],
                                        rhs=wpb[:, cc * 1024 + nh * 512:cc * 1024 + (nh + 1) * 512],
                                        start=(cc == 0), stop=(cc == DCH - 1))
                            N.sPjMM.inc(mm)

                    @block.vector
                    def _(v):
                        def emit_z(j):
                            pj = j % 2
                            cb = pj * 8
                            v.wait_ge(N.sRstd2.sem, bRstd2 + j + 1)
                            if j >= 2:
                                v.wait_ge(N.dG.sem, bG + 16 * (j - 1))
                            ii = v.tensor_scalar(
                                out=zst[:, pj * dm:(pj + 1) * dm],
                                in0=h2b[:, pj * dm:(pj + 1) * dm],
                                scalar1=lncols[:, cb + 0:cb + 1],
                                scalar2=lncols[:, cb + 5:cb + 6],
                                op0=ALU.subtract, op1=ALU.mult)
                            N.sZ.inc(ii)

                        for i in range(NT):
                            pc = i % 2
                            cb = pc * 8
                            v.wait_ge(N.sPjMM.sem, bPj + i + 1)
                            v.wait_ge(N.dSy.sem, bSy + 16 * (i + 1))
                            v.scalar_tensor_tensor(
                                out=h2b[:, pc * dm:(pc + 1) * dm], in0=ps_pj[pc][:],
                                scalar=1.0, in1=yrl[:, pc * dm:(pc + 1) * dm],
                                op0=ALU.mult, op1=ALU.add)
                            ii = v.scalar_tensor_tensor(
                                out=h2b[:, pc * dm:(pc + 1) * dm],
                                in0=h2b[:, pc * dm:(pc + 1) * dm],
                                scalar=1.0, in1=bias_bc[:], op0=ALU.mult, op1=ALU.add,
                                accum_out=lncols[:, cb + 3:cb + 4])
                            N.sH2.inc(ii)
                            v.drain()
                            v.tensor_scalar(out=lncols[:, cb + 0:cb + 1],
                                            in0=lncols[:, cb + 3:cb + 4],
                                            scalar1=1.0 / dm, scalar2=None, op0=ALU.mult)
                            v.drain()
                            v.tensor_tensor(out=lncols[:, cb + 1:cb + 2],
                                            in0=lncols[:, cb + 0:cb + 1],
                                            in1=lncols[:, cb + 0:cb + 1], op=ALU.mult)
                            v.wait_ge(N.sSq.sem, bSqB + i + 1)
                            v.tensor_scalar(out=lncols[:, cb + 2:cb + 3],
                                            in0=lncols[:, cb + 4:cb + 5],
                                            scalar1=1.0 / dm, scalar2=None, op0=ALU.mult)
                            v.drain()
                            ii = v.tensor_tensor(out=lncols[:, cb + 2:cb + 3],
                                                 in0=lncols[:, cb + 2:cb + 3],
                                                 in1=lncols[:, cb + 1:cb + 2],
                                                 op=ALU.subtract)
                            N.sVar2.inc(ii)
                            if i >= 1:
                                emit_z(i - 1)
                        emit_z(NT - 1)

                    @block.scalar
                    def _(s):
                        for i in range(NT):
                            cb = (i % 2) * 8
                            s.wait_ge(N.sH2.sem, bH2 + i + 1)
                            ii = s.activation(out=sqc[:],
                                              in_=h2b[:, (i % 2) * dm:(i % 2 + 1) * dm],
                                              func=AF.Square,
                                              accum_out=lncols[:, cb + 4:cb + 5])
                            N.sSq.inc(ii)
                            s.wait_ge(N.sVar2.sem, bVar2 + i + 1)
                            s.activation(out=lncols[:, cb + 6:cb + 7],
                                         in_=lncols[:, cb + 2:cb + 3],
                                         func=AF.Ln, bias=cbias[:, 0:1])
                            ii = s.activation(out=lncols[:, cb + 5:cb + 6],
                                              in_=lncols[:, cb + 6:cb + 7],
                                              func=AF.Exp, scale=-0.5)
                            N.sRstd2.inc(ii)

                    @block.gpsimd
                    def _(g):
                        for i in range(NT):
                            g.wait_ge(N.sZ.sem, bZ + i + 1)
                            ii = g.dma_start(out=zout[seq, i * 128:(i + 1) * 128, :],
                                             in_=zst[:, (i % 2) * dm:(i % 2 + 1) * dm])
                            N.dG.inc(ii, 16)

                fence()

        @block.gpsimd
        def _(g):
            g.wait_ge(N.dG.sem, N.dG.n)

    return nc


_NC_CACHE = {}


def _get_nc():
    key = "main"
    if key not in _NC_CACHE:
        _NC_CACHE[key] = build()
    return _NC_CACHE[key]


def kernel(L0_attn_x, L0_attn_basis, L0_attn_q_coeffs, L0_attn_k_coeffs,
           L0_attn_v_coeffs, L0_attn_o_coeffs, L0_attn_decay_logit,
           L0_attn_out_scale, L0_ffn_g_info, L0_ffn_W_qkv, L0_ffn_W_proj,
           L0_ffn_b_proj):
    x = np.ascontiguousarray(np.asarray(L0_attn_x, dtype=np.float32))
    B, T, dm = x.shape
    ncore = 8
    nseq = B // ncore
    shards = x.reshape(ncore, nseq, T, dm)

    com = {
        "basis": np.ascontiguousarray(np.asarray(L0_attn_basis, np.float32)),
        "qc": np.ascontiguousarray(np.asarray(L0_attn_q_coeffs, np.float32)),
        "kc": np.ascontiguousarray(np.asarray(L0_attn_k_coeffs, np.float32)),
        "vc": np.ascontiguousarray(np.asarray(L0_attn_v_coeffs, np.float32)),
        "oc": np.ascontiguousarray(np.asarray(L0_attn_o_coeffs, np.float32)),
        "dlog": np.asarray(L0_attn_decay_logit, np.float32).reshape(1, 1),
        "oscale": np.asarray(L0_attn_out_scale, np.float32).reshape(1, 1),
        "gin": np.ascontiguousarray(
            np.asarray(L0_ffn_g_info, np.float32)[0, 0, :].reshape(16, 64)),
        "wqkv": np.ascontiguousarray(np.asarray(L0_ffn_W_qkv, np.float32)),
        "wproj": np.ascontiguousarray(np.asarray(L0_ffn_W_proj, np.float32)),
        "bproj": np.ascontiguousarray(
            np.asarray(L0_ffn_b_proj, np.float32).reshape(1, dm)),
    }
    in_maps = [dict(com, xin=np.ascontiguousarray(shards[c]))
               for c in range(ncore)]

    nc = _get_nc()
    res = run_bass_kernel_spmd(nc, in_maps, core_ids=list(range(ncore)))
    out = np.empty((B, T, dm), np.float32)
    for c in range(ncore):
        out[c * nseq:(c + 1) * nseq] = res.results[c]["zout"]
    return out


# version 20
# speedup vs baseline: 1.1456x; 1.1456x over previous
import numpy as np
import ml_dtypes
import concourse.bass as bass
import concourse.mybir as mybir
from concourse.bass_utils import run_bass_kernel_spmd
from contextlib import ExitStack

F32 = mybir.dt.float32
BF16 = mybir.dt.bfloat16
AF = mybir.ActivationFunctionType
ALU = mybir.AluOpType
AX = mybir.AxisListType

TAPS = False
BIGH = 16384.0
LNEPS = 1e-5
REG = 5
NFLIP = 8

SEM_NAMES = [
    "dSy", "dG", "dB",
    "fSy", "fG", "fT", "fS", "fV",
    "sCast", "sTrX", "sEvX",
    "sMM1", "sEv1",
    "sSc", "sSd", "sAvR", "sRet", "sEvR",
    "sAoMM", "sH1", "sVar", "sRstd", "sY", "sYb", "sTrY", "sEvY",
    "sWcast", "sQKMM", "sEvP",
    "sSq", "sN2MM", "sDotMM", "sEvB", "sRsq",
    "sCol", "sMid",
    "sQKb", "sLog", "sExp", "sAv", "sDen", "sBc", "sNorm",
    "sPjMM", "sH2", "sVar2", "sRstd2", "sZ",
]


class Sem:
    def __init__(self, nc, ctx, name):
        self.sem = ctx.enter_context(nc.semaphore(name))
        self.n = 0

    def inc(self, instr, k=1):
        instr.then_inc(self.sem, k)
        self.n += k


def build(nseq=2, T=2048, dm=1024, taps=TAPS):
    C = 256
    H, DH = 16, 64
    NT = T // 128
    NQ = T // 512
    DCH = dm // 128
    scale = DH ** -0.5
    TF = T + NFLIP

    nc = bass.Bass("TRN2", target_bir_lowering=False, detect_race_conditions=False)

    xin = nc.dram_tensor("xin", [nseq, T, dm], F32, kind="ExternalInput")
    basis = nc.dram_tensor("basis", [dm, 128], F32, kind="ExternalInput")
    coeffs = {nm: nc.dram_tensor(nm, [C, 128], F32, kind="ExternalInput")
              for nm in ["qc", "kc", "vc", "oc"]}
    dlog = nc.dram_tensor("dlog", [1, 1], F32, kind="ExternalInput")
    oscale = nc.dram_tensor("oscale", [1, 1], F32, kind="ExternalInput")
    gin = nc.dram_tensor("gin", [16, 64], F32, kind="ExternalInput")
    wqkv = nc.dram_tensor("wqkv", [dm, 3 * dm], F32, kind="ExternalInput")
    wproj = nc.dram_tensor("wproj", [dm, dm], F32, kind="ExternalInput")
    bproj = nc.dram_tensor("bproj", [1, dm], F32, kind="ExternalInput")
    zout = nc.dram_tensor("zout", [nseq, T, dm], F32, kind="ExternalOutput")

    sck = "ExternalOutput" if taps else "Internal"
    sc_y = nc.dram_tensor("sc_y", [nseq, T, dm], F32, kind=sck)
    sc_q = nc.dram_tensor("sc_q", [nseq, DCH, 128, T], BF16, kind=sck)
    sc_k = nc.dram_tensor("sc_k", [nseq, DCH, 128, T], BF16, kind=sck)
    sc_v = nc.dram_tensor("sc_v", [nseq, NT, 128, H * 65], BF16, kind=sck)
    sc_sim = nc.dram_tensor("sc_sim", [nseq, 1, T], F32, kind=sck)
    cc_in = nc.dram_tensor("cc_in", [1, 2], F32, kind="Internal")
    cc_out = nc.dram_tensor("cc_out", [1, 2], F32, kind="Internal")
    sc_k3 = nc.dram_tensor("sc_k3", [nseq, 3, T], BF16, kind="Internal")
    dbg_b = nc.dram_tensor("dbg_b", [8, 128, 512], BF16, kind=sck)
    dbg_f = nc.dram_tensor("dbg_f", [1, 64], F32, kind=sck)
    dbg_h = nc.dram_tensor("dbg_h", [128, 1024 + 16], F32, kind=sck)
    sc_q2 = nc.dram_tensor("sc_q2", [nseq, 2, TF], BF16, kind="Internal")
    sc_m = nc.dram_tensor("sc_m", [nseq, H, TF], BF16, kind="Internal")

    octx = ExitStack()
    with octx:
        _uid = [0]

        def sb(ctx, name, shape, dt):
            _uid[0] += 1
            return ctx.enter_context(nc.sbuf_tensor(f"{name}_{_uid[0]}", shape, dt))

        def pst(ctx, name, shape, dt=F32):
            _uid[0] += 1
            return ctx.enter_context(nc.psum_tensor(f"{name}_{_uid[0]}", shape, dt))

        ident = sb(octx, "ident", [128, 128], BF16)
        e0 = sb(octx, "e0", [128, 512], F32)
        dconst = sb(octx, "dconst", [128, NT * 512], BF16)
        bias_bc = sb(octx, "bias_bc", [128, dm], F32)
        rqw = sb(octx, "rqw", [128, DCH * C], BF16)
        rkw = sb(octx, "rkw", [128, DCH * C], BF16)
        rvw = sb(octx, "rvw", [128, DCH * C], BF16)
        rowT = sb(octx, "rowT", [128, 2 * dm], BF16)
        gbd = sb(octx, "gbd", [128, 128], BF16)
        obd = sb(octx, "obd", [128, 128], BF16)
        onesf = sb(octx, "onesf", [1, 128], F32)
        ones16f = sb(octx, "ones16f", [16, 1], F32)
        scal = sb(octx, "scal", [1, 32], F32)
        cols = sb(octx, "cols", [128, 4], F32)
        lncols = sb(octx, "lncols", [128, 16], F32)
        arenaF = sb(octx, "arenaF", [128, T], F32)
        arenaB = sb(octx, "arenaB", [128, TF], BF16)
        cbias = sb(octx, "cbias", [128, 2], F32)

        S = {name: Sem(nc, octx, name) for name in SEM_NAMES}

        class N:
            pass

        for k_, v_ in S.items():
            setattr(N, k_, v_)

        block = octx.enter_context(nc.Block())
        ENGS = ["sync", "gpsimd", "tensor", "scalar", "vector"]

        def fence():
            fence.round += 1
            r = fence.round
            fs = {"sync": N.fSy, "gpsimd": N.fG, "tensor": N.fT,
                  "scalar": N.fS, "vector": N.fV}

            def mk(which):
                def f(eng):
                    if which != "sync":
                        eng.drain()
                    eng.sem_inc(fs[which].sem, 1)
                    for w2, s2 in fs.items():
                        if w2 != which:
                            eng.wait_ge(s2.sem, r)
                return f
            for w in ENGS:
                getattr(block, w)(mk(w))
            for s2 in fs.values():
                s2.n = r
        fence.round = 0

        with ExitStack() as pctx:
            bss = sb(pctx, "bss", [128, 1024], F32)
            bsb = sb(pctx, "bsb", [128, 1024], BF16)
            bsT = sb(pctx, "bsT", [128, 1024], BF16)
            ccs = sb(pctx, "ccs", [128, 256], F32)
            ccb = sb(pctx, "ccb", [128, 256], BF16)
            ccT = sb(pctx, "ccT", [128, 256], BF16)
            bpr = sb(pctx, "bpr", [1, dm], F32)
            sct = sb(pctx, "sct", [1, 32], F32)
            gh = sb(pctx, "gh", [16, 64], F32)
            ghb = sb(pctx, "ghb", [16, 64], BF16)
            ghTb = sb(pctx, "ghTb", [64, 16], BF16)
            gtmp = sb(pctx, "gtmp", [16, 64], F32)
            gcol = sb(pctx, "gcol", [16, 4], F32)
            dta = sb(pctx, "dta", [128, 512], F32)
            dtb = sb(pctx, "dtbx", [128, 512], F32)
            dtm = sb(pctx, "dtm", [128, 512], F32)
            ps1 = pst(pctx, "ps1", [128, 512])
            ps2 = pst(pctx, "ps2", [128, 512])
            psT = pst(pctx, "psT", [128, 1024], BF16)
            psB = pst(pctx, "psB", [64, 16], BF16)
            psC = pst(pctx, "psC", [128, 16])

            @block.sync
            def _(sy):
                for c in range(DCH):
                    i = sy.dma_start(out=bss[:, c * 128:(c + 1) * 128],
                                     in_=basis[c * 128:(c + 1) * 128, :])
                    N.dSy.inc(i, 16)
                i = sy.dma_start(out=sct[0:1, 0:1], in_=dlog[:, :])
                N.dSy.inc(i, 16)
                i = sy.dma_start(out=sct[0:1, 8:9], in_=oscale[:, :])
                N.dSy.inc(i, 16)
                i = sy.dma_start(out=gh[:, :], in_=gin[:, :])
                N.dSy.inc(i, 16)
                i = sy.dma_start(out=bpr[:, :], in_=bproj[:, :])
                N.dSy.inc(i, 16)

            @block.gpsimd
            def _(g):
                i = g.iota(e0[:], pattern=[[-1, 512]], base=0, channel_multiplier=1,
                           allow_small_or_imprecise_dtypes=True)
                N.sMid.inc(i)

            @block.vector
            def _(v):
                v.wait_ge(N.sMid.sem, N.sMid.n)
                v.memset(onesf[:], 1.0)
                v.memset(cbias[:, 0:1], LNEPS)
                v.memset(scal[:], 0.0)
                v.memset(lncols[:], 0.0)
                v.memset(cols[:], 0.0)
                v.memset(cbias[:, 1:2], -BIGH)
                v.memset(ones16f[:], 1.0 / 16.0)
                v.memset(obd[:], 0.0)
                v.memset(gbd[:], 0.0)
                for h in range(H):
                    v.memset(obd[(h % 2) * 64:(h % 2) * 64 + 64,
                                 (h // 2) * 16 + h:(h // 2) * 16 + h + 1], 1.0)
                v.tensor_scalar(out=ident[:], in0=e0[:, 0:128], scalar1=0.0,
                                scalar2=None, op0=ALU.is_equal)
                v.wait_ge(N.dSy.sem, N.dSy.n)
                v.tensor_copy(out=bsb[:], in_=bss[:])
                v.tensor_tensor(out=gtmp[:], in0=gh[:], in1=gh[:], op=ALU.mult)
                i = v.tensor_reduce(out=gcol[:, 0:1], in_=gtmp[:], axis=AX.X,
                                    op=ALU.add)
                N.sVar.inc(i)

            @block.scalar
            def _(s):
                s.wait_ge(N.sVar.sem, N.sVar.n)
                s.activation(out=gcol[:, 1:2], in_=gcol[:, 0:1], func=AF.Ln)
                s.activation(out=gcol[:, 2:3], in_=gcol[:, 1:2], func=AF.Exp,
                             scale=-0.5)
                i = s.activation(out=sct[0:1, 2:3], in_=sct[0:1, 0:1], func=AF.Exp,
                                 scale=-1.0)
                N.sRstd.inc(i)

            @block.vector
            def _(v):
                v.wait_ge(N.sRstd.sem, N.sRstd.n)
                v.tensor_scalar(out=ghb[:], in0=gh[:], scalar1=gcol[:, 2:3],
                                scalar2=None, op0=ALU.mult)
                i = v.tensor_scalar(out=sct[0:1, 3:4], in0=sct[0:1, 2:3], scalar1=1.0,
                                    scalar2=None, op0=ALU.add)
                N.sY.inc(i)

            @block.scalar
            def _(s):
                s.wait_ge(N.sY.sem, N.sY.n)
                i = s.activation(out=sct[0:1, 16:17], in_=sct[0:1, 3:4], func=AF.Ln)
                N.sRstd.inc(i)

            @block.vector
            def _(v):
                v.wait_ge(N.sRstd.sem, N.sRstd.n)
                i = v.tensor_scalar(out=scal[0:1, 1:2], in0=sct[0:1, 16:17],
                                    scalar1=-1.0, scalar2=None, op0=ALU.mult)
                N.sY.inc(i)

            @block.scalar
            def _(s):
                s.wait_ge(N.sY.sem, N.sY.n)
                i = s.engine_nop() if False else s.activation(
                    out=sct[0:1, 17:18], in_=sct[0:1, 16:17], func=AF.Identity)
                N.sRstd.inc(i)

            fence()

            @block.tensor
            def _(t):
                for c in range(DCH):
                    i = t.transpose(psT[:, c * 128:(c + 1) * 128],
                                    bsb[:, c * 128:(c + 1) * 128], ident[:])
                i = t.transpose(psB[:], ghb[:], ident[0:16, 0:16])
                i = t.matmul(psC[:, 0:1], lhsT=onesf[:], rhs=scal[0:1, 1:2],
                             start=True, stop=True)
                i = t.matmul(psC[:, 1:2], lhsT=onesf[:], rhs=sct[0:1, 8:9],
                             start=True, stop=True)
                i = t.matmul(ps1[:], lhsT=onesf[:], rhs=bpr[0:1, 0:512],
                             start=True, stop=True)
                i = t.matmul(ps2[:], lhsT=onesf[:], rhs=bpr[0:1, 512:1024],
                             start=True, stop=True)
                N.sMM1.inc(i)

            @block.vector
            def _(v):
                v.wait_ge(N.sMM1.sem, N.sMM1.n)
                v.tensor_copy(out=bsT[:], in_=psT[:])
                v.tensor_copy(out=ghTb[:], in_=psB[:])
                v.drain()
                for h in range(H):
                    v.tensor_copy(out=gbd[(h % 2) * 64:(h % 2) * 64 + 64,
                                          (h // 2) * 16 + h:(h // 2) * 16 + h + 1],
                                  in_=ghTb[:, h:h + 1])
                v.tensor_copy(out=cols[:, 0:1], in_=psC[:, 0:1])
                v.tensor_copy(out=cols[:, 1:2], in_=psC[:, 1:2])
                v.tensor_copy(out=bias_bc[:, 0:512], in_=ps1[:])
                i = v.tensor_copy(out=bias_bc[:, 512:1024], in_=ps2[:])
                N.sEv1.inc(i)

            bV0, bA0 = N.sVar.n, N.sRstd.n

            @block.vector
            def _(v):
                for d in range(NT):
                    buf = dta if d % 2 == 0 else dtb
                    if d >= 2:
                        v.wait_ge(N.sRstd.sem, bA0 + d - 1)
                    i = v.tensor_scalar(out=buf[:], in0=e0[:], scalar1=128.0 * d - 1.0,
                                        scalar2=0.0, op0=ALU.add, op1=ALU.max)
                    N.sVar.inc(i)

            @block.scalar
            def _(s):
                s.wait_ge(N.sEv1.sem, N.sEv1.n)
                for d in range(NT):
                    buf = dta if d % 2 == 0 else dtb
                    s.wait_ge(N.sVar.sem, bV0 + d + 1)
                    i = s.activation(out=dconst[:, d * 512:(d + 1) * 512], in_=buf[:],
                                     func=AF.Exp, scale=cols[:, 0:1])
                    N.sRstd.inc(i)

            @block.vector
            def _(v):
                v.wait_ge(N.sRstd.sem, N.sRstd.n)
                i = None
                for d in range(min(4, NT)):
                    v.tensor_scalar(out=dtm[:], in0=e0[:], scalar1=128.0 * d - 1.0,
                                    scalar2=0.0, op0=ALU.add, op1=ALU.is_ge)
                    i = v.tensor_tensor(out=dconst[:, d * 512:(d + 1) * 512],
                                        in0=dconst[:, d * 512:(d + 1) * 512],
                                        in1=dtm[:], op=ALU.mult)
                N.sEv1.inc(i)

            for nm, dst in [("qc", rqw), ("kc", rkw), ("vc", rvw), ("oc", None)]:
                fence()

                @block.sync
                def _(sy, nm=nm):
                    for c in range(2):
                        i = sy.dma_start(out=ccs[:, c * 128:(c + 1) * 128],
                                         in_=coeffs[nm][c * 128:(c + 1) * 128, :])
                        N.dSy.inc(i, 16)

                @block.vector
                def _(v):
                    v.wait_ge(N.dSy.sem, N.dSy.n)
                    i = v.tensor_copy(out=ccb[:], in_=ccs[:])
                    N.sEv1.inc(i)

                @block.tensor
                def _(t):
                    t.wait_ge(N.sEv1.sem, N.sEv1.n)
                    for c in range(2):
                        i = t.transpose(psT[:, c * 128:(c + 1) * 128],
                                        ccb[:, c * 128:(c + 1) * 128], ident[:])
                    N.sMM1.inc(i)

                @block.vector
                def _(v):
                    v.wait_ge(N.sMM1.sem, N.sMM1.n)
                    i = v.tensor_copy(out=ccT[:], in_=psT[:, 0:256])
                    N.sEv1.inc(i)

                if dst is not None:
                    bM, bE = N.sMM1.n, N.sEv1.n

                    @block.tensor
                    def _(t):
                        t.wait_ge(N.sEv1.sem, bE)
                        for c in range(DCH):
                            if c >= 2:
                                t.wait_ge(N.sEv1.sem, bE + c - 1)
                            i = t.matmul((ps1 if c % 2 == 0 else ps2)[:, 0:C],
                                         lhsT=bsT[:, c * 128:(c + 1) * 128],
                                         rhs=ccT[:], start=True, stop=True)
                            N.sMM1.inc(i)

                    @block.vector
                    def _(v, dst=dst):
                        for c in range(DCH):
                            v.wait_ge(N.sMM1.sem, bM + c + 1)
                            i = v.tensor_copy(out=dst[:, c * C:(c + 1) * C],
                                              in_=(ps1 if c % 2 == 0 else ps2)[:, 0:C])
                            N.sEv1.inc(i)
                else:
                    bM, bE = N.sMM1.n, N.sEv1.n

                    @block.tensor
                    def _(t):
                        t.wait_ge(N.sEv1.sem, bE)
                        k = 0
                        for cc in range(2):
                            for nh in range(2):
                                if k >= 2:
                                    t.wait_ge(N.sEv1.sem, bE + k - 1)
                                i = t.matmul((ps1 if k % 2 == 0 else ps2)[:],
                                             lhsT=ccT[:, cc * 128:(cc + 1) * 128],
                                             rhs=bsT[:, nh * 512:(nh + 1) * 512],
                                             start=True, stop=True)
                                N.sMM1.inc(i)
                                k += 1

                    @block.vector
                    def _(v):
                        k = 0
                        for cc in range(2):
                            for nh in range(2):
                                v.wait_ge(N.sMM1.sem, bM + k + 1)
                                i = v.tensor_scalar(
                                    out=rowT[:, cc * dm + nh * 512:cc * dm + (nh + 1) * 512],
                                    in0=(ps1 if k % 2 == 0 else ps2)[:],
                                    scalar1=cols[:, 1:2], scalar2=None, op0=ALU.mult)
                                N.sEv1.inc(i)
                                k += 1

            fence()

        if taps:
            @block.gpsimd
            def _(g):
                ii = g.dma_start(out=dbg_b[0, :, :], in_=dconst[:, 0:512])
                N.dG.inc(ii, 16)
                ii = g.dma_start(out=dbg_b[1, :, :],
                                 in_=dconst[:, (NT - 1) * 512:NT * 512])
                N.dG.inc(ii, 16)
                ii = g.dma_start(out=dbg_b[2, :, :], in_=rqw[:, 0:512])
                N.dG.inc(ii, 16)
                ii = g.dma_start(out=dbg_f[0, 0:32], in_=scal[0:1, 0:32])
                N.dG.inc(ii, 16)
                ii = g.dma_start(out=dbg_f[0, 32:36], in_=cols[0:1, 0:4])
                N.dG.inc(ii, 16)
                g.wait_ge(N.dG.sem, N.dG.n)

        SDEP_A = 2
        with ExitStack() as actx:
            xyT = sb(actx, "xyT", [128, DCH * T], BF16)
            qtb = sb(actx, "qtb", [128, 2 * T], BF16)
            ktb = sb(actx, "ktb", [128, 2 * T], BF16)
            vtb = sb(actx, "vtb", [128, NT * C], BF16)
            rtb = sb(actx, "rtb", [128, 2 * T], BF16)
            sdb = sb(actx, "sdb", [128, SDEP_A * 512], BF16)

            for seq in range(nseq):
                with ExitStack() as s1:
                    xld = sb(s1, "xld", [128, 2 * dm], F32)
                    xcb = sb(s1, "xcb", [128, 2 * dm], BF16)
                    ps_tr = pst(s1, "ps_tr", [128, 2 * 1024], BF16)
                    bSy, bC, bT, bE = N.dSy.n, N.sCast.n, N.sTrX.n, N.sEvX.n

                    @block.sync
                    def _(sy):
                        for i in range(NT):
                            if i >= 2:
                                sy.wait_ge(N.sCast.sem, bC + i - 1)
                            ii = sy.dma_start(out=xld[:, (i % 2) * dm:(i % 2 + 1) * dm],
                                              in_=xin[seq, i * 128:(i + 1) * 128, :])
                            N.dSy.inc(ii, 16)

                    @block.vector
                    def _(v):
                        for i in range(NT):
                            v.wait_ge(N.dSy.sem, bSy + 16 * (i + 1))
                            if i >= 2:
                                v.wait_ge(N.sTrX.sem, bT + i - 1)
                            ii = v.tensor_copy(out=xcb[:, (i % 2) * dm:(i % 2 + 1) * dm],
                                               in_=xld[:, (i % 2) * dm:(i % 2 + 1) * dm])
                            N.sCast.inc(ii)
                            if i >= 1:
                                j = i - 1
                                v.wait_ge(N.sTrX.sem, bT + j + 1)
                                dst = xyT[:].rearrange("p (c n) -> p c n", c=DCH)[
                                    :, :, j * 128:(j + 1) * 128]
                                src = ps_tr[:, (j % 2) * 1024:(j % 2 + 1) * 1024
                                            ].rearrange("p (c n) -> p c n", c=DCH)
                                ii = v.tensor_copy(out=dst, in_=src)
                                N.sEvX.inc(ii)
                        j = NT - 1
                        v.wait_ge(N.sTrX.sem, bT + NT)
                        dst = xyT[:].rearrange("p (c n) -> p c n", c=DCH)[
                            :, :, j * 128:(j + 1) * 128]
                        src = ps_tr[:, (j % 2) * 1024:(j % 2 + 1) * 1024
                                    ].rearrange("p (c n) -> p c n", c=DCH)
                        ii = v.tensor_copy(out=dst, in_=src)
                        N.sEvX.inc(ii)

                    @block.tensor
                    def _(t):
                        for i in range(NT):
                            t.wait_ge(N.sCast.sem, bC + i + 1)
                            if i >= 2:
                                t.wait_ge(N.sEvX.sem, bE + i - 1)
                            for c in range(DCH):
                                ii = t.transpose(
                                    ps_tr[:, (i % 2) * 1024 + c * 128:
                                          (i % 2) * 1024 + (c + 1) * 128],
                                    xcb[:, (i % 2) * dm + c * 128:(i % 2) * dm + (c + 1) * 128],
                                    ident[:])
                            N.sTrX.inc(ii)

                fence()

                with ExitStack() as s1:
                    ps_q = [pst(s1, "ps_q", [128, 512]) for _ in range(2)]
                    ps_v = [pst(s1, "ps_v", [128, C]) for _ in range(2)]
                    bM, bE = N.sQKMM.n, N.sEvP.n
                    qk_tiles = [(w, cc, nt) for w in range(2) for cc in range(2)
                                for nt in range(NQ)]

                    @block.tensor
                    def _(t):
                        k = 0
                        for (w, cc, nt) in qk_tiles:
                            wsrc = rqw if w == 0 else rkw
                            if k >= 2:
                                t.wait_ge(N.sEvP.sem, bE + k - 1)
                            for c in range(DCH):
                                ii = t.matmul(
                                    ps_q[k % 2][:],
                                    lhsT=wsrc[:, c * C + cc * 128:c * C + (cc + 1) * 128],
                                    rhs=xyT[:, c * T + nt * 512:c * T + (nt + 1) * 512],
                                    start=(c == 0), stop=(c == DCH - 1))
                            N.sQKMM.inc(ii)
                            k += 1
                        for i in range(NT):
                            if k >= 2:
                                t.wait_ge(N.sEvP.sem, bE + k - 1)
                            for c in range(DCH):
                                ii = t.matmul(
                                    ps_v[k % 2][:],
                                    lhsT=xyT[:, c * T + i * 128:c * T + (i + 1) * 128],
                                    rhs=rvw[:, c * C:(c + 1) * C],
                                    start=(c == 0), stop=(c == DCH - 1))
                            N.sQKMM.inc(ii)
                            k += 1

                    @block.vector
                    def _(v):
                        k = 0
                        for (w, cc, nt) in qk_tiles:
                            dstt = qtb if w == 0 else ktb
                            v.wait_ge(N.sQKMM.sem, bM + k + 1)
                            ii = v.tensor_copy(
                                out=dstt[:, cc * T + nt * 512:cc * T + (nt + 1) * 512],
                                in_=ps_q[k % 2][:])
                            N.sEvP.inc(ii)
                            k += 1
                        for i in range(NT):
                            v.wait_ge(N.sQKMM.sem, bM + k + 1)
                            ii = v.tensor_copy(out=vtb[:, i * C:(i + 1) * C],
                                               in_=ps_v[k % 2][:])
                            N.sEvP.inc(ii)
                            k += 1

                fence()

                with ExitStack() as s1:
                    SDEP = SDEP_A
                    ps_sc = [pst(s1, "ps_sc", [128, 512]) for _ in range(SDEP)]
                    ps_rt = [pst(s1, "ps_rt", [128, 512]) for _ in range(2)]
                    bSc, bSd, bAvR, bRet, bEvR = (N.sSc.n, N.sSd.n, N.sAvR.n,
                                                  N.sRet.n, N.sEvR.n)
                    jlists = [list(range(4 * j, NT)) for j in range(NQ)]

                    @block.tensor
                    def _(t):
                        g = 0
                        for j in range(NQ):
                            il = jlists[j]
                            for ii_, i in enumerate(il):
                                if g >= SDEP:
                                    t.wait_ge(N.sSd.sem, bSd + g - SDEP + 1)
                                for cc in range(2):
                                    mm = t.matmul(
                                        ps_sc[g % SDEP][:],
                                        lhsT=ktb[:, cc * T + i * 128:cc * T + (i + 1) * 128],
                                        rhs=qtb[:, cc * T + j * 512:cc * T + (j + 1) * 512],
                                        start=(cc == 0), stop=(cc == 1))
                                N.sSc.inc(mm)
                                if ii_ >= 1:
                                    if ii_ == 1 and j >= 1:
                                        t.wait_ge(N.sEvR.sem, bEvR + 2 * j)
                                    t.wait_ge(N.sSd.sem, bSd + g)
                                    pi = il[ii_ - 1]
                                    for cc in range(2):
                                        mm = t.matmul(
                                            ps_rt[cc][:],
                                            lhsT=vtb[:, pi * C + cc * 128:pi * C + (cc + 1) * 128],
                                            rhs=sdb[:, ((g - 1) % SDEP) * 512:((g - 1) % SDEP + 1) * 512],
                                            start=(ii_ == 1), stop=False)
                                    N.sAvR.inc(mm)
                                g += 1
                            t.wait_ge(N.sSd.sem, bSd + g)
                            pi = il[-1]
                            if len(il) == 1 and j >= 1:
                                t.wait_ge(N.sEvR.sem, bEvR + 2 * j)
                            for cc in range(2):
                                mm = t.matmul(
                                    ps_rt[cc][:],
                                    lhsT=vtb[:, pi * C + cc * 128:pi * C + (cc + 1) * 128],
                                    rhs=sdb[:, ((g - 1) % SDEP) * 512:((g - 1) % SDEP + 1) * 512],
                                    start=(len(il) == 1), stop=True)
                            N.sAvR.inc(mm)

                    @block.vector
                    def _(v):
                        g = 0
                        for j in range(NQ):
                            il = jlists[j]
                            for ii_, i in enumerate(il):
                                d = i - 4 * j
                                v.wait_ge(N.sSc.sem, bSc + g + 1)
                                if g >= SDEP:
                                    v.wait_ge(N.sAvR.sem, bAvR + g - SDEP + 1)
                                ii = v.tensor_tensor(
                                    out=sdb[:, (g % SDEP) * 512:(g % SDEP + 1) * 512],
                                    in0=ps_sc[g % SDEP][:],
                                    in1=dconst[:, d * 512:(d + 1) * 512], op=ALU.mult)
                                N.sSd.inc(ii)
                                g += 1
                            cum = sum(len(jlists[jj]) for jj in range(j + 1))
                            v.wait_ge(N.sAvR.sem, bAvR + cum)
                            for cc in range(2):
                                ii = v.tensor_copy(
                                    out=rtb[:, cc * T + j * 512:cc * T + (j + 1) * 512],
                                    in_=ps_rt[cc][:])
                                N.sEvR.inc(ii)

                fence()

                if taps and seq == 0:
                    @block.gpsimd
                    def _(g):
                        for sl, (src_, c0) in enumerate([(qtb, 0), (ktb, 0),
                                                         (vtb, 0), (rtb, 0)]):
                            ii = g.dma_start(out=dbg_b[3 + sl, :, :],
                                             in_=src_[:, c0:c0 + 512])
                            N.dG.inc(ii, 16)
                        g.wait_ge(N.dG.sem, N.dG.n)
                    fence()

                with ExitStack() as s1:
                    xrs = sb(s1, "xrs", [128, 2 * dm], F32)
                    dbgh = sb(s1, "dbgh", [128, dm + 16], F32) if (taps and seq == 0) else None
                    h1b = sb(s1, "h1b", [128, 2 * dm], F32)
                    sqb = sb(s1, "sqb", [128, dm], F32)
                    yst = sb(s1, "yst", [128, 2 * dm], F32)
                    ybb = sb(s1, "ybb", [128, 2 * dm], BF16)
                    ps_ao = [pst(s1, "ps_ao", [128, 1024]) for _ in range(2)]
                    ps_yt = pst(s1, "ps_yt", [128, 2 * 1024], BF16)
                    bSy, bG = N.dSy.n, N.dG.n
                    bAo, bH1, bVar, bRstd = N.sAoMM.n, N.sH1.n, N.sVar.n, N.sRstd.n
                    bY, bYb, bTrY, bEvY = N.sY.n, N.sYb.n, N.sTrY.n, N.sEvY.n
                    bSqA = N.sSq.n

                    @block.sync
                    def _(sy):
                        for i in range(NT):
                            if i >= 2:
                                sy.wait_ge(N.sH1.sem, bH1 + i - 1)
                            ii = sy.dma_start(out=xrs[:, (i % 2) * dm:(i % 2 + 1) * dm],
                                              in_=xin[seq, i * 128:(i + 1) * 128, :])
                            N.dSy.inc(ii, 16)

                    @block.tensor
                    def _(t):
                        for i in range(NT):
                            if i >= 2:
                                t.wait_ge(N.sH1.sem, bH1 + i - 1)
                            for nh in range(2):
                                for cc in range(2):
                                    mm = t.matmul(
                                        ps_ao[i % 2][:, nh * 512:(nh + 1) * 512],
                                        lhsT=rtb[:, cc * T + i * 128:cc * T + (i + 1) * 128],
                                        rhs=rowT[:, cc * dm + nh * 512:cc * dm + (nh + 1) * 512],
                                        start=(cc == 0), stop=(cc == 1))
                            N.sAoMM.inc(mm)
                            if i >= 1:
                                j = i - 1
                                t.wait_ge(N.sYb.sem, bYb + j + 1)
                                if j >= 2:
                                    t.wait_ge(N.sEvY.sem, bEvY + j - 1)
                                for c in range(DCH):
                                    mm = t.transpose(
                                        ps_yt[:, (j % 2) * 1024 + c * 128:
                                              (j % 2) * 1024 + (c + 1) * 128],
                                        ybb[:, (j % 2) * dm + c * 128:(j % 2) * dm + (c + 1) * 128],
                                        ident[:])
                                N.sTrY.inc(mm)
                        j = NT - 1
                        t.wait_ge(N.sYb.sem, bYb + NT)
                        if j >= 2:
                            t.wait_ge(N.sEvY.sem, bEvY + j - 1)
                        for c in range(DCH):
                            mm = t.transpose(
                                ps_yt[:, (j % 2) * 1024 + c * 128:(j % 2) * 1024 + (c + 1) * 128],
                                ybb[:, (j % 2) * dm + c * 128:(j % 2) * dm + (c + 1) * 128],
                                ident[:])
                        N.sTrY.inc(mm)

                    @block.vector
                    def _(v):
                        def emit_y(j):
                            pj = j % 2
                            cb = pj * 8
                            v.wait_ge(N.sRstd.sem, bRstd + j + 1)
                            if j >= 2:
                                v.wait_ge(N.dG.sem, bG + 16 * (j - 1))
                            ii = v.tensor_scalar(
                                out=yst[:, pj * dm:(pj + 1) * dm],
                                in0=h1b[:, pj * dm:(pj + 1) * dm],
                                scalar1=lncols[:, cb + 0:cb + 1],
                                scalar2=lncols[:, cb + 5:cb + 6],
                                op0=ALU.subtract, op1=ALU.mult)
                            N.sY.inc(ii)
                            ii = v.tensor_copy(out=ybb[:, pj * dm:(pj + 1) * dm],
                                               in_=yst[:, pj * dm:(pj + 1) * dm])
                            N.sYb.inc(ii)
                            if j >= 1:
                                jj = j - 1
                                v.wait_ge(N.sTrY.sem, bTrY + jj + 1)
                                dst = xyT[:].rearrange("p (c n) -> p c n", c=DCH)[
                                    :, :, jj * 128:(jj + 1) * 128]
                                src = ps_yt[:, (jj % 2) * 1024:(jj % 2 + 1) * 1024
                                            ].rearrange("p (c n) -> p c n", c=DCH)
                                ii = v.tensor_copy(out=dst, in_=src)
                                N.sEvY.inc(ii)

                        for i in range(NT):
                            pc = i % 2
                            cb = pc * 8
                            v.wait_ge(N.sAoMM.sem, bAo + i + 1)
                            v.wait_ge(N.dSy.sem, bSy + 16 * (i + 1))
                            ii = v.scalar_tensor_tensor(
                                out=h1b[:, pc * dm:(pc + 1) * dm], in0=ps_ao[pc][:],
                                scalar=1.0, in1=xrs[:, pc * dm:(pc + 1) * dm],
                                op0=ALU.mult, op1=ALU.add,
                                accum_out=lncols[:, cb + 3:cb + 4])
                            N.sH1.inc(ii)
                            v.drain()
                            v.tensor_scalar(out=lncols[:, cb + 0:cb + 1],
                                            in0=lncols[:, cb + 3:cb + 4],
                                            scalar1=1.0 / dm, scalar2=None, op0=ALU.mult)
                            v.drain()
                            v.tensor_tensor(out=lncols[:, cb + 1:cb + 2],
                                            in0=lncols[:, cb + 0:cb + 1],
                                            in1=lncols[:, cb + 0:cb + 1], op=ALU.mult)
                            v.wait_ge(N.sSq.sem, bSqA + i + 1)
                            v.tensor_scalar(out=lncols[:, cb + 2:cb + 3],
                                            in0=lncols[:, cb + 4:cb + 5],
                                            scalar1=1.0 / dm, scalar2=None, op0=ALU.mult)
                            v.drain()
                            ii = v.tensor_tensor(out=lncols[:, cb + 2:cb + 3],
                                                 in0=lncols[:, cb + 2:cb + 3],
                                                 in1=lncols[:, cb + 1:cb + 2],
                                                 op=ALU.subtract)
                            N.sVar.inc(ii)
                            if dbgh is not None and i == 0:
                                v.tensor_copy(out=dbgh[:, 0:dm], in_=h1b[:, 0:dm])
                                ii = v.tensor_copy(out=dbgh[:, dm:dm + 16], in_=lncols[:])
                                N.sEvB.inc(ii)
                            if i >= 1:
                                emit_y(i - 1)
                        emit_y(NT - 1)
                        j = NT - 1
                        v.wait_ge(N.sTrY.sem, bTrY + NT)
                        dst = xyT[:].rearrange("p (c n) -> p c n", c=DCH)[
                            :, :, j * 128:(j + 1) * 128]
                        src = ps_yt[:, (j % 2) * 1024:(j % 2 + 1) * 1024
                                    ].rearrange("p (c n) -> p c n", c=DCH)
                        ii = v.tensor_copy(out=dst, in_=src)
                        N.sEvY.inc(ii)

                    @block.scalar
                    def _(s):
                        for i in range(NT):
                            cb = (i % 2) * 8
                            s.wait_ge(N.sH1.sem, bH1 + i + 1)
                            ii = s.activation(out=sqb[:],
                                              in_=h1b[:, (i % 2) * dm:(i % 2 + 1) * dm],
                                              func=AF.Square,
                                              accum_out=lncols[:, cb + 4:cb + 5])
                            N.sSq.inc(ii)
                            s.wait_ge(N.sVar.sem, bVar + i + 1)
                            s.activation(out=lncols[:, cb + 6:cb + 7],
                                         in_=lncols[:, cb + 2:cb + 3],
                                         func=AF.Ln, bias=cbias[:, 0:1])
                            ii = s.activation(out=lncols[:, cb + 5:cb + 6],
                                              in_=lncols[:, cb + 6:cb + 7],
                                              func=AF.Exp, scale=-0.5)
                            N.sRstd.inc(ii)

                    @block.gpsimd
                    def _(g):
                        for i in range(NT):
                            g.wait_ge(N.sY.sem, bY + i + 1)
                            ii = g.dma_start(out=sc_y[seq, i * 128:(i + 1) * 128, :],
                                             in_=yst[:, (i % 2) * dm:(i % 2 + 1) * dm])
                            N.dG.inc(ii, 16)
                        if dbgh is not None:
                            g.wait_ge(N.sEvB.sem, N.sEvB.n)
                            ii = g.dma_start(out=dbg_h[:, :], in_=dbgh[:])
                            N.dG.inc(ii, 16)
                            g.wait_ge(N.dG.sem, N.dG.n)

                fence()

                with ExitStack() as s2:
                    prod = sb(s2, "prod", [128, NT * 1040], BF16)
                    wst = sb(s2, "wst", [128, 2 * 1024], F32)
                    wbb = sb(s2, "wbb", [128, DCH * 1040], BF16)
                    ps_pA = pst(s2, "ps_pA", [128, 512])
                    ps_pB = pst(s2, "ps_pB", [128, 512])
                    ps_p16 = pst(s2, "ps_p16", [128, 16])

                    def emit_part(part, seq=seq, prod=prod, wst=wst, wbb=wbb,
                                  stage=0):
                        if True:
                            if part < 2:
                                ps_p = [ps_pA, ps_pB]
                            else:
                                ps_p = [ps_pA, ps_pB, ps_p16]
                            if stage != 2:
                                emit_part.bases = (N.dSy.n, N.sWcast.n, N.sQKMM.n,
                                                   N.sEvP.n, N.dG.n, N.sDotMM.n)
                            bSy, bW, bM, bE, bG0, bDot0 = emit_part.bases

                            if stage == 2:
                                pass
                            else:
                              @block.sync
                              def _(sy):
                                for c in range(DCH):
                                    if c >= 2:
                                        sy.wait_ge(N.sWcast.sem, bW + c - 1)
                                    elif part > 0:
                                        sy.wait_ge(N.sWcast.sem, bW)
                                    ii = sy.dma_start(
                                        out=wst[:, (c % 2) * 1024:(c % 2 + 1) * 1024],
                                        in_=wqkv[c * 128:(c + 1) * 128,
                                                 part * dm:(part + 1) * dm])
                                    N.dSy.inc(ii, 16)

                            if stage == 2:
                                pass
                            else:
                              @block.vector
                              def _(v):
                                v.wait_ge(N.sQKMM.sem, bM)
                                for c in range(DCH):
                                    v.wait_ge(N.dSy.sem, bSy + 16 * (c + 1))
                                    if part < 2:
                                        v.memset(wbb[:, c * 1040 + 1024:(c + 1) * 1040], 0.0)
                                        ii = v.tensor_copy(
                                            out=wbb[:, c * 1040:c * 1040 + 1024],
                                            in_=wst[:, (c % 2) * 1024:(c % 2 + 1) * 1024])
                                    else:
                                        vv = wbb[:, c * 1040:(c + 1) * 1040].rearrange(
                                            "p (h j) -> p h j", j=65)[:, :, 64:65]
                                        v.memset(vv, 0.0)
                                        for h in range(H):
                                            ii = v.tensor_copy(
                                                out=wbb[:, c * 1040 + h * 65:
                                                        c * 1040 + h * 65 + 64],
                                                in_=wst[:, (c % 2) * 1024 + h * 64:
                                                        (c % 2) * 1024 + (h + 1) * 64])
                                    N.sWcast.inc(ii)

                            if stage == 1:
                                return
                            @block.tensor
                            def _(t):
                                t.wait_ge(N.sWcast.sem, bW + DCH)
                                t.wait_ge(N.sEvP.sem, bE)
                                if part < 2:
                                    k = 0
                                    for mc in range(DCH):
                                        for nt in range(NQ):
                                            if k >= 2:
                                                t.wait_ge(N.sEvP.sem, bE + k - 1)
                                            for kc in range(DCH):
                                                mm = t.matmul(
                                                    ps_p[k % 2][:],
                                                    lhsT=wbb[:, kc * 1040 + mc * 128:
                                                             kc * 1040 + (mc + 1) * 128],
                                                    rhs=xyT[:, kc * T + nt * 512:
                                                            kc * T + (nt + 1) * 512],
                                                    start=(kc == 0), stop=(kc == DCH - 1))
                                            N.sQKMM.inc(mm)
                                            k += 1
                                else:
                                    for i in range(NT):
                                        if i >= 1:
                                            t.wait_ge(N.sEvP.sem, bE + i)
                                        for kc in range(DCH):
                                            for sl, (n0, n1) in enumerate(
                                                    [(0, 512), (512, 1024), (1024, 1040)]):
                                                mm = t.matmul(
                                                    ps_p[sl][:, 0:n1 - n0],
                                                    lhsT=xyT[:, kc * T + i * 128:
                                                             kc * T + (i + 1) * 128],
                                                    rhs=wbb[:, kc * 1040 + n0:kc * 1040 + n1],
                                                    start=(kc == 0), stop=(kc == DCH - 1))
                                        N.sQKMM.inc(mm)

                            @block.vector
                            def _(v):
                                if part >= 1:
                                    v.wait_ge(N.dG.sem, bG0)
                                    v.wait_ge(N.sDotMM.sem, bDot0)
                                if part < 2:
                                    k = 0
                                    for mc in range(DCH):
                                        for nt in range(NQ):
                                            v.wait_ge(N.sQKMM.sem, bM + k + 1)
                                            if part == 0:
                                                ii = v.tensor_scalar(
                                                    out=prod[:, mc * T + nt * 512:
                                                             mc * T + (nt + 1) * 512],
                                                    in0=ps_p[k % 2][:],
                                                    scalar1=scale, scalar2=None,
                                                    op0=ALU.mult)
                                            else:
                                                ii = v.tensor_copy(
                                                    out=prod[:, mc * T + nt * 512:
                                                             mc * T + (nt + 1) * 512],
                                                    in_=ps_p[k % 2][:])
                                            N.sEvP.inc(ii)
                                            k += 1
                                else:
                                    for i in range(NT):
                                        v.wait_ge(N.sQKMM.sem, bM + i + 1)
                                        for sl, (n0, n1) in enumerate(
                                                [(0, 512), (512, 1024), (1024, 1040)]):
                                            v.tensor_copy(
                                                out=prod[:, i * 1040 + n0:i * 1040 + n1],
                                                in_=ps_p[sl][:, 0:n1 - n0])
                                        vv = prod[:, i * 1040:(i + 1) * 1040].rearrange(
                                            "p (h j) -> p h j", j=65)[:, :, 64:65]
                                        ii = v.memset(vv, 1.0)
                                        N.sEvP.inc(ii)

                            @block.gpsimd
                            def _(g):
                                if part < 2:
                                    dstd = sc_q if part == 0 else sc_k
                                    for mc in range(DCH):
                                        g.wait_ge(N.sEvP.sem, bE + (mc + 1) * NQ)
                                        ii = g.dma_start(out=dstd[seq, mc, :, :],
                                                         in_=prod[:, mc * T:(mc + 1) * T])
                                        N.dG.inc(ii, 16)
                                else:
                                    for i in range(NT):
                                        g.wait_ge(N.sEvP.sem, bE + i + 1)
                                        ii = g.dma_start(out=sc_v[seq, i, :, :],
                                                         in_=prod[:, i * 1040:(i + 1) * 1040])
                                        N.dG.inc(ii, 16)
                                g.wait_ge(N.dG.sem, N.dG.n)

                    emit_part(0)
                    emit_part(1, stage=1)

                    with ExitStack() as s3:
                        sq2 = sb(s3, "sq2", [128, 2 * T], BF16)
                        dots = sb(s3, "dots", [16, T], F32)
                        n2q = sb(s3, "n2q", [16, T], F32)
                        rsq_t = sb(s3, "rsq", [16, T], F32)
                        rsq = rsq_t[:, :]
                        ps_sim = [pst(s3, "ps_sim", [16, 512])
                                  for _ in range(T // 512)]
                        ps_mean = pst(s3, "ps_mean", [1, 512])
                        bSq, bN2, bDot, bEvB, bRsq = (N.sSq.n, N.sN2MM.n, N.sDotMM.n,
                                                      N.sEvB.n, N.sRsq.n)
                        bEvP0 = N.sEvP.n

                        @block.vector
                        def _(v):
                            for mc in range(DCH):
                                v.wait_ge(N.sEvP.sem, bEvP0 - DCH * NQ + (mc + 1) * NQ)
                                if mc >= 2:
                                    v.wait_ge(N.sN2MM.sem, bN2 + mc - 1)
                                ii = v.tensor_tensor(
                                    out=sq2[:, (mc % 2) * T:(mc % 2 + 1) * T],
                                    in0=prod[:, mc * T:(mc + 1) * T],
                                    in1=prod[:, mc * T:(mc + 1) * T], op=ALU.mult)
                                N.sSq.inc(ii)

                        @block.tensor
                        def _(t):
                            for mc in range(DCH):
                                t.wait_ge(N.sSq.sem, bSq + mc + 1)
                                for nsl in range(T // 512):
                                    mm = t.matmul(
                                        ps_sim[nsl][:],
                                        lhsT=obd[:, mc * 16:(mc + 1) * 16],
                                        rhs=sq2[:, (mc % 2) * T + nsl * 512:
                                                (mc % 2) * T + (nsl + 1) * 512],
                                        start=(mc == 0), stop=(mc == DCH - 1))
                                N.sN2MM.inc(mm)

                        @block.vector
                        def _(v):
                            v.wait_ge(N.sN2MM.sem, bN2 + DCH)
                            for nsl in range(T // 512):
                                ii = v.tensor_copy(
                                    out=n2q[:, nsl * 512:(nsl + 1) * 512],
                                    in_=ps_sim[nsl][:])
                            N.sEvB.inc(ii)

                        @block.tensor
                        def _(t):
                            t.wait_ge(N.sEvB.sem, bEvB + 1)
                            for mc in range(DCH):
                                for nsl in range(T // 512):
                                    mm = t.matmul(
                                        ps_sim[nsl][:],
                                        lhsT=gbd[:, mc * 16:(mc + 1) * 16],
                                        rhs=prod[:, mc * T + nsl * 512:mc * T + (nsl + 1) * 512],
                                        start=(mc == 0), stop=(mc == DCH - 1))
                            N.sDotMM.inc(mm)

                        @block.scalar
                        def _(s):
                            s.wait_ge(N.sEvB.sem, bEvB + 1)
                            s.activation(out=rsq, in_=n2q[:], func=AF.Ln)
                            ii = s.activation(out=rsq, in_=rsq, func=AF.Exp,
                                              scale=-0.5)
                            N.sRsq.inc(ii)

                        @block.vector
                        def _(v):
                            v.wait_ge(N.sDotMM.sem, bDot + 1)
                            v.wait_ge(N.sRsq.sem, bRsq + 1)
                            for nsl in range(T // 512):
                                v.tensor_copy(out=dots[:, nsl * 512:(nsl + 1) * 512],
                                              in_=ps_sim[nsl][:])
                            ii = v.tensor_tensor(out=dots[:], in0=dots[:], in1=rsq,
                                                 op=ALU.mult)
                            N.sEvB.inc(ii)

                        @block.tensor
                        def _(t):
                            t.wait_ge(N.sEvB.sem, bEvB + 2)
                            for nsl in range(T // 512):
                                if nsl >= 1:
                                    t.wait_ge(N.sEvB.sem, bEvB + 2 + nsl)
                                mm = t.matmul(ps_mean[:], lhsT=ones16f[:],
                                              rhs=dots[:, nsl * 512:(nsl + 1) * 512],
                                              start=True, stop=True)
                                N.sDotMM.inc(mm)

                        @block.vector
                        def _(v):
                            b0 = 0 if seq == 0 else 32
                            simrow = arenaF[b0:b0 + 1, :]
                            for nsl in range(T // 512):
                                v.wait_ge(N.sDotMM.sem, bDot + 2 + nsl)
                                ii = v.tensor_copy(
                                    out=simrow[:, nsl * 512:(nsl + 1) * 512],
                                    in_=ps_mean[:])
                                N.sEvB.inc(ii)
                            v.tensor_reduce(out=scal[0:1, 4 + 2 * seq:5 + 2 * seq],
                                            in_=simrow[:, REG:T], axis=AX.X, op=ALU.max)
                            ii = v.tensor_reduce(out=scal[0:1, 5 + 2 * seq:6 + 2 * seq],
                                                 in_=simrow[:, REG:T], axis=AX.X,
                                                 op=ALU.min)
                            N.sEvB.inc(ii)

                        if taps:
                            @block.gpsimd
                            def _(g):
                                g.wait_ge(N.sEvB.sem, N.sEvB.n)
                                b0 = 0 if seq == 0 else 32
                                ii = g.dma_start(out=sc_sim[seq, :, :],
                                                 in_=arenaF[b0:b0 + 1, :])
                                N.dG.inc(ii, 16)

                    emit_part(1, stage=2)
                    emit_part(2)
                    fence()

        def emit_mid():
            bGmid = N.dG.n
            with ExitStack() as mctx:
                tmpb = sb(mctx, "tmpb", [1, 16], F32)

                @block.vector
                def _(v):
                    v.drain()
                    if nseq == 2:
                        v.tensor_tensor(out=tmpb[0:1, 1:2], in0=scal[0:1, 4:5],
                                        in1=scal[0:1, 6:7], op=ALU.max)
                        v.tensor_tensor(out=tmpb[0:1, 2:3], in0=scal[0:1, 5:6],
                                        in1=scal[0:1, 7:8], op=ALU.min)
                    else:
                        v.tensor_copy(out=tmpb[0:1, 1:2], in_=scal[0:1, 4:5])
                        v.tensor_copy(out=tmpb[0:1, 2:3], in_=scal[0:1, 5:6])
                    v.drain()
                    ii = v.tensor_scalar(out=tmpb[0:1, 0:1], in0=tmpb[0:1, 2:3],
                                         scalar1=-1.0, scalar2=None, op0=ALU.mult)
                    N.sMid.inc(ii)

                @block.gpsimd
                def _(g):
                    g.wait_ge(N.sMid.sem, N.sMid.n)
                    ii = g.dma_start(out=cc_in[:, :], in_=tmpb[0:1, 0:2])
                    N.dG.inc(ii, 16)
                    g.wait_ge(N.dG.sem, N.dG.n)
                    ii = g.collective_compute(
                        "AllReduce", ALU.max, replica_groups=[list(range(8))],
                        ins=[cc_in.ap()], outs=[cc_out.ap()])
                    N.sCol.inc(ii)
                    g.wait_ge(N.sCol.sem, N.sCol.n)
                    ii = g.dma_start(out=tmpb[0:1, 4:6], in_=cc_out[:, :])
                    N.dG.inc(ii, 16)

                @block.vector
                def _(v):
                    v.wait_ge(N.dG.sem, N.dG.n)
                    v.tensor_scalar(out=tmpb[0:1, 6:7], in0=tmpb[0:1, 4:5],
                                    scalar1=-0.1, scalar2=None, op0=ALU.mult)
                    v.tensor_scalar(out=tmpb[0:1, 7:8], in0=tmpb[0:1, 5:6],
                                    scalar1=0.9, scalar2=None, op0=ALU.mult)
                    v.drain()
                    v.tensor_tensor(out=scal[0:1, 2:3], in0=tmpb[0:1, 6:7],
                                    in1=tmpb[0:1, 7:8], op=ALU.add)
                    v.drain()
                    for seq in range(nseq):
                        if seq >= 1:
                            v.wait_ge(N.dG.sem, bGmid + 32 + 48 * seq)
                        b0 = 0 if seq == 0 else 32
                        simrow = arenaF[b0:b0 + 1, :]
                        v.tensor_scalar(out=simrow[:, :], in0=simrow[:, :],
                                        scalar1=scal[0:1, 2:3], scalar2=0.0,
                                        op0=ALU.subtract, op1=ALU.is_gt)
                        v.tensor_scalar(out=simrow[:, :], in0=simrow[:, :],
                                        scalar1=2.0, scalar2=-1.0, op0=ALU.mult,
                                        op1=ALU.add)
                        v.memset(simrow[:, 0:REG], 1.0)
                        v.drain()
                        v.tensor_copy(out=arenaB[0:1, 0:T], in_=simrow[:, :])
                        v.tensor_scalar(out=arenaB[0:1, T:TF], in0=simrow[:, 0:NFLIP],
                                        scalar1=-1.0, scalar2=None, op0=ALU.mult)
                        v.tensor_scalar(out=arenaB[32:33, 0:T], in0=simrow[:, :],
                                        scalar1=BIGH, scalar2=None, op0=ALU.mult)
                        v.memset(arenaB[32:33, 0:REG], 0.0)
                        v.memset(arenaB[64:65, 0:T], 0.0)
                        ii = v.memset(arenaB[64:65, 0:REG], BIGH)
                        N.sMid.inc(ii)

                bMid = N.sMid.n - nseq

                @block.gpsimd
                def _(g):
                    for seq in range(nseq):
                        g.wait_ge(N.sMid.sem, bMid + seq + 1)
                        ii = g.dma_start(out=sc_k3[seq, 0:1, :], in_=arenaB[32:33, 0:T])
                        N.dG.inc(ii, 16)
                        ii = g.dma_start(out=sc_k3[seq, 2:3, :], in_=arenaB[64:65, 0:T])
                        N.dG.inc(ii, 16)
                        ii = g.dma_start(out=sc_q2[seq, 0:1, :], in_=arenaB[0:1, 0:TF])
                        N.dG.inc(ii, 16)

                @block.vector
                def _(v):
                    v.wait_ge(N.dG.sem, N.dG.n)
                    ii = v.memset(arenaB[0:1, 0:TF], 1.0)
                    N.sMid.inc(ii)

                @block.gpsimd
                def _(g):
                    g.wait_ge(N.sMid.sem, N.sMid.n)
                    for seq in range(nseq):
                        ii = g.dma_start(out=sc_k3[seq, 1:2, :], in_=arenaB[0:1, 0:T])
                        N.dG.inc(ii, 16)
                        ii = g.dma_start(out=sc_q2[seq, 1:2, :], in_=arenaB[0:1, 0:TF])
                        N.dG.inc(ii, 16)


        bGA = N.dG.n
        emit_mid()
        fence()

        with ExitStack() as bctx:
            vsb = sb(bctx, "vsb", [128, NT * 1040], BF16)
            oT = sb(bctx, "oT", [128, DCH * T], BF16)
            qch = sb(bctx, "qch", [128, 2 * T], BF16)
            kch = sb(bctx, "kch", [128, 2 * T], BF16)
            qpb = sb(bctx, "qpb", [67, 2 * TF], BF16)
            kpb = sb(bctx, "kpb", [67, 2 * T], BF16)
            ptb = sb(bctx, "ptb", [128, 2 * 1024], BF16)
            osb = sb(bctx, "osb", [64, 512], BF16)
            wpb = sb(bctx, "wpb", [128, DCH * 1024], BF16)
            flb = sb(bctx, "flb", [128, DCH * NFLIP], BF16)
            fltmp = sb(bctx, "fltmp", [64, REG], F32)
            denr = sb(bctx, "denr", [1, 1024], F32)
            recr = None

            with ExitStack() as s0:
                wpl = sb(s0, "wpl", [128, 2 * 1024], F32)
                bSy, bW = N.dSy.n, N.sWcast.n

                @block.sync
                def _(sy):
                    for c in range(DCH):
                        if c >= 2:
                            sy.wait_ge(N.sWcast.sem, bW + c - 1)
                        ii = sy.dma_start(out=wpl[:, (c % 2) * 1024:(c % 2 + 1) * 1024],
                                          in_=wproj[c * 128:(c + 1) * 128, :])
                        N.dSy.inc(ii, 16)

                @block.vector
                def _(v):
                    for c in range(DCH):
                        v.wait_ge(N.dSy.sem, bSy + 16 * (c + 1))
                        ii = v.tensor_copy(out=wpb[:, c * 1024:(c + 1) * 1024],
                                           in_=wpl[:, (c % 2) * 1024:(c % 2 + 1) * 1024])
                        N.sWcast.inc(ii)

            fence()

            for seq in range(nseq):
                NQT = NQ + 1
                NG = NT // 2
                QW = [512] * NQ + [NFLIP]
                QOF = [512 * i for i in range(NQ)] + [T]

                with ExitStack() as s1:
                    sqh = sb(s1, "sqh", [128, 2 * T], BF16)
                    kmx = sb(s1, "kmx", [16, 12], F32)
                    mall = sb(s1, "mall", [16, T], F32)
                    mneg = sb(s1, "mneg", [16, TF], BF16)
                    ps_n1 = [pst(s1, "ps_n1", [16, 512]) for _ in range(T // 512)]
                    ps_n2 = [pst(s1, "ps_n2", [16, 512]) for _ in range(T // 512)]
                    bSy, bSq, bN2, bEvB, bRsq, bB = (N.dSy.n, N.sSq.n, N.sN2MM.n,
                                                     N.sEvB.n, N.sRsq.n, N.dB.n)

                    @block.sync
                    def _(sy):
                        sy.wait_ge(N.dG.sem, bGA)
                        sy.wait_ge(N.sQKb.sem, N.sQKb.n)
                        for c in range(DCH):
                            if c >= 2:
                                sy.wait_ge(N.sSq.sem, bSq + c - 1)
                            ii = sy.dma_start(out=qch[:, (c % 2) * T:(c % 2 + 1) * T],
                                              in_=sc_q[seq, c, :, :])
                            N.dSy.inc(ii, 16)
                            ii = sy.dma_start(out=kch[:, (c % 2) * T:(c % 2 + 1) * T],
                                              in_=sc_k[seq, c, :, :])
                            N.dSy.inc(ii, 16)

                    @block.vector
                    def _(v):
                        for c in range(DCH):
                            v.wait_ge(N.dSy.sem, bSy + 32 * (c + 1))
                            if c >= 1:
                                v.wait_ge(N.sN2MM.sem, bN2 + c)
                            v.tensor_tensor(out=sqh[:, 0:T],
                                            in0=qch[:, (c % 2) * T:(c % 2 + 1) * T],
                                            in1=qch[:, (c % 2) * T:(c % 2 + 1) * T],
                                            op=ALU.mult)
                            ii = v.tensor_tensor(out=sqh[:, T:2 * T],
                                                 in0=kch[:, (c % 2) * T:(c % 2 + 1) * T],
                                                 in1=kch[:, (c % 2) * T:(c % 2 + 1) * T],
                                                 op=ALU.mult)
                            N.sSq.inc(ii)

                    @block.tensor
                    def _(t):
                        for c in range(DCH):
                            t.wait_ge(N.sSq.sem, bSq + c + 1)
                            for w in range(2):
                                pdst = ps_n1 if w == 0 else ps_n2
                                for nsl in range(T // 512):
                                    mm = t.matmul(
                                        pdst[nsl][:],
                                        lhsT=obd[:, c * 16:(c + 1) * 16],
                                        rhs=sqh[:, w * T + nsl * 512:w * T + (nsl + 1) * 512],
                                        start=(c == 0), stop=(c == DCH - 1))
                            N.sN2MM.inc(mm)

                    @block.vector
                    def _(v):
                        v.wait_ge(N.sN2MM.sem, bN2 + DCH)
                        for nsl in range(T // 512):
                            v.tensor_reduce(out=kmx[:, 4 + nsl:5 + nsl],
                                            in_=ps_n2[nsl][:], axis=AX.X, op=ALU.max)
                        v.drain()
                        ii = v.tensor_reduce(out=kmx[:, 0:1],
                                             in_=kmx[:, 4:4 + T // 512], axis=AX.X,
                                             op=ALU.max)
                        N.sEvB.inc(ii)

                    @block.scalar
                    def _(s):
                        s.wait_ge(N.sEvB.sem, bEvB + 1)
                        ii = s.activation(out=kmx[:, 1:2], in_=kmx[:, 0:1], func=AF.Ln)
                        N.sRsq.inc(ii)

                    @block.vector
                    def _(v):
                        v.wait_ge(N.sRsq.sem, bRsq + 1)
                        ii = v.tensor_scalar(out=kmx[:, 2:3], in0=kmx[:, 1:2],
                                             scalar1=0.5, scalar2=None, op0=ALU.mult)
                        N.sEvB.inc(ii)

                    @block.scalar
                    def _(s):
                        s.wait_ge(N.sEvB.sem, bEvB + 2)
                        for nsl in range(T // 512):
                            s.activation(out=mall[:, nsl * 512:(nsl + 1) * 512],
                                         in_=ps_n1[nsl][:], func=AF.Ln)
                        ii = s.activation(out=mall[:, 0:T], in_=mall[:, 0:T],
                                          func=AF.Exp, scale=0.5, bias=kmx[:, 2:3])
                        N.sRsq.inc(ii)

                    @block.vector
                    def _(v):
                        v.wait_ge(N.sRsq.sem, bRsq + 2)
                        v.tensor_scalar(out=mneg[:, 0:T], in0=mall[:, 0:T],
                                        scalar1=-1.0, scalar2=None, op0=ALU.mult)
                        ii = v.tensor_scalar(out=mneg[:, T:TF], in0=mall[:, 0:NFLIP],
                                             scalar1=-1.0, scalar2=None, op0=ALU.mult)
                        N.sEvB.inc(ii)

                    @block.gpsimd
                    def _(g):
                        g.wait_ge(N.sEvB.sem, N.sEvB.n)
                        for h in range(H):
                            ii = g.dma_start(out=sc_m[seq, h, :], in_=mneg[h:h + 1, :])
                            N.dB.inc(ii, 16)

                fence()

                with ExitStack() as s1:
                    ps_log = [pst(s1, "ps_log", [128, 1024]) for _ in range(2)]
                    ps_av = [pst(s1, "ps_av", [65, 512]) for _ in range(2)]
                    ps_bcp = pst(s1, "ps_bcp", [64, 512])
                    bLog, bExp, bAv, bDen, bBc, bNorm, bQKb = (
                        N.sLog.n, N.sExp.n, N.sAv.n, N.sDen.n, N.sBc.n, N.sNorm.n,
                        N.sQKb.n)
                    bSy, bB = N.dSy.n, N.dB.n
                    bSqB = N.sSq.n

                    def idx(h, qt):
                        return h * NQT + qt

                    NGq = [NG] * NQ + [1]
                    GPH = sum(NGq)
                    G0T = {}
                    acc = 0
                    for hh in range(H):
                        for qq in range(NQT):
                            G0T[(hh, qq)] = acc
                            acc += NGq[qq]
                    NGG_ALL = acc

                    def gofq(h, qt):
                        return G0T[(h, qt)]

                    @block.sync
                    def _(sy):
                        sy.wait_ge(N.dG.sem, N.dG.n)
                        sy.wait_ge(N.sAv.sem, bAv)
                        sy.wait_ge(N.sLog.sem, bLog)
                        for i in range(NT):
                            ii = sy.dma_start(out=vsb[:, i * 1040:(i + 1) * 1040],
                                              in_=sc_v[seq, i, :, :])
                            N.dSy.inc(ii, 16)
                        for par in range(2):
                            ii = sy.dma_start(out=kpb[64:67, par * T:(par + 1) * T],
                                              in_=sc_k3[seq, :, :])
                            N.dSy.inc(ii, 16)
                            ii = sy.dma_start(out=qpb[64:65, par * TF:(par + 1) * TF],
                                              in_=sc_q2[seq, 0:1, :])
                            N.dSy.inc(ii, 16)
                            ii = sy.dma_start(out=qpb[66:67, par * TF:(par + 1) * TF],
                                              in_=sc_q2[seq, 1:2, :])
                            N.dSy.inc(ii, 16)
                        for c in range(DCH):
                            if c >= 2:
                                sy.wait_ge(N.sQKb.sem, bQKb + 2 * (c - 1))
                            else:
                                sy.wait_ge(N.sSq.sem, bSqB)
                            ii = sy.dma_start(out=qch[:, (c % 2) * T:(c % 2 + 1) * T],
                                              in_=sc_q[seq, c, :, :])
                            N.dSy.inc(ii, 16)
                            ii = sy.dma_start(out=kch[:, (c % 2) * T:(c % 2 + 1) * T],
                                              in_=sc_k[seq, c, :, :])
                            N.dSy.inc(ii, 16)

                    @block.gpsimd
                    def _(g):
                        g.wait_ge(N.dB.sem, bB)
                        for h in range(H):
                            if h >= 2:
                                g.wait_ge(N.sLog.sem, bLog + (h - 1) * GPH)
                            else:
                                g.wait_ge(N.sLog.sem, bLog)
                            ii = g.dma_start(
                                out=qpb[65:66, (h % 2) * TF:(h % 2 + 1) * TF],
                                in_=sc_m[seq, h, :])
                            N.dB.inc(ii, 16)

                    def emit_build(v, hb):
                        c = hb // 2
                        hw = (hb % 2) * 64
                        v.wait_ge(N.dSy.sem, bSy + 16 * (NT + 6 + 2 * (c + 1)))
                        if hb >= 1:
                            v.wait_ge(N.sAv.sem, bAv + (hb - 1) * NQT)
                        v.tensor_copy(
                            out=qpb[0:64, (hb % 2) * TF:(hb % 2) * TF + T],
                            in_=qch[hw:hw + 64, (c % 2) * T:(c % 2 + 1) * T])
                        v.tensor_copy(
                            out=qpb[0:64, (hb % 2) * TF + T:(hb % 2 + 1) * TF],
                            in_=qch[hw:hw + 64, (c % 2) * T:(c % 2) * T + NFLIP])
                        ii = v.tensor_copy(
                            out=kpb[0:64, (hb % 2) * T:(hb % 2 + 1) * T],
                            in_=kch[hw:hw + 64, (c % 2) * T:(c % 2 + 1) * T])
                        N.sQKb.inc(ii)

                    @block.vector
                    def _(v):
                        emit_build(v, 0)
                        for h in range(H):
                            hw = (h % 2) * 64
                            if h + 1 < H:
                                emit_build(v, h + 1)
                            for qt in range(NQT):
                                ix = idx(h, qt)
                                v.wait_ge(N.sAv.sem, bAv + ix + 1)
                                v.tensor_copy(
                                    out=denr[0:1, 0:QW[qt]],
                                    in_=ps_av[ix % 2][64:65, 0:QW[qt]])
                                if ix >= 1:
                                    v.wait_ge(N.sBc.sem, bBc + ix)
                                v.drain()
                                ii = v.reciprocal(out=denr[0:1, 512:512 + QW[qt]],
                                                  in_=denr[0:1, 0:QW[qt]])
                                N.sDen.inc(ii)
                                v.wait_ge(N.sBc.sem, bBc + ix + 1)
                                v.tensor_copy(
                                    out=osb[:, 0:QW[qt]],
                                    in_=ps_av[ix % 2][0:64, 0:QW[qt]])
                                v.drain()
                                if qt < NQT - 1:
                                    dst = oT[hw:hw + 64,
                                             (h // 2) * T + QOF[qt]:(h // 2) * T + QOF[qt] + QW[qt]]
                                else:
                                    dst = flb[hw:hw + 64,
                                              (h // 2) * NFLIP:(h // 2 + 1) * NFLIP]
                                ii = v.tensor_tensor(out=dst, in0=osb[:, 0:QW[qt]],
                                                     in1=ps_bcp[:, 0:QW[qt]],
                                                     op=ALU.mult)
                                N.sNorm.inc(ii)
                            oslc = oT[hw:hw + 64, (h // 2) * T:(h // 2) * T + REG]
                            fslc = flb[hw:hw + 64, (h // 2) * NFLIP:(h // 2) * NFLIP + REG]
                            v.drain()
                            v.tensor_tensor(out=fltmp[:, :], in0=oslc, in1=fslc,
                                            op=ALU.add)
                            v.drain()
                            v.tensor_scalar(out=oslc, in0=fltmp[:, :], scalar1=0.5,
                                            scalar2=None, op0=ALU.mult)

                    @block.tensor
                    def _(t):
                        NGG = NGG_ALL
                        DECT = {}
                        for hh in range(H):
                            for qq in range(NQT):
                                for g_ in range(NGq[qq]):
                                    DECT[G0T[(hh, qq)] + g_] = (hh, qq, g_)

                        def emit_av(t, gg):
                            hh, qq, g = dec_ = DECT[gg]
                            ixx = idx(hh, qq)
                            if g == 0 and ixx >= 2:
                                t.wait_ge(N.sNorm.sem, bNorm + ixx - 1)
                            t.wait_ge(N.sExp.sem, bExp + gg + 1)
                            if qq < NQ:
                                for half in range(2):
                                    kt = 2 * g + half
                                    mm = t.matmul(
                                        ps_av[ixx % 2][:, 0:QW[qq]],
                                        lhsT=vsb[:, kt * 1040 + hh * 65:
                                                 kt * 1040 + (hh + 1) * 65],
                                        rhs=ptb[:, (gg % 2) * 1024 + half * QW[qq]:
                                                (gg % 2) * 1024 + (half + 1) * QW[qq]],
                                        start=(g == 0 and half == 0),
                                        stop=(g == NG - 1 and half == 1))
                                if g == NG - 1:
                                    N.sAv.inc(mm)
                            else:
                                for kt in range(NT):
                                    mm = t.matmul(
                                        ps_av[ixx % 2][:, 0:NFLIP],
                                        lhsT=vsb[:, kt * 1040 + hh * 65:
                                                 kt * 1040 + (hh + 1) * 65],
                                        rhs=ptb[:, (gg % 2) * 1024 + kt * NFLIP:
                                                (gg % 2) * 1024 + (kt + 1) * NFLIP],
                                        start=(kt == 0), stop=(kt == NT - 1))
                                N.sAv.inc(mm)

                        def emit_bcast(t, ixx):
                            t.wait_ge(N.sDen.sem, bDen + ixx + 1)
                            t.wait_ge(N.sNorm.sem, bNorm + max(0, ixx))
                            pqw = QW[ixx % NQT]
                            mm = t.matmul(ps_bcp[:, 0:pqw], lhsT=onesf[0:1, 0:64],
                                          rhs=denr[0:1, 512:512 + pqw],
                                          start=True, stop=True)
                            N.sBc.inc(mm)

                        cslack = min(7, NG - 1)
                        slot_of = {}
                        tail_bcasts = []
                        for j in range(H * NQT):
                            hh, qq = j // NQT, j % NQT
                            gend = G0T[(hh, qq)] + NGq[qq]
                            s_ = gend + cslack
                            if j + 1 < H * NQT:
                                h2_, q2_ = (j + 1) // NQT, (j + 1) % NQT
                                gend2 = G0T[(h2_, q2_)] + NGq[q2_]
                                s_ = min(s_, gend2)
                            if s_ >= NGG:
                                tail_bcasts.append(j)
                            else:
                                slot_of.setdefault(s_, []).append(j)

                        for h in range(H):
                            t.wait_ge(N.sQKb.sem, bQKb + h + 1)
                            t.wait_ge(N.dB.sem, bB + 16 * (h + 1))
                            for qt in range(NQT):
                                ix = idx(h, qt)
                                g0 = gofq(h, qt)
                                for g in range(NGq[qt]):
                                    gg = g0 + g
                                    for j in slot_of.get(gg, []):
                                        emit_bcast(t, j)
                                    if gg >= 2:
                                        t.wait_ge(N.sExp.sem, bExp + gg - 1)
                                    if qt < NQ:
                                        for half in range(2):
                                            kt = 2 * g + half
                                            kk = 67 if kt == 0 else 66
                                            mm = t.matmul(
                                                ps_log[gg % 2][:, half * QW[qt]:
                                                               (half + 1) * QW[qt]],
                                                lhsT=kpb[0:kk, (h % 2) * T + kt * 128:
                                                         (h % 2) * T + (kt + 1) * 128],
                                                rhs=qpb[0:kk, (h % 2) * TF + QOF[qt]:
                                                        (h % 2) * TF + QOF[qt] + QW[qt]],
                                                start=True, stop=True)
                                    else:
                                        for kt in range(NT):
                                            kk = 67 if kt == 0 else 66
                                            mm = t.matmul(
                                                ps_log[gg % 2][:, kt * NFLIP:
                                                               (kt + 1) * NFLIP],
                                                lhsT=kpb[0:kk, (h % 2) * T + kt * 128:
                                                         (h % 2) * T + (kt + 1) * 128],
                                                rhs=qpb[0:kk, (h % 2) * TF + QOF[qt]:
                                                        (h % 2) * TF + QOF[qt] + QW[qt]],
                                                start=True, stop=True)
                                    N.sLog.inc(mm)
                                    if gg >= 1:
                                        emit_av(t, gg - 1)
                        emit_av(t, NGG - 1)
                        for j in tail_bcasts:
                            emit_bcast(t, j)

                    @block.scalar
                    def _(s):
                        for h in range(H):
                            for qt in range(NQT):
                                g0 = gofq(h, qt)
                                for g in range(NGq[qt]):
                                    gg = g0 + g
                                    s.wait_ge(N.sLog.sem, bLog + gg + 1)
                                    w2 = 2 * QW[qt] if qt < NQ else NT * NFLIP
                                    ii = s.activation(
                                        out=ptb[:, (gg % 2) * 1024:(gg % 2) * 1024 + w2],
                                        in_=ps_log[gg % 2][:, 0:w2],
                                        func=AF.Exp, bias=cbias[:, 1:2])
                                    N.sExp.inc(ii)

                fence()

                with ExitStack() as s1:
                    yrl = sb(s1, "yrl", [128, 2 * dm], F32)
                    h2b = sb(s1, "h2b", [128, 2 * dm], F32)
                    sqc = sb(s1, "sqc", [128, dm], F32)
                    zst = sb(s1, "zst", [128, 2 * dm], F32)
                    ps_pj = [pst(s1, "ps_pj", [128, 1024]) for _ in range(2)]
                    bSy, bG = N.dSy.n, N.dG.n
                    bPj, bH2, bVar2, bRstd2, bZ = (N.sPjMM.n, N.sH2.n, N.sVar2.n,
                                                   N.sRstd2.n, N.sZ.n)
                    bSqB = N.sSq.n

                    @block.sync
                    def _(sy):
                        sy.wait_ge(N.dG.sem, bG)
                        for i in range(NT):
                            if i >= 2:
                                sy.wait_ge(N.sH2.sem, bH2 + i - 1)
                            ii = sy.dma_start(out=yrl[:, (i % 2) * dm:(i % 2 + 1) * dm],
                                              in_=sc_y[seq, i * 128:(i + 1) * 128, :])
                            N.dSy.inc(ii, 16)

                    @block.tensor
                    def _(t):
                        for i in range(NT):
                            if i >= 2:
                                t.wait_ge(N.sH2.sem, bH2 + i - 1)
                            for nh in range(2):
                                for cc in range(DCH):
                                    mm = t.matmul(
                                        ps_pj[i % 2][:, nh * 512:(nh + 1) * 512],
                                        lhsT=oT[:, cc * T + i * 128:cc * T + (i + 1) * 128],
                                        rhs=wpb[:, cc * 1024 + nh * 512:cc * 1024 + (nh + 1) * 512],
                                        start=(cc == 0), stop=(cc == DCH - 1))
                            N.sPjMM.inc(mm)

                    @block.vector
                    def _(v):
                        def emit_z(j):
                            pj = j % 2
                            cb = pj * 8
                            v.wait_ge(N.sRstd2.sem, bRstd2 + j + 1)
                            if j >= 2:
                                v.wait_ge(N.dG.sem, bG + 16 * (j - 1))
                            ii = v.tensor_scalar(
                                out=zst[:, pj * dm:(pj + 1) * dm],
                                in0=h2b[:, pj * dm:(pj + 1) * dm],
                                scalar1=lncols[:, cb + 0:cb + 1],
                                scalar2=lncols[:, cb + 5:cb + 6],
                                op0=ALU.subtract, op1=ALU.mult)
                            N.sZ.inc(ii)

                        for i in range(NT):
                            pc = i % 2
                            cb = pc * 8
                            v.wait_ge(N.sPjMM.sem, bPj + i + 1)
                            v.wait_ge(N.dSy.sem, bSy + 16 * (i + 1))
                            v.scalar_tensor_tensor(
                                out=h2b[:, pc * dm:(pc + 1) * dm], in0=ps_pj[pc][:],
                                scalar=1.0, in1=yrl[:, pc * dm:(pc + 1) * dm],
                                op0=ALU.mult, op1=ALU.add)
                            ii = v.scalar_tensor_tensor(
                                out=h2b[:, pc * dm:(pc + 1) * dm],
                                in0=h2b[:, pc * dm:(pc + 1) * dm],
                                scalar=1.0, in1=bias_bc[:], op0=ALU.mult, op1=ALU.add,
                                accum_out=lncols[:, cb + 3:cb + 4])
                            N.sH2.inc(ii)
                            v.drain()
                            v.tensor_scalar(out=lncols[:, cb + 0:cb + 1],
                                            in0=lncols[:, cb + 3:cb + 4],
                                            scalar1=1.0 / dm, scalar2=None, op0=ALU.mult)
                            v.drain()
                            v.tensor_tensor(out=lncols[:, cb + 1:cb + 2],
                                            in0=lncols[:, cb + 0:cb + 1],
                                            in1=lncols[:, cb + 0:cb + 1], op=ALU.mult)
                            v.wait_ge(N.sSq.sem, bSqB + i + 1)
                            v.tensor_scalar(out=lncols[:, cb + 2:cb + 3],
                                            in0=lncols[:, cb + 4:cb + 5],
                                            scalar1=1.0 / dm, scalar2=None, op0=ALU.mult)
                            v.drain()
                            ii = v.tensor_tensor(out=lncols[:, cb + 2:cb + 3],
                                                 in0=lncols[:, cb + 2:cb + 3],
                                                 in1=lncols[:, cb + 1:cb + 2],
                                                 op=ALU.subtract)
                            N.sVar2.inc(ii)
                            if i >= 1:
                                emit_z(i - 1)
                        emit_z(NT - 1)

                    @block.scalar
                    def _(s):
                        for i in range(NT):
                            cb = (i % 2) * 8
                            s.wait_ge(N.sH2.sem, bH2 + i + 1)
                            ii = s.activation(out=sqc[:],
                                              in_=h2b[:, (i % 2) * dm:(i % 2 + 1) * dm],
                                              func=AF.Square,
                                              accum_out=lncols[:, cb + 4:cb + 5])
                            N.sSq.inc(ii)
                            s.wait_ge(N.sVar2.sem, bVar2 + i + 1)
                            s.activation(out=lncols[:, cb + 6:cb + 7],
                                         in_=lncols[:, cb + 2:cb + 3],
                                         func=AF.Ln, bias=cbias[:, 0:1])
                            ii = s.activation(out=lncols[:, cb + 5:cb + 6],
                                              in_=lncols[:, cb + 6:cb + 7],
                                              func=AF.Exp, scale=-0.5)
                            N.sRstd2.inc(ii)

                    @block.gpsimd
                    def _(g):
                        for i in range(NT):
                            g.wait_ge(N.sZ.sem, bZ + i + 1)
                            ii = g.dma_start(out=zout[seq, i * 128:(i + 1) * 128, :],
                                             in_=zst[:, (i % 2) * dm:(i % 2 + 1) * dm])
                            N.dG.inc(ii, 16)

                fence()

        @block.gpsimd
        def _(g):
            g.wait_ge(N.dG.sem, N.dG.n)

    return nc


_NC_CACHE = {}


def _get_nc():
    key = "main"
    if key not in _NC_CACHE:
        _NC_CACHE[key] = build()
    return _NC_CACHE[key]


def kernel(L0_attn_x, L0_attn_basis, L0_attn_q_coeffs, L0_attn_k_coeffs,
           L0_attn_v_coeffs, L0_attn_o_coeffs, L0_attn_decay_logit,
           L0_attn_out_scale, L0_ffn_g_info, L0_ffn_W_qkv, L0_ffn_W_proj,
           L0_ffn_b_proj):
    x = np.ascontiguousarray(np.asarray(L0_attn_x, dtype=np.float32))
    B, T, dm = x.shape
    ncore = 8
    nseq = B // ncore
    shards = x.reshape(ncore, nseq, T, dm)

    com = {
        "basis": np.ascontiguousarray(np.asarray(L0_attn_basis, np.float32)),
        "qc": np.ascontiguousarray(np.asarray(L0_attn_q_coeffs, np.float32)),
        "kc": np.ascontiguousarray(np.asarray(L0_attn_k_coeffs, np.float32)),
        "vc": np.ascontiguousarray(np.asarray(L0_attn_v_coeffs, np.float32)),
        "oc": np.ascontiguousarray(np.asarray(L0_attn_o_coeffs, np.float32)),
        "dlog": np.asarray(L0_attn_decay_logit, np.float32).reshape(1, 1),
        "oscale": np.asarray(L0_attn_out_scale, np.float32).reshape(1, 1),
        "gin": np.ascontiguousarray(
            np.asarray(L0_ffn_g_info, np.float32)[0, 0, :].reshape(16, 64)),
        "wqkv": np.ascontiguousarray(np.asarray(L0_ffn_W_qkv, np.float32)),
        "wproj": np.ascontiguousarray(np.asarray(L0_ffn_W_proj, np.float32)),
        "bproj": np.ascontiguousarray(
            np.asarray(L0_ffn_b_proj, np.float32).reshape(1, dm)),
    }
    in_maps = [dict(com, xin=np.ascontiguousarray(shards[c]))
               for c in range(ncore)]

    nc = _get_nc()
    res = run_bass_kernel_spmd(nc, in_maps, core_ids=list(range(ncore)))
    out = np.empty((B, T, dm), np.float32)
    for c in range(ncore):
        out[c * nseq:(c + 1) * nseq] = res.results[c]["zout"]
    return out


# version 21
# speedup vs baseline: 1.1761x; 1.0265x over previous
import numpy as np
import ml_dtypes
import concourse.bass as bass
import concourse.mybir as mybir
from concourse.bass_utils import run_bass_kernel_spmd
from contextlib import ExitStack

F32 = mybir.dt.float32
BF16 = mybir.dt.bfloat16
AF = mybir.ActivationFunctionType
ALU = mybir.AluOpType
AX = mybir.AxisListType

TAPS = False
BIGH = 16384.0
LNEPS = 1e-5
REG = 5
NFLIP = 8

SEM_NAMES = [
    "dSy", "dG", "dB",
    "fSy", "fG", "fT", "fS", "fV",
    "sCast", "sTrX", "sEvX",
    "sMM1", "sEv1",
    "sSc", "sSd", "sAvR", "sRet", "sEvR",
    "sAoMM", "sH1", "sVar", "sRstd", "sY", "sYb", "sTrY", "sEvY",
    "sWcast", "sQKMM", "sEvP",
    "sSq", "sN2MM", "sDotMM", "sEvB", "sRsq",
    "sCol", "sMid",
    "sQKb", "sLog", "sExp", "sAv", "sDen", "sBc", "sNorm",
    "sPjMM", "sH2", "sVar2", "sRstd2", "sZ",
]


class Sem:
    def __init__(self, nc, ctx, name):
        self.sem = ctx.enter_context(nc.semaphore(name))
        self.n = 0

    def inc(self, instr, k=1):
        instr.then_inc(self.sem, k)
        self.n += k


def build(nseq=2, T=2048, dm=1024, taps=TAPS):
    C = 256
    H, DH = 16, 64
    NT = T // 128
    NQ = T // 512
    DCH = dm // 128
    scale = DH ** -0.5
    TF = T + NFLIP

    nc = bass.Bass("TRN2", target_bir_lowering=False, detect_race_conditions=False)

    xin = nc.dram_tensor("xin", [nseq, T, dm], F32, kind="ExternalInput")
    basis = nc.dram_tensor("basis", [dm, 128], F32, kind="ExternalInput")
    coeffs = {nm: nc.dram_tensor(nm, [C, 128], F32, kind="ExternalInput")
              for nm in ["qc", "kc", "vc", "oc"]}
    dlog = nc.dram_tensor("dlog", [1, 1], F32, kind="ExternalInput")
    oscale = nc.dram_tensor("oscale", [1, 1], F32, kind="ExternalInput")
    gin = nc.dram_tensor("gin", [16, 64], F32, kind="ExternalInput")
    wqkv = nc.dram_tensor("wqkv", [dm, 3 * dm], F32, kind="ExternalInput")
    wproj = nc.dram_tensor("wproj", [dm, dm], F32, kind="ExternalInput")
    bproj = nc.dram_tensor("bproj", [1, dm], F32, kind="ExternalInput")
    zout = nc.dram_tensor("zout", [nseq, T, dm], F32, kind="ExternalOutput")

    sck = "ExternalOutput" if taps else "Internal"
    sc_y = nc.dram_tensor("sc_y", [nseq, T, dm], F32, kind=sck)
    sc_q = nc.dram_tensor("sc_q", [nseq, DCH, 128, T], BF16, kind=sck)
    sc_k = nc.dram_tensor("sc_k", [nseq, DCH, 128, T], BF16, kind=sck)
    sc_v = nc.dram_tensor("sc_v", [nseq, NT, 128, H * 65], BF16, kind=sck)
    sc_sim = nc.dram_tensor("sc_sim", [nseq, 1, T], F32, kind=sck)
    cc_in = nc.dram_tensor("cc_in", [1, 2], F32, kind="Internal")
    cc_out = nc.dram_tensor("cc_out", [1, 2], F32, kind="Internal")
    sc_k3 = nc.dram_tensor("sc_k3", [nseq, 3, T], BF16, kind="Internal")
    dbg_b = nc.dram_tensor("dbg_b", [8, 128, 512], BF16, kind=sck)
    dbg_f = nc.dram_tensor("dbg_f", [1, 64], F32, kind=sck)
    dbg_h = nc.dram_tensor("dbg_h", [128, 1024 + 16], F32, kind=sck)
    sc_q2 = nc.dram_tensor("sc_q2", [nseq, 2, TF], BF16, kind="Internal")
    sc_m = nc.dram_tensor("sc_m", [nseq, H, TF], BF16, kind="Internal")

    octx = ExitStack()
    with octx:
        _uid = [0]

        def sb(ctx, name, shape, dt):
            _uid[0] += 1
            return ctx.enter_context(nc.sbuf_tensor(f"{name}_{_uid[0]}", shape, dt))

        def pst(ctx, name, shape, dt=F32):
            _uid[0] += 1
            return ctx.enter_context(nc.psum_tensor(f"{name}_{_uid[0]}", shape, dt))

        ident = sb(octx, "ident", [128, 128], BF16)
        e0 = sb(octx, "e0", [128, 512], F32)
        dconst = sb(octx, "dconst", [128, NT * 512], BF16)
        bias_bc = sb(octx, "bias_bc", [128, dm], F32)
        rqw = sb(octx, "rqw", [128, DCH * C], BF16)
        rkw = sb(octx, "rkw", [128, DCH * C], BF16)
        rvw = sb(octx, "rvw", [128, DCH * C], BF16)
        rowT = sb(octx, "rowT", [128, 2 * dm], BF16)
        gbd = sb(octx, "gbd", [128, 128], BF16)
        obd = sb(octx, "obd", [128, 128], BF16)
        onesf = sb(octx, "onesf", [1, 128], F32)
        ones16f = sb(octx, "ones16f", [16, 1], F32)
        scal = sb(octx, "scal", [1, 32], F32)
        cols = sb(octx, "cols", [128, 4], F32)
        lncols = sb(octx, "lncols", [128, 16], F32)
        arenaF = sb(octx, "arenaF", [128, T], F32)
        arenaB = sb(octx, "arenaB", [128, TF], BF16)
        cbias = sb(octx, "cbias", [128, 2], F32)

        S = {name: Sem(nc, octx, name) for name in SEM_NAMES}

        class N:
            pass

        for k_, v_ in S.items():
            setattr(N, k_, v_)

        block = octx.enter_context(nc.Block())
        ENGS = ["sync", "gpsimd", "tensor", "scalar", "vector"]

        def fence():
            fence.round += 1
            r = fence.round
            fs = {"sync": N.fSy, "gpsimd": N.fG, "tensor": N.fT,
                  "scalar": N.fS, "vector": N.fV}

            def mk(which):
                def f(eng):
                    if which != "sync":
                        eng.drain()
                    eng.sem_inc(fs[which].sem, 1)
                    for w2, s2 in fs.items():
                        if w2 != which:
                            eng.wait_ge(s2.sem, r)
                return f
            for w in ENGS:
                getattr(block, w)(mk(w))
            for s2 in fs.values():
                s2.n = r
        fence.round = 0

        with ExitStack() as pctx:
            bss = sb(pctx, "bss", [128, 1024], F32)
            bsb = sb(pctx, "bsb", [128, 1024], BF16)
            bsT = sb(pctx, "bsT", [128, 1024], BF16)
            ccs = sb(pctx, "ccs", [128, 256], F32)
            ccb = sb(pctx, "ccb", [128, 256], BF16)
            ccT = sb(pctx, "ccT", [128, 256], BF16)
            bpr = sb(pctx, "bpr", [1, dm], F32)
            sct = sb(pctx, "sct", [1, 32], F32)
            gh = sb(pctx, "gh", [16, 64], F32)
            ghb = sb(pctx, "ghb", [16, 64], BF16)
            ghTb = sb(pctx, "ghTb", [64, 16], BF16)
            gtmp = sb(pctx, "gtmp", [16, 64], F32)
            gcol = sb(pctx, "gcol", [16, 4], F32)
            dta = sb(pctx, "dta", [128, 512], F32)
            dtb = sb(pctx, "dtbx", [128, 512], F32)
            dtm = sb(pctx, "dtm", [128, 512], F32)
            ps1 = pst(pctx, "ps1", [128, 512])
            ps2 = pst(pctx, "ps2", [128, 512])
            psT = pst(pctx, "psT", [128, 1024], BF16)
            psB = pst(pctx, "psB", [64, 16], BF16)
            psC = pst(pctx, "psC", [128, 16])

            @block.sync
            def _(sy):
                for c in range(DCH):
                    i = sy.dma_start(out=bss[:, c * 128:(c + 1) * 128],
                                     in_=basis[c * 128:(c + 1) * 128, :])
                    N.dSy.inc(i, 16)
                i = sy.dma_start(out=sct[0:1, 0:1], in_=dlog[:, :])
                N.dSy.inc(i, 16)
                i = sy.dma_start(out=sct[0:1, 8:9], in_=oscale[:, :])
                N.dSy.inc(i, 16)
                i = sy.dma_start(out=gh[:, :], in_=gin[:, :])
                N.dSy.inc(i, 16)
                i = sy.dma_start(out=bpr[:, :], in_=bproj[:, :])
                N.dSy.inc(i, 16)

            @block.gpsimd
            def _(g):
                i = g.iota(e0[:], pattern=[[-1, 512]], base=0, channel_multiplier=1,
                           allow_small_or_imprecise_dtypes=True)
                N.sMid.inc(i)

            @block.vector
            def _(v):
                v.wait_ge(N.sMid.sem, N.sMid.n)
                v.memset(onesf[:], 1.0)
                v.memset(cbias[:, 0:1], LNEPS)
                v.memset(scal[:], 0.0)
                v.memset(lncols[:], 0.0)
                v.memset(cols[:], 0.0)
                v.memset(cbias[:, 1:2], -BIGH)
                v.memset(ones16f[:], 1.0 / 16.0)
                v.memset(obd[:], 0.0)
                v.memset(gbd[:], 0.0)
                for h in range(H):
                    v.memset(obd[(h % 2) * 64:(h % 2) * 64 + 64,
                                 (h // 2) * 16 + h:(h // 2) * 16 + h + 1], 1.0)
                v.tensor_scalar(out=ident[:], in0=e0[:, 0:128], scalar1=0.0,
                                scalar2=None, op0=ALU.is_equal)
                v.wait_ge(N.dSy.sem, N.dSy.n)
                v.tensor_copy(out=bsb[:], in_=bss[:])
                v.tensor_tensor(out=gtmp[:], in0=gh[:], in1=gh[:], op=ALU.mult)
                i = v.tensor_reduce(out=gcol[:, 0:1], in_=gtmp[:], axis=AX.X,
                                    op=ALU.add)
                N.sVar.inc(i)

            @block.scalar
            def _(s):
                s.wait_ge(N.sVar.sem, N.sVar.n)
                s.activation(out=gcol[:, 1:2], in_=gcol[:, 0:1], func=AF.Ln)
                s.activation(out=gcol[:, 2:3], in_=gcol[:, 1:2], func=AF.Exp,
                             scale=-0.5)
                i = s.activation(out=sct[0:1, 2:3], in_=sct[0:1, 0:1], func=AF.Exp,
                                 scale=-1.0)
                N.sRstd.inc(i)

            @block.vector
            def _(v):
                v.wait_ge(N.sRstd.sem, N.sRstd.n)
                v.tensor_scalar(out=ghb[:], in0=gh[:], scalar1=gcol[:, 2:3],
                                scalar2=None, op0=ALU.mult)
                i = v.tensor_scalar(out=sct[0:1, 3:4], in0=sct[0:1, 2:3], scalar1=1.0,
                                    scalar2=None, op0=ALU.add)
                N.sY.inc(i)

            @block.scalar
            def _(s):
                s.wait_ge(N.sY.sem, N.sY.n)
                i = s.activation(out=sct[0:1, 16:17], in_=sct[0:1, 3:4], func=AF.Ln)
                N.sRstd.inc(i)

            @block.vector
            def _(v):
                v.wait_ge(N.sRstd.sem, N.sRstd.n)
                i = v.tensor_scalar(out=scal[0:1, 1:2], in0=sct[0:1, 16:17],
                                    scalar1=-1.0, scalar2=None, op0=ALU.mult)
                N.sY.inc(i)

            @block.scalar
            def _(s):
                s.wait_ge(N.sY.sem, N.sY.n)
                i = s.engine_nop() if False else s.activation(
                    out=sct[0:1, 17:18], in_=sct[0:1, 16:17], func=AF.Identity)
                N.sRstd.inc(i)

            fence()

            @block.tensor
            def _(t):
                for c in range(DCH):
                    i = t.transpose(psT[:, c * 128:(c + 1) * 128],
                                    bsb[:, c * 128:(c + 1) * 128], ident[:])
                i = t.transpose(psB[:], ghb[:], ident[0:16, 0:16])
                i = t.matmul(psC[:, 0:1], lhsT=onesf[:], rhs=scal[0:1, 1:2],
                             start=True, stop=True)
                i = t.matmul(psC[:, 1:2], lhsT=onesf[:], rhs=sct[0:1, 8:9],
                             start=True, stop=True)
                i = t.matmul(ps1[:], lhsT=onesf[:], rhs=bpr[0:1, 0:512],
                             start=True, stop=True)
                i = t.matmul(ps2[:], lhsT=onesf[:], rhs=bpr[0:1, 512:1024],
                             start=True, stop=True)
                N.sMM1.inc(i)

            @block.vector
            def _(v):
                v.wait_ge(N.sMM1.sem, N.sMM1.n)
                v.tensor_copy(out=bsT[:], in_=psT[:])
                v.tensor_copy(out=ghTb[:], in_=psB[:])
                v.drain()
                for h in range(H):
                    v.tensor_copy(out=gbd[(h % 2) * 64:(h % 2) * 64 + 64,
                                          (h // 2) * 16 + h:(h // 2) * 16 + h + 1],
                                  in_=ghTb[:, h:h + 1])
                v.tensor_copy(out=cols[:, 0:1], in_=psC[:, 0:1])
                v.tensor_copy(out=cols[:, 1:2], in_=psC[:, 1:2])
                v.tensor_copy(out=bias_bc[:, 0:512], in_=ps1[:])
                i = v.tensor_copy(out=bias_bc[:, 512:1024], in_=ps2[:])
                N.sEv1.inc(i)

            bV0, bA0 = N.sVar.n, N.sRstd.n

            @block.vector
            def _(v):
                for d in range(NT):
                    buf = dta if d % 2 == 0 else dtb
                    if d >= 2:
                        v.wait_ge(N.sRstd.sem, bA0 + d - 1)
                    i = v.tensor_scalar(out=buf[:], in0=e0[:], scalar1=128.0 * d - 1.0,
                                        scalar2=0.0, op0=ALU.add, op1=ALU.max)
                    N.sVar.inc(i)

            @block.scalar
            def _(s):
                s.wait_ge(N.sEv1.sem, N.sEv1.n)
                for d in range(NT):
                    buf = dta if d % 2 == 0 else dtb
                    s.wait_ge(N.sVar.sem, bV0 + d + 1)
                    i = s.activation(out=dconst[:, d * 512:(d + 1) * 512], in_=buf[:],
                                     func=AF.Exp, scale=cols[:, 0:1])
                    N.sRstd.inc(i)

            @block.vector
            def _(v):
                v.wait_ge(N.sRstd.sem, N.sRstd.n)
                i = None
                for d in range(min(4, NT)):
                    v.tensor_scalar(out=dtm[:], in0=e0[:], scalar1=128.0 * d - 1.0,
                                    scalar2=0.0, op0=ALU.add, op1=ALU.is_ge)
                    i = v.tensor_tensor(out=dconst[:, d * 512:(d + 1) * 512],
                                        in0=dconst[:, d * 512:(d + 1) * 512],
                                        in1=dtm[:], op=ALU.mult)
                N.sEv1.inc(i)

            for nm, dst in [("qc", rqw), ("kc", rkw), ("vc", rvw), ("oc", None)]:
                fence()

                @block.sync
                def _(sy, nm=nm):
                    for c in range(2):
                        i = sy.dma_start(out=ccs[:, c * 128:(c + 1) * 128],
                                         in_=coeffs[nm][c * 128:(c + 1) * 128, :])
                        N.dSy.inc(i, 16)

                @block.vector
                def _(v):
                    v.wait_ge(N.dSy.sem, N.dSy.n)
                    i = v.tensor_copy(out=ccb[:], in_=ccs[:])
                    N.sEv1.inc(i)

                @block.tensor
                def _(t):
                    t.wait_ge(N.sEv1.sem, N.sEv1.n)
                    for c in range(2):
                        i = t.transpose(psT[:, c * 128:(c + 1) * 128],
                                        ccb[:, c * 128:(c + 1) * 128], ident[:])
                    N.sMM1.inc(i)

                @block.vector
                def _(v):
                    v.wait_ge(N.sMM1.sem, N.sMM1.n)
                    i = v.tensor_copy(out=ccT[:], in_=psT[:, 0:256])
                    N.sEv1.inc(i)

                if dst is not None:
                    bM, bE = N.sMM1.n, N.sEv1.n

                    @block.tensor
                    def _(t):
                        t.wait_ge(N.sEv1.sem, bE)
                        for c in range(DCH):
                            if c >= 2:
                                t.wait_ge(N.sEv1.sem, bE + c - 1)
                            i = t.matmul((ps1 if c % 2 == 0 else ps2)[:, 0:C],
                                         lhsT=bsT[:, c * 128:(c + 1) * 128],
                                         rhs=ccT[:], start=True, stop=True)
                            N.sMM1.inc(i)

                    @block.vector
                    def _(v, dst=dst):
                        for c in range(DCH):
                            v.wait_ge(N.sMM1.sem, bM + c + 1)
                            i = v.tensor_copy(out=dst[:, c * C:(c + 1) * C],
                                              in_=(ps1 if c % 2 == 0 else ps2)[:, 0:C])
                            N.sEv1.inc(i)
                else:
                    bM, bE = N.sMM1.n, N.sEv1.n

                    @block.tensor
                    def _(t):
                        t.wait_ge(N.sEv1.sem, bE)
                        k = 0
                        for cc in range(2):
                            for nh in range(2):
                                if k >= 2:
                                    t.wait_ge(N.sEv1.sem, bE + k - 1)
                                i = t.matmul((ps1 if k % 2 == 0 else ps2)[:],
                                             lhsT=ccT[:, cc * 128:(cc + 1) * 128],
                                             rhs=bsT[:, nh * 512:(nh + 1) * 512],
                                             start=True, stop=True)
                                N.sMM1.inc(i)
                                k += 1

                    @block.vector
                    def _(v):
                        k = 0
                        for cc in range(2):
                            for nh in range(2):
                                v.wait_ge(N.sMM1.sem, bM + k + 1)
                                i = v.tensor_scalar(
                                    out=rowT[:, cc * dm + nh * 512:cc * dm + (nh + 1) * 512],
                                    in0=(ps1 if k % 2 == 0 else ps2)[:],
                                    scalar1=cols[:, 1:2], scalar2=None, op0=ALU.mult)
                                N.sEv1.inc(i)
                                k += 1

            fence()

        if taps:
            @block.gpsimd
            def _(g):
                ii = g.dma_start(out=dbg_b[0, :, :], in_=dconst[:, 0:512])
                N.dG.inc(ii, 16)
                ii = g.dma_start(out=dbg_b[1, :, :],
                                 in_=dconst[:, (NT - 1) * 512:NT * 512])
                N.dG.inc(ii, 16)
                ii = g.dma_start(out=dbg_b[2, :, :], in_=rqw[:, 0:512])
                N.dG.inc(ii, 16)
                ii = g.dma_start(out=dbg_f[0, 0:32], in_=scal[0:1, 0:32])
                N.dG.inc(ii, 16)
                ii = g.dma_start(out=dbg_f[0, 32:36], in_=cols[0:1, 0:4])
                N.dG.inc(ii, 16)
                g.wait_ge(N.dG.sem, N.dG.n)

        SDEP_A = 2
        with ExitStack() as actx:
            xyT = sb(actx, "xyT", [128, DCH * T], BF16)
            qtb = sb(actx, "qtb", [128, 2 * T], BF16)
            ktb = sb(actx, "ktb", [128, 2 * T], BF16)
            vtb = sb(actx, "vtb", [128, NT * C], BF16)
            rtb = sb(actx, "rtb", [128, 2 * T], BF16)
            sdb = sb(actx, "sdb", [128, SDEP_A * 512], BF16)

            for seq in range(nseq):
                with ExitStack() as s1:
                    xld = sb(s1, "xld", [128, 2 * dm], F32)
                    xcb = sb(s1, "xcb", [128, 2 * dm], BF16)
                    ps_tr = pst(s1, "ps_tr", [128, 2 * 1024], BF16)
                    bSy, bC, bT, bE = N.dSy.n, N.sCast.n, N.sTrX.n, N.sEvX.n

                    @block.sync
                    def _(sy):
                        for i in range(NT):
                            if i >= 2:
                                sy.wait_ge(N.sCast.sem, bC + i - 1)
                            ii = sy.dma_start(out=xld[:, (i % 2) * dm:(i % 2 + 1) * dm],
                                              in_=xin[seq, i * 128:(i + 1) * 128, :])
                            N.dSy.inc(ii, 16)

                    @block.vector
                    def _(v):
                        for i in range(NT):
                            v.wait_ge(N.dSy.sem, bSy + 16 * (i + 1))
                            if i >= 2:
                                v.wait_ge(N.sTrX.sem, bT + i - 1)
                            ii = v.tensor_copy(out=xcb[:, (i % 2) * dm:(i % 2 + 1) * dm],
                                               in_=xld[:, (i % 2) * dm:(i % 2 + 1) * dm])
                            N.sCast.inc(ii)
                            if i >= 1:
                                j = i - 1
                                v.wait_ge(N.sTrX.sem, bT + j + 1)
                                dst = xyT[:].rearrange("p (c n) -> p c n", c=DCH)[
                                    :, :, j * 128:(j + 1) * 128]
                                src = ps_tr[:, (j % 2) * 1024:(j % 2 + 1) * 1024
                                            ].rearrange("p (c n) -> p c n", c=DCH)
                                ii = v.tensor_copy(out=dst, in_=src)
                                N.sEvX.inc(ii)
                        j = NT - 1
                        v.wait_ge(N.sTrX.sem, bT + NT)
                        dst = xyT[:].rearrange("p (c n) -> p c n", c=DCH)[
                            :, :, j * 128:(j + 1) * 128]
                        src = ps_tr[:, (j % 2) * 1024:(j % 2 + 1) * 1024
                                    ].rearrange("p (c n) -> p c n", c=DCH)
                        ii = v.tensor_copy(out=dst, in_=src)
                        N.sEvX.inc(ii)

                    @block.tensor
                    def _(t):
                        for i in range(NT):
                            t.wait_ge(N.sCast.sem, bC + i + 1)
                            if i >= 2:
                                t.wait_ge(N.sEvX.sem, bE + i - 1)
                            for c in range(DCH):
                                ii = t.transpose(
                                    ps_tr[:, (i % 2) * 1024 + c * 128:
                                          (i % 2) * 1024 + (c + 1) * 128],
                                    xcb[:, (i % 2) * dm + c * 128:(i % 2) * dm + (c + 1) * 128],
                                    ident[:])
                            N.sTrX.inc(ii)

                fence()

                with ExitStack() as s1:
                    ps_q = [pst(s1, "ps_q", [128, 512]) for _ in range(2)]
                    ps_v = [pst(s1, "ps_v", [128, C]) for _ in range(2)]
                    bM, bE = N.sQKMM.n, N.sEvP.n
                    qk_tiles = [(w, cc, nt) for w in range(2) for cc in range(2)
                                for nt in range(NQ)]

                    @block.tensor
                    def _(t):
                        k = 0
                        for (w, cc, nt) in qk_tiles:
                            wsrc = rqw if w == 0 else rkw
                            if k >= 2:
                                t.wait_ge(N.sEvP.sem, bE + k - 1)
                            for c in range(DCH):
                                ii = t.matmul(
                                    ps_q[k % 2][:],
                                    lhsT=wsrc[:, c * C + cc * 128:c * C + (cc + 1) * 128],
                                    rhs=xyT[:, c * T + nt * 512:c * T + (nt + 1) * 512],
                                    start=(c == 0), stop=(c == DCH - 1))
                            N.sQKMM.inc(ii)
                            k += 1
                        for i in range(NT):
                            if k >= 2:
                                t.wait_ge(N.sEvP.sem, bE + k - 1)
                            for c in range(DCH):
                                ii = t.matmul(
                                    ps_v[k % 2][:],
                                    lhsT=xyT[:, c * T + i * 128:c * T + (i + 1) * 128],
                                    rhs=rvw[:, c * C:(c + 1) * C],
                                    start=(c == 0), stop=(c == DCH - 1))
                            N.sQKMM.inc(ii)
                            k += 1

                    @block.vector
                    def _(v):
                        k = 0
                        for (w, cc, nt) in qk_tiles:
                            dstt = qtb if w == 0 else ktb
                            v.wait_ge(N.sQKMM.sem, bM + k + 1)
                            ii = v.tensor_copy(
                                out=dstt[:, cc * T + nt * 512:cc * T + (nt + 1) * 512],
                                in_=ps_q[k % 2][:])
                            N.sEvP.inc(ii)
                            k += 1
                        for i in range(NT):
                            v.wait_ge(N.sQKMM.sem, bM + k + 1)
                            ii = v.tensor_copy(out=vtb[:, i * C:(i + 1) * C],
                                               in_=ps_v[k % 2][:])
                            N.sEvP.inc(ii)
                            k += 1

                fence()

                with ExitStack() as s1:
                    SDEP = SDEP_A
                    ps_sc = [pst(s1, "ps_sc", [128, 512]) for _ in range(SDEP)]
                    ps_rt = [pst(s1, "ps_rt", [128, 512]) for _ in range(2)]
                    bSc, bSd, bAvR, bRet, bEvR = (N.sSc.n, N.sSd.n, N.sAvR.n,
                                                  N.sRet.n, N.sEvR.n)
                    jlists = [list(range(4 * j, NT)) for j in range(NQ)]

                    @block.tensor
                    def _(t):
                        g = 0
                        for j in range(NQ):
                            il = jlists[j]
                            for ii_, i in enumerate(il):
                                if g >= SDEP:
                                    t.wait_ge(N.sSd.sem, bSd + g - SDEP + 1)
                                for cc in range(2):
                                    mm = t.matmul(
                                        ps_sc[g % SDEP][:],
                                        lhsT=ktb[:, cc * T + i * 128:cc * T + (i + 1) * 128],
                                        rhs=qtb[:, cc * T + j * 512:cc * T + (j + 1) * 512],
                                        start=(cc == 0), stop=(cc == 1))
                                N.sSc.inc(mm)
                                if ii_ >= 1:
                                    if ii_ == 1 and j >= 1:
                                        t.wait_ge(N.sEvR.sem, bEvR + 2 * j)
                                    t.wait_ge(N.sSd.sem, bSd + g)
                                    pi = il[ii_ - 1]
                                    for cc in range(2):
                                        mm = t.matmul(
                                            ps_rt[cc][:],
                                            lhsT=vtb[:, pi * C + cc * 128:pi * C + (cc + 1) * 128],
                                            rhs=sdb[:, ((g - 1) % SDEP) * 512:((g - 1) % SDEP + 1) * 512],
                                            start=(ii_ == 1), stop=False)
                                    N.sAvR.inc(mm)
                                g += 1
                            t.wait_ge(N.sSd.sem, bSd + g)
                            pi = il[-1]
                            if len(il) == 1 and j >= 1:
                                t.wait_ge(N.sEvR.sem, bEvR + 2 * j)
                            for cc in range(2):
                                mm = t.matmul(
                                    ps_rt[cc][:],
                                    lhsT=vtb[:, pi * C + cc * 128:pi * C + (cc + 1) * 128],
                                    rhs=sdb[:, ((g - 1) % SDEP) * 512:((g - 1) % SDEP + 1) * 512],
                                    start=(len(il) == 1), stop=True)
                            N.sAvR.inc(mm)

                    @block.vector
                    def _(v):
                        g = 0
                        for j in range(NQ):
                            il = jlists[j]
                            for ii_, i in enumerate(il):
                                d = i - 4 * j
                                v.wait_ge(N.sSc.sem, bSc + g + 1)
                                if g >= SDEP:
                                    v.wait_ge(N.sAvR.sem, bAvR + g - SDEP + 1)
                                ii = v.tensor_tensor(
                                    out=sdb[:, (g % SDEP) * 512:(g % SDEP + 1) * 512],
                                    in0=ps_sc[g % SDEP][:],
                                    in1=dconst[:, d * 512:(d + 1) * 512], op=ALU.mult)
                                N.sSd.inc(ii)
                                g += 1
                            cum = sum(len(jlists[jj]) for jj in range(j + 1))
                            v.wait_ge(N.sAvR.sem, bAvR + cum)
                            for cc in range(2):
                                ii = v.tensor_copy(
                                    out=rtb[:, cc * T + j * 512:cc * T + (j + 1) * 512],
                                    in_=ps_rt[cc][:])
                                N.sEvR.inc(ii)

                fence()

                if taps and seq == 0:
                    @block.gpsimd
                    def _(g):
                        for sl, (src_, c0) in enumerate([(qtb, 0), (ktb, 0),
                                                         (vtb, 0), (rtb, 0)]):
                            ii = g.dma_start(out=dbg_b[3 + sl, :, :],
                                             in_=src_[:, c0:c0 + 512])
                            N.dG.inc(ii, 16)
                        g.wait_ge(N.dG.sem, N.dG.n)
                    fence()

                with ExitStack() as s1:
                    xrs = sb(s1, "xrs", [128, 2 * dm], F32)
                    dbgh = sb(s1, "dbgh", [128, dm + 16], F32) if (taps and seq == 0) else None
                    h1b = sb(s1, "h1b", [128, 2 * dm], F32)
                    sqb = sb(s1, "sqb", [128, dm], F32)
                    yst = sb(s1, "yst", [128, 2 * dm], F32)
                    ybb = sb(s1, "ybb", [128, 2 * dm], BF16)
                    ps_ao = [pst(s1, "ps_ao", [128, 1024]) for _ in range(2)]
                    ps_yt = pst(s1, "ps_yt", [128, 2 * 1024], BF16)
                    bSy, bG = N.dSy.n, N.dG.n
                    bAo, bH1, bVar, bRstd = N.sAoMM.n, N.sH1.n, N.sVar.n, N.sRstd.n
                    bY, bYb, bTrY, bEvY = N.sY.n, N.sYb.n, N.sTrY.n, N.sEvY.n
                    bSqA = N.sSq.n

                    @block.sync
                    def _(sy):
                        for i in range(NT):
                            if i >= 2:
                                sy.wait_ge(N.sH1.sem, bH1 + i - 1)
                            ii = sy.dma_start(out=xrs[:, (i % 2) * dm:(i % 2 + 1) * dm],
                                              in_=xin[seq, i * 128:(i + 1) * 128, :])
                            N.dSy.inc(ii, 16)

                    @block.tensor
                    def _(t):
                        for i in range(NT):
                            if i >= 2:
                                t.wait_ge(N.sH1.sem, bH1 + i - 1)
                            for nh in range(2):
                                for cc in range(2):
                                    mm = t.matmul(
                                        ps_ao[i % 2][:, nh * 512:(nh + 1) * 512],
                                        lhsT=rtb[:, cc * T + i * 128:cc * T + (i + 1) * 128],
                                        rhs=rowT[:, cc * dm + nh * 512:cc * dm + (nh + 1) * 512],
                                        start=(cc == 0), stop=(cc == 1))
                            N.sAoMM.inc(mm)
                            if i >= 1:
                                j = i - 1
                                t.wait_ge(N.sYb.sem, bYb + j + 1)
                                if j >= 2:
                                    t.wait_ge(N.sEvY.sem, bEvY + j - 1)
                                for c in range(DCH):
                                    mm = t.transpose(
                                        ps_yt[:, (j % 2) * 1024 + c * 128:
                                              (j % 2) * 1024 + (c + 1) * 128],
                                        ybb[:, (j % 2) * dm + c * 128:(j % 2) * dm + (c + 1) * 128],
                                        ident[:])
                                N.sTrY.inc(mm)
                        j = NT - 1
                        t.wait_ge(N.sYb.sem, bYb + NT)
                        if j >= 2:
                            t.wait_ge(N.sEvY.sem, bEvY + j - 1)
                        for c in range(DCH):
                            mm = t.transpose(
                                ps_yt[:, (j % 2) * 1024 + c * 128:(j % 2) * 1024 + (c + 1) * 128],
                                ybb[:, (j % 2) * dm + c * 128:(j % 2) * dm + (c + 1) * 128],
                                ident[:])
                        N.sTrY.inc(mm)

                    @block.vector
                    def _(v):
                        def emit_y(j):
                            pj = j % 2
                            cb = pj * 8
                            v.wait_ge(N.sRstd.sem, bRstd + j + 1)
                            if j >= 2:
                                v.wait_ge(N.dG.sem, bG + 16 * (j - 1))
                            ii = v.tensor_scalar(
                                out=yst[:, pj * dm:(pj + 1) * dm],
                                in0=h1b[:, pj * dm:(pj + 1) * dm],
                                scalar1=lncols[:, cb + 0:cb + 1],
                                scalar2=lncols[:, cb + 5:cb + 6],
                                op0=ALU.subtract, op1=ALU.mult)
                            N.sY.inc(ii)
                            ii = v.tensor_copy(out=ybb[:, pj * dm:(pj + 1) * dm],
                                               in_=yst[:, pj * dm:(pj + 1) * dm])
                            N.sYb.inc(ii)
                            if j >= 1:
                                jj = j - 1
                                v.wait_ge(N.sTrY.sem, bTrY + jj + 1)
                                dst = xyT[:].rearrange("p (c n) -> p c n", c=DCH)[
                                    :, :, jj * 128:(jj + 1) * 128]
                                src = ps_yt[:, (jj % 2) * 1024:(jj % 2 + 1) * 1024
                                            ].rearrange("p (c n) -> p c n", c=DCH)
                                ii = v.tensor_copy(out=dst, in_=src)
                                N.sEvY.inc(ii)

                        for i in range(NT):
                            pc = i % 2
                            cb = pc * 8
                            v.wait_ge(N.sAoMM.sem, bAo + i + 1)
                            v.wait_ge(N.dSy.sem, bSy + 16 * (i + 1))
                            ii = v.scalar_tensor_tensor(
                                out=h1b[:, pc * dm:(pc + 1) * dm], in0=ps_ao[pc][:],
                                scalar=1.0, in1=xrs[:, pc * dm:(pc + 1) * dm],
                                op0=ALU.mult, op1=ALU.add,
                                accum_out=lncols[:, cb + 3:cb + 4])
                            N.sH1.inc(ii)
                            v.drain()
                            v.tensor_scalar(out=lncols[:, cb + 0:cb + 1],
                                            in0=lncols[:, cb + 3:cb + 4],
                                            scalar1=1.0 / dm, scalar2=None, op0=ALU.mult)
                            v.drain()
                            v.tensor_tensor(out=lncols[:, cb + 1:cb + 2],
                                            in0=lncols[:, cb + 0:cb + 1],
                                            in1=lncols[:, cb + 0:cb + 1], op=ALU.mult)
                            v.wait_ge(N.sSq.sem, bSqA + i + 1)
                            v.tensor_scalar(out=lncols[:, cb + 2:cb + 3],
                                            in0=lncols[:, cb + 4:cb + 5],
                                            scalar1=1.0 / dm, scalar2=None, op0=ALU.mult)
                            v.drain()
                            ii = v.tensor_tensor(out=lncols[:, cb + 2:cb + 3],
                                                 in0=lncols[:, cb + 2:cb + 3],
                                                 in1=lncols[:, cb + 1:cb + 2],
                                                 op=ALU.subtract)
                            N.sVar.inc(ii)
                            if dbgh is not None and i == 0:
                                v.tensor_copy(out=dbgh[:, 0:dm], in_=h1b[:, 0:dm])
                                ii = v.tensor_copy(out=dbgh[:, dm:dm + 16], in_=lncols[:])
                                N.sEvB.inc(ii)
                            if i >= 1:
                                emit_y(i - 1)
                        emit_y(NT - 1)
                        j = NT - 1
                        v.wait_ge(N.sTrY.sem, bTrY + NT)
                        dst = xyT[:].rearrange("p (c n) -> p c n", c=DCH)[
                            :, :, j * 128:(j + 1) * 128]
                        src = ps_yt[:, (j % 2) * 1024:(j % 2 + 1) * 1024
                                    ].rearrange("p (c n) -> p c n", c=DCH)
                        ii = v.tensor_copy(out=dst, in_=src)
                        N.sEvY.inc(ii)

                    @block.scalar
                    def _(s):
                        for i in range(NT):
                            cb = (i % 2) * 8
                            s.wait_ge(N.sH1.sem, bH1 + i + 1)
                            ii = s.activation(out=sqb[:],
                                              in_=h1b[:, (i % 2) * dm:(i % 2 + 1) * dm],
                                              func=AF.Square,
                                              accum_out=lncols[:, cb + 4:cb + 5])
                            N.sSq.inc(ii)
                            s.wait_ge(N.sVar.sem, bVar + i + 1)
                            s.activation(out=lncols[:, cb + 6:cb + 7],
                                         in_=lncols[:, cb + 2:cb + 3],
                                         func=AF.Ln, bias=cbias[:, 0:1])
                            ii = s.activation(out=lncols[:, cb + 5:cb + 6],
                                              in_=lncols[:, cb + 6:cb + 7],
                                              func=AF.Exp, scale=-0.5)
                            N.sRstd.inc(ii)

                    @block.gpsimd
                    def _(g):
                        for i in range(NT):
                            g.wait_ge(N.sY.sem, bY + i + 1)
                            ii = g.dma_start(out=sc_y[seq, i * 128:(i + 1) * 128, :],
                                             in_=yst[:, (i % 2) * dm:(i % 2 + 1) * dm])
                            N.dG.inc(ii, 16)
                        if dbgh is not None:
                            g.wait_ge(N.sEvB.sem, N.sEvB.n)
                            ii = g.dma_start(out=dbg_h[:, :], in_=dbgh[:])
                            N.dG.inc(ii, 16)
                            g.wait_ge(N.dG.sem, N.dG.n)

                fence()

                with ExitStack() as s2:
                    prod = sb(s2, "prod", [128, NT * 1040], BF16)
                    wst = sb(s2, "wst", [128, 2 * 1024], F32)
                    wbb = sb(s2, "wbb", [128, DCH * 1040], BF16)
                    ps_pA = pst(s2, "ps_pA", [128, 512])
                    ps_pB = pst(s2, "ps_pB", [128, 512])
                    ps_p16 = pst(s2, "ps_p16", [128, 16])

                    def emit_part(part, seq=seq, prod=prod, wst=wst, wbb=wbb,
                                  stage=0):
                        if True:
                            if part < 2:
                                ps_p = [ps_pA, ps_pB]
                            else:
                                ps_p = [ps_pA, ps_pB, ps_p16]
                            if stage != 2:
                                emit_part.bases = (N.dSy.n, N.sWcast.n, N.sQKMM.n,
                                                   N.sEvP.n, N.dG.n, N.sDotMM.n)
                            bSy, bW, bM, bE, bG0, bDot0 = emit_part.bases

                            if stage == 2:
                                pass
                            else:
                              @block.sync
                              def _(sy):
                                for c in range(DCH):
                                    if c >= 2:
                                        sy.wait_ge(N.sWcast.sem, bW + c - 1)
                                    elif part > 0:
                                        sy.wait_ge(N.sWcast.sem, bW)
                                    ii = sy.dma_start(
                                        out=wst[:, (c % 2) * 1024:(c % 2 + 1) * 1024],
                                        in_=wqkv[c * 128:(c + 1) * 128,
                                                 part * dm:(part + 1) * dm])
                                    N.dSy.inc(ii, 16)

                            if stage == 2:
                                pass
                            else:
                              @block.vector
                              def _(v):
                                v.wait_ge(N.sQKMM.sem, bM)
                                for c in range(DCH):
                                    v.wait_ge(N.dSy.sem, bSy + 16 * (c + 1))
                                    if part < 2:
                                        v.memset(wbb[:, c * 1040 + 1024:(c + 1) * 1040], 0.0)
                                        ii = v.tensor_copy(
                                            out=wbb[:, c * 1040:c * 1040 + 1024],
                                            in_=wst[:, (c % 2) * 1024:(c % 2 + 1) * 1024])
                                    else:
                                        vv = wbb[:, c * 1040:(c + 1) * 1040].rearrange(
                                            "p (h j) -> p h j", j=65)[:, :, 64:65]
                                        v.memset(vv, 0.0)
                                        for h in range(H):
                                            ii = v.tensor_copy(
                                                out=wbb[:, c * 1040 + h * 65:
                                                        c * 1040 + h * 65 + 64],
                                                in_=wst[:, (c % 2) * 1024 + h * 64:
                                                        (c % 2) * 1024 + (h + 1) * 64])
                                    N.sWcast.inc(ii)

                            if stage == 1:
                                return
                            @block.tensor
                            def _(t):
                                t.wait_ge(N.sWcast.sem, bW + DCH)
                                t.wait_ge(N.sEvP.sem, bE)
                                if part < 2:
                                    k = 0
                                    for mc in range(DCH):
                                        for nt in range(NQ):
                                            if k >= 2:
                                                t.wait_ge(N.sEvP.sem, bE + k - 1)
                                            for kc in range(DCH):
                                                mm = t.matmul(
                                                    ps_p[k % 2][:],
                                                    lhsT=wbb[:, kc * 1040 + mc * 128:
                                                             kc * 1040 + (mc + 1) * 128],
                                                    rhs=xyT[:, kc * T + nt * 512:
                                                            kc * T + (nt + 1) * 512],
                                                    start=(kc == 0), stop=(kc == DCH - 1))
                                            N.sQKMM.inc(mm)
                                            k += 1
                                else:
                                    for i in range(NT):
                                        if i >= 1:
                                            t.wait_ge(N.sEvP.sem, bE + i)
                                        for kc in range(DCH):
                                            for sl, (n0, n1) in enumerate(
                                                    [(0, 512), (512, 1024), (1024, 1040)]):
                                                mm = t.matmul(
                                                    ps_p[sl][:, 0:n1 - n0],
                                                    lhsT=xyT[:, kc * T + i * 128:
                                                             kc * T + (i + 1) * 128],
                                                    rhs=wbb[:, kc * 1040 + n0:kc * 1040 + n1],
                                                    start=(kc == 0), stop=(kc == DCH - 1))
                                        N.sQKMM.inc(mm)

                            @block.vector
                            def _(v):
                                if part >= 1:
                                    v.wait_ge(N.dG.sem, bG0)
                                    v.wait_ge(N.sDotMM.sem, bDot0)
                                if part < 2:
                                    k = 0
                                    for mc in range(DCH):
                                        for nt in range(NQ):
                                            v.wait_ge(N.sQKMM.sem, bM + k + 1)
                                            if part == 0:
                                                ii = v.tensor_scalar(
                                                    out=prod[:, mc * T + nt * 512:
                                                             mc * T + (nt + 1) * 512],
                                                    in0=ps_p[k % 2][:],
                                                    scalar1=scale, scalar2=None,
                                                    op0=ALU.mult)
                                            else:
                                                ii = v.tensor_copy(
                                                    out=prod[:, mc * T + nt * 512:
                                                             mc * T + (nt + 1) * 512],
                                                    in_=ps_p[k % 2][:])
                                            N.sEvP.inc(ii)
                                            k += 1
                                else:
                                    for i in range(NT):
                                        v.wait_ge(N.sQKMM.sem, bM + i + 1)
                                        for sl, (n0, n1) in enumerate(
                                                [(0, 512), (512, 1024), (1024, 1040)]):
                                            v.tensor_copy(
                                                out=prod[:, i * 1040 + n0:i * 1040 + n1],
                                                in_=ps_p[sl][:, 0:n1 - n0])
                                        vv = prod[:, i * 1040:(i + 1) * 1040].rearrange(
                                            "p (h j) -> p h j", j=65)[:, :, 64:65]
                                        ii = v.memset(vv, 1.0)
                                        N.sEvP.inc(ii)

                            @block.gpsimd
                            def _(g):
                                if part < 2:
                                    dstd = sc_q if part == 0 else sc_k
                                    for mc in range(DCH):
                                        g.wait_ge(N.sEvP.sem, bE + (mc + 1) * NQ)
                                        ii = g.dma_start(out=dstd[seq, mc, :, :],
                                                         in_=prod[:, mc * T:(mc + 1) * T])
                                        N.dG.inc(ii, 16)
                                else:
                                    for i in range(NT):
                                        g.wait_ge(N.sEvP.sem, bE + i + 1)
                                        ii = g.dma_start(out=sc_v[seq, i, :, :],
                                                         in_=prod[:, i * 1040:(i + 1) * 1040])
                                        N.dG.inc(ii, 16)
                                g.wait_ge(N.dG.sem, N.dG.n)

                    emit_part(0)
                    emit_part(1, stage=1)

                    with ExitStack() as s3:
                        sq2 = sb(s3, "sq2", [128, 2 * T], BF16)
                        dots = sb(s3, "dots", [16, T], F32)
                        n2q = sb(s3, "n2q", [16, T], F32)
                        rsq_t = sb(s3, "rsq", [16, T], F32)
                        rsq = rsq_t[:, :]
                        ps_sim = [pst(s3, "ps_sim", [16, 512])
                                  for _ in range(T // 512)]
                        ps_mean = pst(s3, "ps_mean", [1, 512])
                        bSq, bN2, bDot, bEvB, bRsq = (N.sSq.n, N.sN2MM.n, N.sDotMM.n,
                                                      N.sEvB.n, N.sRsq.n)
                        bEvP0 = N.sEvP.n

                        @block.vector
                        def _(v):
                            for mc in range(DCH):
                                v.wait_ge(N.sEvP.sem, bEvP0 - DCH * NQ + (mc + 1) * NQ)
                                if mc >= 2:
                                    v.wait_ge(N.sN2MM.sem, bN2 + mc - 1)
                                ii = v.tensor_tensor(
                                    out=sq2[:, (mc % 2) * T:(mc % 2 + 1) * T],
                                    in0=prod[:, mc * T:(mc + 1) * T],
                                    in1=prod[:, mc * T:(mc + 1) * T], op=ALU.mult)
                                N.sSq.inc(ii)

                        @block.tensor
                        def _(t):
                            for mc in range(DCH):
                                t.wait_ge(N.sSq.sem, bSq + mc + 1)
                                for nsl in range(T // 512):
                                    mm = t.matmul(
                                        ps_sim[nsl][:],
                                        lhsT=obd[:, mc * 16:(mc + 1) * 16],
                                        rhs=sq2[:, (mc % 2) * T + nsl * 512:
                                                (mc % 2) * T + (nsl + 1) * 512],
                                        start=(mc == 0), stop=(mc == DCH - 1))
                                N.sN2MM.inc(mm)

                        @block.vector
                        def _(v):
                            v.wait_ge(N.sN2MM.sem, bN2 + DCH)
                            for nsl in range(T // 512):
                                ii = v.tensor_copy(
                                    out=n2q[:, nsl * 512:(nsl + 1) * 512],
                                    in_=ps_sim[nsl][:])
                            N.sEvB.inc(ii)

                        @block.tensor
                        def _(t):
                            t.wait_ge(N.sEvB.sem, bEvB + 1)
                            for mc in range(DCH):
                                for nsl in range(T // 512):
                                    mm = t.matmul(
                                        ps_sim[nsl][:],
                                        lhsT=gbd[:, mc * 16:(mc + 1) * 16],
                                        rhs=prod[:, mc * T + nsl * 512:mc * T + (nsl + 1) * 512],
                                        start=(mc == 0), stop=(mc == DCH - 1))
                            N.sDotMM.inc(mm)

                        @block.scalar
                        def _(s):
                            s.wait_ge(N.sEvB.sem, bEvB + 1)
                            s.activation(out=rsq, in_=n2q[:], func=AF.Ln)
                            ii = s.activation(out=rsq, in_=rsq, func=AF.Exp,
                                              scale=-0.5)
                            N.sRsq.inc(ii)

                        @block.vector
                        def _(v):
                            v.wait_ge(N.sDotMM.sem, bDot + 1)
                            v.wait_ge(N.sRsq.sem, bRsq + 1)
                            for nsl in range(T // 512):
                                v.tensor_copy(out=dots[:, nsl * 512:(nsl + 1) * 512],
                                              in_=ps_sim[nsl][:])
                            ii = v.tensor_tensor(out=dots[:], in0=dots[:], in1=rsq,
                                                 op=ALU.mult)
                            N.sEvB.inc(ii)

                        @block.tensor
                        def _(t):
                            t.wait_ge(N.sEvB.sem, bEvB + 2)
                            for nsl in range(T // 512):
                                if nsl >= 1:
                                    t.wait_ge(N.sEvB.sem, bEvB + 2 + nsl)
                                mm = t.matmul(ps_mean[:], lhsT=ones16f[:],
                                              rhs=dots[:, nsl * 512:(nsl + 1) * 512],
                                              start=True, stop=True)
                                N.sDotMM.inc(mm)

                        @block.vector
                        def _(v):
                            b0 = 0 if seq == 0 else 32
                            simrow = arenaF[b0:b0 + 1, :]
                            for nsl in range(T // 512):
                                v.wait_ge(N.sDotMM.sem, bDot + 2 + nsl)
                                ii = v.tensor_copy(
                                    out=simrow[:, nsl * 512:(nsl + 1) * 512],
                                    in_=ps_mean[:])
                                N.sEvB.inc(ii)
                            v.tensor_reduce(out=scal[0:1, 4 + 2 * seq:5 + 2 * seq],
                                            in_=simrow[:, REG:T], axis=AX.X, op=ALU.max)
                            ii = v.tensor_reduce(out=scal[0:1, 5 + 2 * seq:6 + 2 * seq],
                                                 in_=simrow[:, REG:T], axis=AX.X,
                                                 op=ALU.min)
                            N.sEvB.inc(ii)

                        if taps:
                            @block.gpsimd
                            def _(g):
                                g.wait_ge(N.sEvB.sem, N.sEvB.n)
                                b0 = 0 if seq == 0 else 32
                                ii = g.dma_start(out=sc_sim[seq, :, :],
                                                 in_=arenaF[b0:b0 + 1, :])
                                N.dG.inc(ii, 16)

                    emit_part(1, stage=2)
                    emit_part(2)
                    fence()

        tmpbm = sb(octx, "tmpbm", [1, 16], F32)

        def emit_mid_trigger():
            tmpb = tmpbm

            @block.vector
            def _(v):
                v.drain()
                if nseq == 2:
                    v.tensor_tensor(out=tmpb[0:1, 1:2], in0=scal[0:1, 4:5],
                                    in1=scal[0:1, 6:7], op=ALU.max)
                    v.tensor_tensor(out=tmpb[0:1, 2:3], in0=scal[0:1, 5:6],
                                    in1=scal[0:1, 7:8], op=ALU.min)
                else:
                    v.tensor_copy(out=tmpb[0:1, 1:2], in_=scal[0:1, 4:5])
                    v.tensor_copy(out=tmpb[0:1, 2:3], in_=scal[0:1, 5:6])
                v.drain()
                ii = v.tensor_scalar(out=tmpb[0:1, 0:1], in0=tmpb[0:1, 2:3],
                                     scalar1=-1.0, scalar2=None, op0=ALU.mult)
                N.sMid.inc(ii)

            @block.gpsimd
            def _(g):
                g.wait_ge(N.sMid.sem, N.sMid.n)
                ii = g.dma_start(out=cc_in[:, :], in_=tmpb[0:1, 0:2])
                N.dG.inc(ii, 16)
                g.wait_ge(N.dG.sem, N.dG.n)
                ii = g.collective_compute(
                    "AllReduce", ALU.max, replica_groups=[list(range(8))],
                    ins=[cc_in.ap()], outs=[cc_out.ap()])
                N.sCol.inc(ii)

        def emit_mid():
            bGmid = N.dG.n
            with ExitStack() as mctx:
                tmpb = tmpbm

                @block.gpsimd
                def _(g):
                    g.wait_ge(N.sCol.sem, N.sCol.n)
                    ii = g.dma_start(out=tmpb[0:1, 4:6], in_=cc_out[:, :])
                    N.dG.inc(ii, 16)

                @block.vector
                def _(v):
                    v.wait_ge(N.dG.sem, N.dG.n)
                    v.tensor_scalar(out=tmpb[0:1, 6:7], in0=tmpb[0:1, 4:5],
                                    scalar1=-0.1, scalar2=None, op0=ALU.mult)
                    v.tensor_scalar(out=tmpb[0:1, 7:8], in0=tmpb[0:1, 5:6],
                                    scalar1=0.9, scalar2=None, op0=ALU.mult)
                    v.drain()
                    v.tensor_tensor(out=scal[0:1, 2:3], in0=tmpb[0:1, 6:7],
                                    in1=tmpb[0:1, 7:8], op=ALU.add)
                    v.drain()
                    for seq in range(nseq):
                        if seq >= 1:
                            v.wait_ge(N.dG.sem, bGmid + 16 + 48 * seq)
                        b0 = 0 if seq == 0 else 32
                        simrow = arenaF[b0:b0 + 1, :]
                        v.tensor_scalar(out=simrow[:, :], in0=simrow[:, :],
                                        scalar1=scal[0:1, 2:3], scalar2=0.0,
                                        op0=ALU.subtract, op1=ALU.is_gt)
                        v.tensor_scalar(out=simrow[:, :], in0=simrow[:, :],
                                        scalar1=2.0, scalar2=-1.0, op0=ALU.mult,
                                        op1=ALU.add)
                        v.memset(simrow[:, 0:REG], 1.0)
                        v.drain()
                        v.tensor_copy(out=arenaB[0:1, 0:T], in_=simrow[:, :])
                        v.tensor_scalar(out=arenaB[0:1, T:TF], in0=simrow[:, 0:NFLIP],
                                        scalar1=-1.0, scalar2=None, op0=ALU.mult)
                        v.tensor_scalar(out=arenaB[32:33, 0:T], in0=simrow[:, :],
                                        scalar1=BIGH, scalar2=None, op0=ALU.mult)
                        v.memset(arenaB[32:33, 0:REG], 0.0)
                        v.memset(arenaB[64:65, 0:T], 0.0)
                        ii = v.memset(arenaB[64:65, 0:REG], BIGH)
                        N.sMid.inc(ii)

                bMid = N.sMid.n - nseq

                @block.gpsimd
                def _(g):
                    for seq in range(nseq):
                        g.wait_ge(N.sMid.sem, bMid + seq + 1)
                        ii = g.dma_start(out=sc_k3[seq, 0:1, :], in_=arenaB[32:33, 0:T])
                        N.dG.inc(ii, 16)
                        ii = g.dma_start(out=sc_k3[seq, 2:3, :], in_=arenaB[64:65, 0:T])
                        N.dG.inc(ii, 16)
                        ii = g.dma_start(out=sc_q2[seq, 0:1, :], in_=arenaB[0:1, 0:TF])
                        N.dG.inc(ii, 16)

                @block.vector
                def _(v):
                    v.wait_ge(N.dG.sem, N.dG.n)
                    ii = v.memset(arenaB[0:1, 0:TF], 1.0)
                    N.sMid.inc(ii)

                @block.gpsimd
                def _(g):
                    g.wait_ge(N.sMid.sem, N.sMid.n)
                    for seq in range(nseq):
                        ii = g.dma_start(out=sc_k3[seq, 1:2, :], in_=arenaB[0:1, 0:T])
                        N.dG.inc(ii, 16)
                        ii = g.dma_start(out=sc_q2[seq, 1:2, :], in_=arenaB[0:1, 0:TF])
                        N.dG.inc(ii, 16)


        bGA = N.dG.n
        emit_mid_trigger()
        fence()

        with ExitStack() as bctx:
            vsb = sb(bctx, "vsb", [128, NT * 1040], BF16)
            oT = sb(bctx, "oT", [128, DCH * T], BF16)
            qch = sb(bctx, "qch", [128, 2 * T], BF16)
            kch = sb(bctx, "kch", [128, 2 * T], BF16)
            qpb = sb(bctx, "qpb", [67, 2 * TF], BF16)
            kpb = sb(bctx, "kpb", [67, 2 * T], BF16)
            ptb = sb(bctx, "ptb", [128, 2 * 1024], BF16)
            osb = sb(bctx, "osb", [64, 512], BF16)
            wpb = sb(bctx, "wpb", [128, DCH * 1024], BF16)
            flb = sb(bctx, "flb", [128, DCH * NFLIP], BF16)
            fltmp = sb(bctx, "fltmp", [64, REG], F32)
            denr = sb(bctx, "denr", [1, 1024], F32)
            recr = None

            with ExitStack() as s0:
                wpl = sb(s0, "wpl", [128, 2 * 1024], F32)
                bSy, bW = N.dSy.n, N.sWcast.n

                @block.sync
                def _(sy):
                    for c in range(DCH):
                        if c >= 2:
                            sy.wait_ge(N.sWcast.sem, bW + c - 1)
                        ii = sy.dma_start(out=wpl[:, (c % 2) * 1024:(c % 2 + 1) * 1024],
                                          in_=wproj[c * 128:(c + 1) * 128, :])
                        N.dSy.inc(ii, 16)

                @block.vector
                def _(v):
                    for c in range(DCH):
                        v.wait_ge(N.dSy.sem, bSy + 16 * (c + 1))
                        ii = v.tensor_copy(out=wpb[:, c * 1024:(c + 1) * 1024],
                                           in_=wpl[:, (c % 2) * 1024:(c % 2 + 1) * 1024])
                        N.sWcast.inc(ii)

            fence()

            for seq in range(nseq):
                NQT = NQ + 1
                NG = NT // 2
                QW = [512] * NQ + [NFLIP]
                QOF = [512 * i for i in range(NQ)] + [T]

                with ExitStack() as s1:
                    sqh = sb(s1, "sqh", [128, 2 * T], BF16)
                    kmx = sb(s1, "kmx", [16, 12], F32)
                    mall = sb(s1, "mall", [16, T], F32)
                    mneg = sb(s1, "mneg", [16, TF], BF16)
                    ps_n1 = [pst(s1, "ps_n1", [16, 512]) for _ in range(T // 512)]
                    ps_n2 = [pst(s1, "ps_n2", [16, 512]) for _ in range(T // 512)]
                    bSy, bSq, bN2, bEvB, bRsq, bB = (N.dSy.n, N.sSq.n, N.sN2MM.n,
                                                     N.sEvB.n, N.sRsq.n, N.dB.n)

                    @block.sync
                    def _(sy):
                        sy.wait_ge(N.dG.sem, bGA)
                        sy.wait_ge(N.sQKb.sem, N.sQKb.n)
                        for c in range(DCH):
                            if c >= 2:
                                sy.wait_ge(N.sSq.sem, bSq + c - 1)
                            ii = sy.dma_start(out=qch[:, (c % 2) * T:(c % 2 + 1) * T],
                                              in_=sc_q[seq, c, :, :])
                            N.dSy.inc(ii, 16)
                            ii = sy.dma_start(out=kch[:, (c % 2) * T:(c % 2 + 1) * T],
                                              in_=sc_k[seq, c, :, :])
                            N.dSy.inc(ii, 16)

                    @block.vector
                    def _(v):
                        for c in range(DCH):
                            v.wait_ge(N.dSy.sem, bSy + 32 * (c + 1))
                            if c >= 1:
                                v.wait_ge(N.sN2MM.sem, bN2 + c)
                            v.tensor_tensor(out=sqh[:, 0:T],
                                            in0=qch[:, (c % 2) * T:(c % 2 + 1) * T],
                                            in1=qch[:, (c % 2) * T:(c % 2 + 1) * T],
                                            op=ALU.mult)
                            ii = v.tensor_tensor(out=sqh[:, T:2 * T],
                                                 in0=kch[:, (c % 2) * T:(c % 2 + 1) * T],
                                                 in1=kch[:, (c % 2) * T:(c % 2 + 1) * T],
                                                 op=ALU.mult)
                            N.sSq.inc(ii)

                    @block.tensor
                    def _(t):
                        for c in range(DCH):
                            t.wait_ge(N.sSq.sem, bSq + c + 1)
                            for w in range(2):
                                pdst = ps_n1 if w == 0 else ps_n2
                                for nsl in range(T // 512):
                                    mm = t.matmul(
                                        pdst[nsl][:],
                                        lhsT=obd[:, c * 16:(c + 1) * 16],
                                        rhs=sqh[:, w * T + nsl * 512:w * T + (nsl + 1) * 512],
                                        start=(c == 0), stop=(c == DCH - 1))
                            N.sN2MM.inc(mm)

                    @block.vector
                    def _(v):
                        v.wait_ge(N.sN2MM.sem, bN2 + DCH)
                        for nsl in range(T // 512):
                            v.tensor_reduce(out=kmx[:, 4 + nsl:5 + nsl],
                                            in_=ps_n2[nsl][:], axis=AX.X, op=ALU.max)
                        v.drain()
                        ii = v.tensor_reduce(out=kmx[:, 0:1],
                                             in_=kmx[:, 4:4 + T // 512], axis=AX.X,
                                             op=ALU.max)
                        N.sEvB.inc(ii)

                    @block.scalar
                    def _(s):
                        s.wait_ge(N.sEvB.sem, bEvB + 1)
                        ii = s.activation(out=kmx[:, 1:2], in_=kmx[:, 0:1], func=AF.Ln)
                        N.sRsq.inc(ii)

                    @block.vector
                    def _(v):
                        v.wait_ge(N.sRsq.sem, bRsq + 1)
                        ii = v.tensor_scalar(out=kmx[:, 2:3], in0=kmx[:, 1:2],
                                             scalar1=0.5, scalar2=None, op0=ALU.mult)
                        N.sEvB.inc(ii)

                    @block.scalar
                    def _(s):
                        s.wait_ge(N.sEvB.sem, bEvB + 2)
                        for nsl in range(T // 512):
                            s.activation(out=mall[:, nsl * 512:(nsl + 1) * 512],
                                         in_=ps_n1[nsl][:], func=AF.Ln)
                        ii = s.activation(out=mall[:, 0:T], in_=mall[:, 0:T],
                                          func=AF.Exp, scale=0.5, bias=kmx[:, 2:3])
                        N.sRsq.inc(ii)

                    @block.vector
                    def _(v):
                        v.wait_ge(N.sRsq.sem, bRsq + 2)
                        v.tensor_scalar(out=mneg[:, 0:T], in0=mall[:, 0:T],
                                        scalar1=-1.0, scalar2=None, op0=ALU.mult)
                        ii = v.tensor_scalar(out=mneg[:, T:TF], in0=mall[:, 0:NFLIP],
                                             scalar1=-1.0, scalar2=None, op0=ALU.mult)
                        N.sEvB.inc(ii)

                    @block.gpsimd
                    def _(g):
                        g.wait_ge(N.sEvB.sem, N.sEvB.n)
                        for h in range(H):
                            ii = g.dma_start(out=sc_m[seq, h, :], in_=mneg[h:h + 1, :])
                            N.dB.inc(ii, 16)

                if seq == 0:
                    emit_mid()
                fence()

                with ExitStack() as s1:
                    ps_log = [pst(s1, "ps_log", [128, 1024]) for _ in range(2)]
                    ps_av = [pst(s1, "ps_av", [65, 512]) for _ in range(2)]
                    ps_bcp = pst(s1, "ps_bcp", [64, 512])
                    bLog, bExp, bAv, bDen, bBc, bNorm, bQKb = (
                        N.sLog.n, N.sExp.n, N.sAv.n, N.sDen.n, N.sBc.n, N.sNorm.n,
                        N.sQKb.n)
                    bSy, bB = N.dSy.n, N.dB.n
                    bSqB = N.sSq.n

                    def idx(h, qt):
                        return h * NQT + qt

                    NGq = [NG] * NQ + [1]
                    GPH = sum(NGq)
                    G0T = {}
                    acc = 0
                    for hh in range(H):
                        for qq in range(NQT):
                            G0T[(hh, qq)] = acc
                            acc += NGq[qq]
                    NGG_ALL = acc

                    def gofq(h, qt):
                        return G0T[(h, qt)]

                    @block.sync
                    def _(sy):
                        sy.wait_ge(N.dG.sem, N.dG.n)
                        sy.wait_ge(N.sAv.sem, bAv)
                        sy.wait_ge(N.sLog.sem, bLog)
                        for i in range(NT):
                            ii = sy.dma_start(out=vsb[:, i * 1040:(i + 1) * 1040],
                                              in_=sc_v[seq, i, :, :])
                            N.dSy.inc(ii, 16)
                        for par in range(2):
                            ii = sy.dma_start(out=kpb[64:67, par * T:(par + 1) * T],
                                              in_=sc_k3[seq, :, :])
                            N.dSy.inc(ii, 16)
                            ii = sy.dma_start(out=qpb[64:65, par * TF:(par + 1) * TF],
                                              in_=sc_q2[seq, 0:1, :])
                            N.dSy.inc(ii, 16)
                            ii = sy.dma_start(out=qpb[66:67, par * TF:(par + 1) * TF],
                                              in_=sc_q2[seq, 1:2, :])
                            N.dSy.inc(ii, 16)
                        for c in range(DCH):
                            if c >= 2:
                                sy.wait_ge(N.sQKb.sem, bQKb + 2 * (c - 1))
                            else:
                                sy.wait_ge(N.sSq.sem, bSqB)
                            ii = sy.dma_start(out=qch[:, (c % 2) * T:(c % 2 + 1) * T],
                                              in_=sc_q[seq, c, :, :])
                            N.dSy.inc(ii, 16)
                            ii = sy.dma_start(out=kch[:, (c % 2) * T:(c % 2 + 1) * T],
                                              in_=sc_k[seq, c, :, :])
                            N.dSy.inc(ii, 16)

                    @block.gpsimd
                    def _(g):
                        g.wait_ge(N.dB.sem, bB)
                        for h in range(H):
                            if h >= 2:
                                g.wait_ge(N.sLog.sem, bLog + (h - 1) * GPH)
                            else:
                                g.wait_ge(N.sLog.sem, bLog)
                            ii = g.dma_start(
                                out=qpb[65:66, (h % 2) * TF:(h % 2 + 1) * TF],
                                in_=sc_m[seq, h, :])
                            N.dB.inc(ii, 16)

                    def emit_build(v, hb):
                        c = hb // 2
                        hw = (hb % 2) * 64
                        v.wait_ge(N.dSy.sem, bSy + 16 * (NT + 6 + 2 * (c + 1)))
                        if hb >= 1:
                            v.wait_ge(N.sAv.sem, bAv + (hb - 1) * NQT)
                        v.tensor_copy(
                            out=qpb[0:64, (hb % 2) * TF:(hb % 2) * TF + T],
                            in_=qch[hw:hw + 64, (c % 2) * T:(c % 2 + 1) * T])
                        v.tensor_copy(
                            out=qpb[0:64, (hb % 2) * TF + T:(hb % 2 + 1) * TF],
                            in_=qch[hw:hw + 64, (c % 2) * T:(c % 2) * T + NFLIP])
                        ii = v.tensor_copy(
                            out=kpb[0:64, (hb % 2) * T:(hb % 2 + 1) * T],
                            in_=kch[hw:hw + 64, (c % 2) * T:(c % 2 + 1) * T])
                        N.sQKb.inc(ii)

                    @block.vector
                    def _(v):
                        emit_build(v, 0)
                        for h in range(H):
                            hw = (h % 2) * 64
                            if h + 1 < H:
                                emit_build(v, h + 1)
                            for qt in range(NQT):
                                ix = idx(h, qt)
                                v.wait_ge(N.sAv.sem, bAv + ix + 1)
                                v.tensor_copy(
                                    out=denr[0:1, 0:QW[qt]],
                                    in_=ps_av[ix % 2][64:65, 0:QW[qt]])
                                if ix >= 1:
                                    v.wait_ge(N.sBc.sem, bBc + ix)
                                v.drain()
                                ii = v.reciprocal(out=denr[0:1, 512:512 + QW[qt]],
                                                  in_=denr[0:1, 0:QW[qt]])
                                N.sDen.inc(ii)
                                v.wait_ge(N.sBc.sem, bBc + ix + 1)
                                v.tensor_copy(
                                    out=osb[:, 0:QW[qt]],
                                    in_=ps_av[ix % 2][0:64, 0:QW[qt]])
                                v.drain()
                                if qt < NQT - 1:
                                    dst = oT[hw:hw + 64,
                                             (h // 2) * T + QOF[qt]:(h // 2) * T + QOF[qt] + QW[qt]]
                                else:
                                    dst = flb[hw:hw + 64,
                                              (h // 2) * NFLIP:(h // 2 + 1) * NFLIP]
                                ii = v.tensor_tensor(out=dst, in0=osb[:, 0:QW[qt]],
                                                     in1=ps_bcp[:, 0:QW[qt]],
                                                     op=ALU.mult)
                                N.sNorm.inc(ii)
                            oslc = oT[hw:hw + 64, (h // 2) * T:(h // 2) * T + REG]
                            fslc = flb[hw:hw + 64, (h // 2) * NFLIP:(h // 2) * NFLIP + REG]
                            v.drain()
                            v.tensor_tensor(out=fltmp[:, :], in0=oslc, in1=fslc,
                                            op=ALU.add)
                            v.drain()
                            v.tensor_scalar(out=oslc, in0=fltmp[:, :], scalar1=0.5,
                                            scalar2=None, op0=ALU.mult)

                    @block.tensor
                    def _(t):
                        NGG = NGG_ALL
                        DECT = {}
                        for hh in range(H):
                            for qq in range(NQT):
                                for g_ in range(NGq[qq]):
                                    DECT[G0T[(hh, qq)] + g_] = (hh, qq, g_)

                        def emit_av(t, gg):
                            hh, qq, g = dec_ = DECT[gg]
                            ixx = idx(hh, qq)
                            if g == 0 and ixx >= 2:
                                t.wait_ge(N.sNorm.sem, bNorm + ixx - 1)
                            t.wait_ge(N.sExp.sem, bExp + gg + 1)
                            if qq < NQ:
                                for half in range(2):
                                    kt = 2 * g + half
                                    mm = t.matmul(
                                        ps_av[ixx % 2][:, 0:QW[qq]],
                                        lhsT=vsb[:, kt * 1040 + hh * 65:
                                                 kt * 1040 + (hh + 1) * 65],
                                        rhs=ptb[:, (gg % 2) * 1024 + half * QW[qq]:
                                                (gg % 2) * 1024 + (half + 1) * QW[qq]],
                                        start=(g == 0 and half == 0),
                                        stop=(g == NG - 1 and half == 1))
                                if g == NG - 1:
                                    N.sAv.inc(mm)
                            else:
                                for kt in range(NT):
                                    mm = t.matmul(
                                        ps_av[ixx % 2][:, 0:NFLIP],
                                        lhsT=vsb[:, kt * 1040 + hh * 65:
                                                 kt * 1040 + (hh + 1) * 65],
                                        rhs=ptb[:, (gg % 2) * 1024 + kt * NFLIP:
                                                (gg % 2) * 1024 + (kt + 1) * NFLIP],
                                        start=(kt == 0), stop=(kt == NT - 1))
                                N.sAv.inc(mm)

                        def emit_bcast(t, ixx):
                            t.wait_ge(N.sDen.sem, bDen + ixx + 1)
                            t.wait_ge(N.sNorm.sem, bNorm + max(0, ixx))
                            pqw = QW[ixx % NQT]
                            mm = t.matmul(ps_bcp[:, 0:pqw], lhsT=onesf[0:1, 0:64],
                                          rhs=denr[0:1, 512:512 + pqw],
                                          start=True, stop=True)
                            N.sBc.inc(mm)

                        cslack = min(7, NG - 1)
                        slot_of = {}
                        tail_bcasts = []
                        for j in range(H * NQT):
                            hh, qq = j // NQT, j % NQT
                            gend = G0T[(hh, qq)] + NGq[qq]
                            s_ = gend + cslack
                            if j + 1 < H * NQT:
                                h2_, q2_ = (j + 1) // NQT, (j + 1) % NQT
                                gend2 = G0T[(h2_, q2_)] + NGq[q2_]
                                s_ = min(s_, gend2)
                            if s_ >= NGG:
                                tail_bcasts.append(j)
                            else:
                                slot_of.setdefault(s_, []).append(j)

                        for h in range(H):
                            t.wait_ge(N.sQKb.sem, bQKb + h + 1)
                            t.wait_ge(N.dB.sem, bB + 16 * (h + 1))
                            for qt in range(NQT):
                                ix = idx(h, qt)
                                g0 = gofq(h, qt)
                                for g in range(NGq[qt]):
                                    gg = g0 + g
                                    for j in slot_of.get(gg, []):
                                        emit_bcast(t, j)
                                    if gg >= 2:
                                        t.wait_ge(N.sExp.sem, bExp + gg - 1)
                                    if qt < NQ:
                                        for half in range(2):
                                            kt = 2 * g + half
                                            kk = 67 if kt == 0 else 66
                                            mm = t.matmul(
                                                ps_log[gg % 2][:, half * QW[qt]:
                                                               (half + 1) * QW[qt]],
                                                lhsT=kpb[0:kk, (h % 2) * T + kt * 128:
                                                         (h % 2) * T + (kt + 1) * 128],
                                                rhs=qpb[0:kk, (h % 2) * TF + QOF[qt]:
                                                        (h % 2) * TF + QOF[qt] + QW[qt]],
                                                start=True, stop=True)
                                    else:
                                        for kt in range(NT):
                                            kk = 67 if kt == 0 else 66
                                            mm = t.matmul(
                                                ps_log[gg % 2][:, kt * NFLIP:
                                                               (kt + 1) * NFLIP],
                                                lhsT=kpb[0:kk, (h % 2) * T + kt * 128:
                                                         (h % 2) * T + (kt + 1) * 128],
                                                rhs=qpb[0:kk, (h % 2) * TF + QOF[qt]:
                                                        (h % 2) * TF + QOF[qt] + QW[qt]],
                                                start=True, stop=True)
                                    N.sLog.inc(mm)
                                    if gg >= 1:
                                        emit_av(t, gg - 1)
                        emit_av(t, NGG - 1)
                        for j in tail_bcasts:
                            emit_bcast(t, j)

                    @block.scalar
                    def _(s):
                        for h in range(H):
                            for qt in range(NQT):
                                g0 = gofq(h, qt)
                                for g in range(NGq[qt]):
                                    gg = g0 + g
                                    s.wait_ge(N.sLog.sem, bLog + gg + 1)
                                    w2 = 2 * QW[qt] if qt < NQ else NT * NFLIP
                                    ii = s.activation(
                                        out=ptb[:, (gg % 2) * 1024:(gg % 2) * 1024 + w2],
                                        in_=ps_log[gg % 2][:, 0:w2],
                                        func=AF.Exp, bias=cbias[:, 1:2])
                                    N.sExp.inc(ii)

                fence()

                with ExitStack() as s1:
                    yrl = sb(s1, "yrl", [128, 2 * dm], F32)
                    h2b = sb(s1, "h2b", [128, 2 * dm], F32)
                    sqc = sb(s1, "sqc", [128, dm], F32)
                    zst = sb(s1, "zst", [128, 2 * dm], F32)
                    ps_pj = [pst(s1, "ps_pj", [128, 1024]) for _ in range(2)]
                    bSy, bG = N.dSy.n, N.dG.n
                    bPj, bH2, bVar2, bRstd2, bZ = (N.sPjMM.n, N.sH2.n, N.sVar2.n,
                                                   N.sRstd2.n, N.sZ.n)
                    bSqB = N.sSq.n

                    @block.sync
                    def _(sy):
                        sy.wait_ge(N.dG.sem, bG)
                        for i in range(NT):
                            if i >= 2:
                                sy.wait_ge(N.sH2.sem, bH2 + i - 1)
                            ii = sy.dma_start(out=yrl[:, (i % 2) * dm:(i % 2 + 1) * dm],
                                              in_=sc_y[seq, i * 128:(i + 1) * 128, :])
                            N.dSy.inc(ii, 16)

                    @block.tensor
                    def _(t):
                        for i in range(NT):
                            if i >= 2:
                                t.wait_ge(N.sH2.sem, bH2 + i - 1)
                            for nh in range(2):
                                for cc in range(DCH):
                                    mm = t.matmul(
                                        ps_pj[i % 2][:, nh * 512:(nh + 1) * 512],
                                        lhsT=oT[:, cc * T + i * 128:cc * T + (i + 1) * 128],
                                        rhs=wpb[:, cc * 1024 + nh * 512:cc * 1024 + (nh + 1) * 512],
                                        start=(cc == 0), stop=(cc == DCH - 1))
                            N.sPjMM.inc(mm)

                    @block.vector
                    def _(v):
                        def emit_z(j):
                            pj = j % 2
                            cb = pj * 8
                            v.wait_ge(N.sRstd2.sem, bRstd2 + j + 1)
                            if j >= 2:
                                v.wait_ge(N.dG.sem, bG + 16 * (j - 1))
                            ii = v.tensor_scalar(
                                out=zst[:, pj * dm:(pj + 1) * dm],
                                in0=h2b[:, pj * dm:(pj + 1) * dm],
                                scalar1=lncols[:, cb + 0:cb + 1],
                                scalar2=lncols[:, cb + 5:cb + 6],
                                op0=ALU.subtract, op1=ALU.mult)
                            N.sZ.inc(ii)

                        for i in range(NT):
                            pc = i % 2
                            cb = pc * 8
                            v.wait_ge(N.sPjMM.sem, bPj + i + 1)
                            v.wait_ge(N.dSy.sem, bSy + 16 * (i + 1))
                            v.scalar_tensor_tensor(
                                out=h2b[:, pc * dm:(pc + 1) * dm], in0=ps_pj[pc][:],
                                scalar=1.0, in1=yrl[:, pc * dm:(pc + 1) * dm],
                                op0=ALU.mult, op1=ALU.add)
                            ii = v.scalar_tensor_tensor(
                                out=h2b[:, pc * dm:(pc + 1) * dm],
                                in0=h2b[:, pc * dm:(pc + 1) * dm],
                                scalar=1.0, in1=bias_bc[:], op0=ALU.mult, op1=ALU.add,
                                accum_out=lncols[:, cb + 3:cb + 4])
                            N.sH2.inc(ii)
                            v.drain()
                            v.tensor_scalar(out=lncols[:, cb + 0:cb + 1],
                                            in0=lncols[:, cb + 3:cb + 4],
                                            scalar1=1.0 / dm, scalar2=None, op0=ALU.mult)
                            v.drain()
                            v.tensor_tensor(out=lncols[:, cb + 1:cb + 2],
                                            in0=lncols[:, cb + 0:cb + 1],
                                            in1=lncols[:, cb + 0:cb + 1], op=ALU.mult)
                            v.wait_ge(N.sSq.sem, bSqB + i + 1)
                            v.tensor_scalar(out=lncols[:, cb + 2:cb + 3],
                                            in0=lncols[:, cb + 4:cb + 5],
                                            scalar1=1.0 / dm, scalar2=None, op0=ALU.mult)
                            v.drain()
                            ii = v.tensor_tensor(out=lncols[:, cb + 2:cb + 3],
                                                 in0=lncols[:, cb + 2:cb + 3],
                                                 in1=lncols[:, cb + 1:cb + 2],
                                                 op=ALU.subtract)
                            N.sVar2.inc(ii)
                            if i >= 1:
                                emit_z(i - 1)
                        emit_z(NT - 1)

                    @block.scalar
                    def _(s):
                        for i in range(NT):
                            cb = (i % 2) * 8
                            s.wait_ge(N.sH2.sem, bH2 + i + 1)
                            ii = s.activation(out=sqc[:],
                                              in_=h2b[:, (i % 2) * dm:(i % 2 + 1) * dm],
                                              func=AF.Square,
                                              accum_out=lncols[:, cb + 4:cb + 5])
                            N.sSq.inc(ii)
                            s.wait_ge(N.sVar2.sem, bVar2 + i + 1)
                            s.activation(out=lncols[:, cb + 6:cb + 7],
                                         in_=lncols[:, cb + 2:cb + 3],
                                         func=AF.Ln, bias=cbias[:, 0:1])
                            ii = s.activation(out=lncols[:, cb + 5:cb + 6],
                                              in_=lncols[:, cb + 6:cb + 7],
                                              func=AF.Exp, scale=-0.5)
                            N.sRstd2.inc(ii)

                    @block.gpsimd
                    def _(g):
                        for i in range(NT):
                            g.wait_ge(N.sZ.sem, bZ + i + 1)
                            ii = g.dma_start(out=zout[seq, i * 128:(i + 1) * 128, :],
                                             in_=zst[:, (i % 2) * dm:(i % 2 + 1) * dm])
                            N.dG.inc(ii, 16)

                fence()

        @block.gpsimd
        def _(g):
            g.wait_ge(N.dG.sem, N.dG.n)

    return nc


_NC_CACHE = {}


def _get_nc():
    key = "main"
    if key not in _NC_CACHE:
        _NC_CACHE[key] = build()
    return _NC_CACHE[key]


def kernel(L0_attn_x, L0_attn_basis, L0_attn_q_coeffs, L0_attn_k_coeffs,
           L0_attn_v_coeffs, L0_attn_o_coeffs, L0_attn_decay_logit,
           L0_attn_out_scale, L0_ffn_g_info, L0_ffn_W_qkv, L0_ffn_W_proj,
           L0_ffn_b_proj):
    x = np.ascontiguousarray(np.asarray(L0_attn_x, dtype=np.float32))
    B, T, dm = x.shape
    ncore = 8
    nseq = B // ncore
    shards = x.reshape(ncore, nseq, T, dm)

    com = {
        "basis": np.ascontiguousarray(np.asarray(L0_attn_basis, np.float32)),
        "qc": np.ascontiguousarray(np.asarray(L0_attn_q_coeffs, np.float32)),
        "kc": np.ascontiguousarray(np.asarray(L0_attn_k_coeffs, np.float32)),
        "vc": np.ascontiguousarray(np.asarray(L0_attn_v_coeffs, np.float32)),
        "oc": np.ascontiguousarray(np.asarray(L0_attn_o_coeffs, np.float32)),
        "dlog": np.asarray(L0_attn_decay_logit, np.float32).reshape(1, 1),
        "oscale": np.asarray(L0_attn_out_scale, np.float32).reshape(1, 1),
        "gin": np.ascontiguousarray(
            np.asarray(L0_ffn_g_info, np.float32)[0, 0, :].reshape(16, 64)),
        "wqkv": np.ascontiguousarray(np.asarray(L0_ffn_W_qkv, np.float32)),
        "wproj": np.ascontiguousarray(np.asarray(L0_ffn_W_proj, np.float32)),
        "bproj": np.ascontiguousarray(
            np.asarray(L0_ffn_b_proj, np.float32).reshape(1, dm)),
    }
    in_maps = [dict(com, xin=np.ascontiguousarray(shards[c]))
               for c in range(ncore)]

    nc = _get_nc()
    res = run_bass_kernel_spmd(nc, in_maps, core_ids=list(range(ncore)))
    out = np.empty((B, T, dm), np.float32)
    for c in range(ncore):
        out[c * nseq:(c + 1) * nseq] = res.results[c]["zout"]
    return out
